# version 04
# baseline (speedup 1.0000x reference)
import contextlib
import numpy as np
import ml_dtypes
import concourse.bass as bass
import concourse.mybir as mybir
from concourse.bass_utils import run_bass_kernel_spmd
from concourse.alu_op_type import AluOpType as OP

F32 = mybir.dt.float32
BF16 = mybir.dt.bfloat16
I16 = mybir.dt.int16
I8 = mybir.dt.int8
AF = mybir.ActivationFunctionType
AX = mybir.AxisListType

T, D, NH, KVH, HD, FF = 2048, 2048, 32, 8, 64, 8192
NT, ND, QB, NQB, NFH = T // 128, D // 128, 512, 4, FF // 128
EPS_LN, EPS_RMS = 1e-5, 1e-6
NW = FF * D

BUILD_UPTO = "ALL"


def host_constants():
    inv_freq = 1.0 / (10000.0 ** (np.arange(0, HD, 2, dtype=np.float64) / HD))
    fr = np.outer(np.arange(T, dtype=np.float64), inv_freq)
    s = 8.0 ** -0.5
    cosf = np.repeat(np.cos(fr), 2, axis=1) * s
    sinf = np.repeat(np.sin(fr), 2, axis=1) * s
    bf = ml_dtypes.bfloat16
    return {
        "cosT": np.ascontiguousarray(np.tile(cosf, (1, NH))).astype(bf),
        "sinT": np.ascontiguousarray(np.tile(sinf, (1, NH))).astype(bf),
        "ident_bf": np.eye(128, dtype=np.float32).astype(bf),
        "ident_f32": np.eye(128, dtype=np.float32),
        "onesband": ((np.arange(127)[None, :] == 63) * np.ones((128, 1))).astype(np.float32).astype(bf),
    }


class Step:
    __slots__ = ("eng", "ch", "inc", "key", "deps", "fn")

    def __init__(self, eng, ch, inc, key, deps, fn):
        self.eng, self.ch, self.inc, self.key, self.deps, self.fn = \
            eng, ch, inc, key, deps, fn


class Sched:
    def __init__(self, nc, stack):
        self.nc, self.stack = nc, stack
        self.sems, self.table, self.ch_eng, self.cnt = {}, {}, {}, {}
        self.lastkey = {}
        self.steps = []
        self.ecnt = {}
        self._cur_ch = None

    def sem(self, ch):
        if ch not in self.sems:
            self.sems[ch] = self.stack.enter_context(self.nc.semaphore(ch))
        return self.sems[ch]

    def add(self, eng, ch, key, fn, deps=(), dma=False, nbar=0):
        if ch in self.ch_eng:
            assert self.ch_eng[ch] == eng, f"{ch}: {self.ch_eng[ch]} vs {eng}"
        else:
            self.ch_eng[ch] = eng
        inc = 16 if dma else 1
        self.cnt[ch] = self.cnt.get(ch, 0) + inc + nbar
        assert key not in self.table, f"dup {key}"
        self.table[key] = (ch, self.cnt[ch], eng)
        self.lastkey[ch] = key
        flat = []
        for d in deps:
            if d is None:
                continue
            if isinstance(d, list):
                flat += [x for x in d if x is not None]
            else:
                flat.append(d)
        self.steps.append(Step(eng, ch, inc, key, flat, fn))

    def bar(self, e, inst):
        ch = self._cur_ch
        self.ecnt[ch] = self.ecnt.get(ch, 0) + 1
        inst.then_inc(self.sem(ch), 1)
        e.wait_ge(self.sem(ch), self.ecnt[ch])

    def last(self, ch):
        return self.lastkey.get(ch)

    def fence(self, tag):
        deps = [k for c, k in self.lastkey.items() if c.startswith("d_")]
        self.add("vector", "ve", ("fence", tag), lambda e: e.engine_nop(), deps=deps)

    def emit(self):
        per = {}
        for s in self.steps:
            per.setdefault(s.eng, []).append(s)
        table = self.table

        ENGCH = {"vector": "ve", "scalar": "sc", "gpsimd": "gp"}

        def mk(engsteps):
            def body(eng):
                waited = {}
                for s in engsteps:
                    own = ENGCH.get(s.eng)
                    ownv = self.ecnt.get(own, 0) if own is not None else 0
                    if own is not None and ownv > waited.get(own, -1):
                        eng.wait_ge(self.sem(own), ownv)
                        waited[own] = ownv
                    for d in s.deps:
                        ch, v, peng = table[d]
                        if waited.get(ch, -1) >= v:
                            continue
                        eng.wait_ge(self.sem(ch), v)
                        waited[ch] = v
                    self._cur_ch = s.ch
                    inst = s.fn(eng)
                    inst.then_inc(self.sem(s.ch), s.inc)
                    self.ecnt[s.ch] = self.ecnt.get(s.ch, 0) + s.inc
                    assert self.ecnt[s.ch] == table[s.key][1], \
                        f"step {s.key}: emitted {self.ecnt[s.ch]} != planned {table[s.key][1]} (nbar mismatch)"
            return body

        with self.nc.Block() as blk:
            for engname, engsteps in per.items():
                getattr(blk, engname)(mk(engsteps))
        self.steps = []


class Rot:
    def __init__(self, bufs, name):
        self.bufs, self.name, self.n = bufs, name, len(bufs)
        self.i = -1
        self.cons = [[] for _ in range(self.n)]

    def next(self):
        self.i += 1
        s = self.i % self.n
        deps = self.cons[s]
        self.cons[s] = []
        return s, self.bufs[s], deps

    def ch(self, s):
        return f"d_{self.name}{s}"

    def used(self, s, key):
        self.cons[s].append(key)


def build(nc, debug=(), upto="ALL"):
    debug = set(debug)
    order = ["A", "B", "F1", "ALL"]
    lvl = order.index(upto)

    def want(p):
        return order.index(p) <= lvl

    def dram(name, shape, dtype, kind=None):
        return nc.dram_tensor(name, shape, dtype) if kind is None else \
            nc.dram_tensor(name, shape, dtype, kind=kind)

    x_h = dram("x", [T, D], F32, kind="ExternalInput")
    wq_h = dram("Wq", [D, D], F32, kind="ExternalInput")
    bq_h = dram("bq", [1, D], F32, kind="ExternalInput")
    wk_h = dram("Wk", [D, 512], F32, kind="ExternalInput")
    bk_h = dram("bk", [1, 512], F32, kind="ExternalInput")
    wv_h = dram("Wv", [D, 512], F32, kind="ExternalInput")
    bv_h = dram("bv", [1, 512], F32, kind="ExternalInput")
    wo_h = dram("Wo", [D, D], F32, kind="ExternalInput")
    bo_h = dram("bo", [1, D], F32, kind="ExternalInput")
    wg_h = dram("w_gate", [FF, D], F32, kind="ExternalInput")
    wu_h = dram("w_up", [FF, D], F32, kind="ExternalInput")
    wd_h = dram("w_down", [D, FF], F32, kind="ExternalInput")
    cos_h = dram("cosT", [T, D], BF16, kind="ExternalInput")
    sin_h = dram("sinT", [T, D], BF16, kind="ExternalInput")
    identb_h = dram("ident_bf", [128, 128], BF16, kind="ExternalInput")
    identf_h = dram("ident_f32", [128, 128], F32, kind="ExternalInput")
    onesband_h = dram("onesband", [128, 127], BF16, kind="ExternalInput")
    out_h = dram("out", [T, D], F32, kind="ExternalOutput")

    qsp_h = dram("q_spill", [T, D], BF16)
    wobf_h = dram("wo_bf", [D, D], BF16)
    x2sp_h = dram("x2_spill", [T, D], F32)
    xqsp_h = dram("xq_spill", [T, D], BF16)
    h2sp_h = dram("h2_spill", [FF, T], F32)
    wdtsp_h = dram("wdt_spill", [FF, D], BF16)
    rowb_h = dram("rowbounce", [128, 16], F32)
    rowf_h = dram("rowflat", [1, 2048], F32)

    top = contextlib.ExitStack()
    S = Sched(nc, top)
    dbg_bufs = {}

    def sb(stk, n, shp, dt):
        return stk.enter_context(nc.sbuf_tensor(n, shp, dt))

    def ps(stk, n, shp, dt):
        return stk.enter_context(nc.psum_tensor(n, shp, dt))

    identb = sb(top, "identb", [128, 128], BF16)
    identf = sb(top, "identf", [128, 128], F32)
    onesband = sb(top, "onesb", [128, 127], BF16)
    ppart = sb(top, "ppart", [128, 192], F32)
    wsc = sb(top, "wsc", [128, 16], F32)
    sxq = sb(top, "sxq", [128, NT], F32)
    c3col = sb(top, "c3col", [128, NT], F32)
    cbc = sb(top, "cbc", [128, 2048], F32)
    smv = sb(top, "smv", [128, 16], F32)
    onesr = sb(top, "onesr", [1, 128], F32)
    onesf = sb(top, "onesf", [128, 1], F32)

    ab = contextlib.ExitStack()
    kT = sb(ab, "kT", [128, 4 * T], BF16)
    v_buf = sb(ab, "v_buf", [128, NT * 520], BF16)
    pp_stg = Rot([sb(ab, "pp_stg0", [128, 2048], F32)], "pp")
    pp_dump = sb(ab, "pp_dump", [128, 2048], BF16)

    rw = contextlib.ExitStack()
    rows4 = rw.enter_context(nc.sbuf_tensor("rows4", [128, 2048], F32, side="right"))

    S.add("sync", "d_ini0", ("ini", 0), lambda e: e.dma_start(out=identb[:], in_=identb_h.ap()), dma=True)
    S.add("sync", "d_ini1", ("ini", 1), lambda e: e.dma_start(out=identf[:], in_=identf_h.ap()), dma=True)
    S.add("sync", "d_ini2", ("ini", 2), lambda e: e.dma_start(out=onesband[:], in_=onesband_h.ap()), dma=True)

    def vones(e):
        for m in range(NT):
            ap = v_buf[:, m * 520:(m + 1) * 520].rearrange("p (h c) -> p h c", c=65)[:, :, 64:65]
            i = e.memset(ap, 1.0)
        return i
    S.add("gpsimd", "gp", ("ini", "vones"), vones)
    S.add("vector", "ve", ("ini", "amx0"), lambda e: e.memset(cbc[:], -3.0e38))
    S.add("vector", "ve", ("ini", "ones1"), lambda e: e.memset(onesr[:], 1.0))
    S.add("vector", "ve", ("ini", "ones2"), lambda e: e.memset(onesf[:], 1.0))

    ast = contextlib.ExitStack()
    xT = sb(ast, "xT", [128, ND * T], BF16)
    bias_bc = sb(ast, "bias_bc", [128, 3072], F32)
    x_stage = Rot([sb(ast, f"x_stage{i}", [128, D], F32) for i in range(2)], "xst")
    xbf = Rot([sb(ast, f"xbf{i}", [128, D], BF16) for i in range(2)], "xbf")
    wpiece = Rot([sb(ast, f"wpiece{i}", [128, 512], F32) for i in range(2)], "wp")
    wslab = Rot([sb(ast, "wslab0", [128, 16 * 512], BF16)], "ws")
    ctile = Rot([sb(ast, f"ctile{i}", [128, 512], BF16) for i in range(2)], "ct")
    stile = Rot([sb(ast, f"stile{i}", [128, 512], BF16) for i in range(2)], "sn")
    qb_st = sb(ast, "qb_st", [128, 512], F32)
    qc_st = sb(ast, "qc_st", [128, 512], F32)
    qs_st = sb(ast, "qs_st", [128, 512], F32)
    qrope = Rot([sb(ast, f"qrope{i}", [128, 512], BF16) for i in range(2)], "qr")
    wop = Rot([sb(ast, f"wop{i}", [128, 512], F32) for i in range(2)], "wopl")
    wob = Rot([sb(ast, f"wob{i}", [128, 512], BF16) for i in range(2)], "wos")

    psX = ps(ast, "psX", [128, 1024], BF16)
    psQ = Rot([ps(ast, f"psQ{i}", [128, 512], F32) for i in range(2)], "psQ")

    S.add("sync", "d_ba0", ("abl", 0), lambda e: e.dma_start(out=bias_bc[0:1, 0:D], in_=bq_h.ap()), dma=True)
    S.add("sync", "d_ba1", ("abl", 1), lambda e: e.dma_start(out=bias_bc[0:1, D:D + 512], in_=bk_h.ap()), dma=True)
    S.add("sync", "d_ba2", ("abl", 2), lambda e: e.dma_start(out=bias_bc[0:1, D + 512:D + 1024], in_=bv_h.ap()), dma=True)
    for n in range(6):
        qsl0, qps0, qfr0 = psQ.next()
        S.add("tensor", "mm", ("abm", n),
              lambda e, n=n, p_=qps0: e.matmul(
                  p_[:], onesr[:], bias_bc[0:1, n * 512:(n + 1) * 512],
                  start=True, stop=True),
              deps=[("abl", 0), ("abl", 1), ("abl", 2), ("ini", "ones1")] + qfr0)
        S.add("vector", "ve", ("abe", n),
              lambda e, n=n, p_=qps0: e.tensor_copy(
                  bias_bc[:, n * 512:(n + 1) * 512], p_[:]),
              deps=[("abm", n)])
        psQ.used(qsl0, ("abe", n))
    S.add("vector", "ve", ("abb",), lambda e: e.engine_nop(),
          deps=[("abe", n) for n in range(6)])

    for m in range(NT):
        sl, buf, fr = x_stage.next()
        S.add("sync", x_stage.ch(sl), ("axl", m),
              lambda e, b=buf, m=m: e.dma_start(out=b[:], in_=x_h.ap()[m * 128:(m + 1) * 128, :]),
              deps=[fr], dma=True)
        bsl, bb, bfr = xbf.next()
        S.add("scalar", "sc", ("axc", m),
              lambda e, o=bb, i_=buf: e.activation(o[:], i_[:], AF.Copy),
              deps=[("axl", m), bfr])
        x_stage.used(sl, ("axc", m))
        for j in range(4):
            g = m * 4 + j
            pd = ("axe",) + divmod(g - 1, 4) if g >= 1 else None
            def tf(e, b=bb, j=j):
                for c in range(4 * j, 4 * j + 4):
                    i = e.transpose(psX[:, (c % 8) * 128:(c % 8 + 1) * 128],
                                    b[:, c * 128:(c + 1) * 128], identb[:])
                return i
            S.add("tensor", "mm", ("axt", m, j), tf, deps=[("axc", m), pd, ("ini", 0)])
            def ef(e, m=m, j=j):
                dst = xT[:].rearrange("p (f t) -> p f t", f=ND)[:, 4 * j:4 * j + 4,
                                                               m * 128:(m + 1) * 128]
                src = psX[:].rearrange("p (s c) -> p s c", s=8)[:, (j % 2) * 4:(j % 2) * 4 + 4, :]
                return e.tensor_copy(dst, src)
            S.add("vector", "ve", ("axe", m, j), ef, deps=[("axt", m, j)])
        xbf.used(bsl, ("axt", m, 3))

    for oc in range(6):
        wsl, wbuf, wfr = wslab.next()
        for kc in range(16):
            psl, pbuf, pfr = wpiece.next()
            src = (wq_h.ap()[:, oc * 512:(oc + 1) * 512] if oc < 4 else
                   (wk_h.ap() if oc == 4 else wv_h.ap()))
            S.add("sync", wpiece.ch(psl), ("awl", oc, kc),
                  lambda e, b=pbuf, s=src, kc=kc: e.dma_start(
                      out=b[:], in_=s[kc * 128:(kc + 1) * 128, :]),
                  deps=[pfr], dma=True)
            S.add("scalar", "sc", ("awc", oc, kc),
                  lambda e, o=wbuf, i_=pbuf, kc=kc: e.activation(
                      o[:, kc * 512:(kc + 1) * 512], i_[:], AF.Copy),
                  deps=[("awl", oc, kc), wfr if kc == 0 else None])
            wpiece.used(psl, ("awc", oc, kc))
        for m in range(NT):
            if oc < 5:
                c0 = oc * 512 if oc < 4 else 0
                csl, cb, cfr = ctile.next()
                ssl, sb2, sfr = stile.next()
                S.add("sync", ctile.ch(csl), ("acl", oc, m),
                      lambda e, b=cb, m=m, c0=c0: e.dma_start(
                          out=b[:], in_=cos_h.ap()[m * 128:(m + 1) * 128, c0:c0 + 512]),
                      deps=[cfr], dma=True)
                S.add("sync", stile.ch(ssl), ("asl", oc, m),
                      lambda e, b=sb2, m=m, c0=c0: e.dma_start(
                          out=b[:], in_=sin_h.ap()[m * 128:(m + 1) * 128, c0:c0 + 512]),
                      deps=[sfr], dma=True)
            qsl, qps, qfr = psQ.next()
            def mf(e, oc=oc, m=m, w=wbuf, p_=qps):
                for kc in range(16):
                    i = e.matmul(p_[:], xT[:, kc * T + m * 128:kc * T + (m + 1) * 128],
                                 w[:, kc * 512:(kc + 1) * 512],
                                 start=(kc == 0), stop=(kc == 15))
                return i
            S.add("tensor", "mm", ("aqm", oc, m), mf,
                  deps=[("awc", oc, 15), ("axe", NT - 1, 3), qfr])
            if m == NT - 1:
                wslab.used(wsl, ("aqm", oc, m))
            if oc < 5:
                S.add("vector", "ve", ("aqb", oc, m),
                      lambda e, p_=qps, oc=oc: e.tensor_tensor(
                          qb_st[:], p_[:], bias_bc[:, oc * 512:(oc + 1) * 512], OP.add),
                      deps=[("aqm", oc, m), ("abb",)])
                psQ.used(qsl, ("aqb", oc, m))
                S.add("vector", "ve", ("aqc", oc, m),
                      lambda e, b=cb: e.tensor_tensor(qc_st[:], qb_st[:], b[:], OP.mult),
                      deps=[("acl", oc, m)])
                ctile.used(csl, ("aqc", oc, m))
                S.add("vector", "ve", ("aqs", oc, m),
                      lambda e, b=sb2: e.tensor_tensor(qs_st[:], qb_st[:], b[:], OP.mult),
                      deps=[("asl", oc, m)])
                stile.used(ssl, ("aqs", oc, m))
                rsl, rb, rfr = qrope.next()
                def rp(e, o=rb):
                    qcr = qc_st[:].rearrange("p (h d) -> p h d", h=8)
                    qsr = qs_st[:].rearrange("p (h d) -> p h d", h=8)
                    dr = o[:].rearrange("p (h d) -> p h d", h=8)
                    e.tensor_tensor(dr[:, :, 0:32], qcr[:, :, 0:64:2], qsr[:, :, 1:64:2], OP.subtract)
                    return e.tensor_tensor(dr[:, :, 32:64], qcr[:, :, 1:64:2], qsr[:, :, 0:64:2], OP.add)
                S.add("vector", "ve", ("arp", oc, m), rp, deps=[("aqs", oc, m)] + [rfr])
                if oc < 4:
                    S.add("gpsimd", qrope.ch(rsl), ("aqsp", oc, m),
                          lambda e, b=rb, m=m, oc=oc: e.dma_start(
                              out=qsp_h.ap()[m * 128:(m + 1) * 128, oc * 512:(oc + 1) * 512],
                              in_=b[:]),
                          deps=[("arp", oc, m)], dma=True)
                    qrope.used(rsl, ("aqsp", oc, m))
                else:
                    def ktf(e, b=rb):
                        for c in range(4):
                            i = e.transpose(psX[:, c * 128:(c + 1) * 128],
                                            b[:, c * 128:(c + 1) * 128], identb[:])
                        return i
                    S.add("tensor", "mm", ("akt", m), ktf,
                          deps=[("arp", oc, m),
                                ("ake", m - 1) if m > 0 else ("axe", NT - 1, 3)])
                    qrope.used(rsl, ("akt", m))
                    def kte(e, m=m):
                        dst = kT[:].rearrange("p (f t) -> p f t", f=4)[:, :, m * 128:(m + 1) * 128]
                        return e.tensor_copy(dst, psX[:].rearrange("p (s c) -> p s c", s=8)[:, 0:4, :])
                    S.add("vector", "ve", ("ake", m), kte, deps=[("akt", m)])
            else:
                def vf(e, p_=qps, m=m):
                    dst = v_buf[:, m * 520:(m + 1) * 520].rearrange("p (h c) -> p h c", c=65)[:, :, 0:64]
                    return e.tensor_tensor(dst, p_[:], bias_bc[:, 2560:3072], OP.add)
                S.add("vector", "ve", ("av", m), vf,
                      deps=[("aqm", oc, m), ("ini", "vones"), ("abb",)])
                psQ.used(qsl, ("av", m))

    for i in range(64):
        r, c = divmod(i, 4)
        wsl2, wb2, wfr2 = wop.next()
        S.add("scalar", wop.ch(wsl2), ("awol", i),
              lambda e, b=wb2, r=r, c=c: e.dma_start(
                  out=b[:], in_=wo_h.ap()[r * 128:(r + 1) * 128, c * 512:(c + 1) * 512]),
              deps=[wfr2], dma=True)
        bsl2, bb2, bfr2 = wob.next()
        S.add("scalar", "sc", ("awoc", i),
              lambda e, o=bb2, i_=wb2: e.activation(o[:], i_[:], AF.Copy),
              deps=[("awol", i), bfr2])
        wop.used(wsl2, ("awoc", i))
        S.add("gpsimd", wob.ch(bsl2), ("awos", i),
              lambda e, b=bb2, r=r, c=c: e.dma_start(
                  out=wobf_h.ap()[r * 128:(r + 1) * 128, c * 512:(c + 1) * 512], in_=b[:]),
              deps=[("awoc", i)], dma=True)
        wob.used(bsl2, ("awos", i))

    def pp_piece(idx):
        if idx < 64:
            src = wg_h.ap()[idx * 128:(idx + 1) * 128, :]
        elif idx < 128:
            src = wu_h.ap()[(idx - 64) * 128:(idx - 63) * 128, :]
        else:
            j = idx - 128
            src = wd_h.ap()[(j // 4) * 128:(j // 4 + 1) * 128, (j % 4) * 2048:(j % 4 + 1) * 2048]
        sl, buf, fr = pp_stg.next()
        S.add("scalar", pp_stg.ch(sl), ("ppl", idx),
              lambda e, b=buf, s=src: e.dma_start(out=b[:], in_=s), deps=[fr], dma=True)
        S.add("scalar", "sc", ("ppa", idx),
              lambda e, b=buf, idx=idx: e.activation(pp_dump[:], b[:], AF.Abs,
                                                     accum_out=ppart[:, idx:idx + 1]),
              deps=[("ppl", idx)])
        pp_stg.used(sl, ("ppa", idx))
    for idx in range(40):
        pp_piece(idx)

    if not want("A"):
        pass
    dbg_bufs.update(d_xT=(xT, BF16), d_kT=(kT, BF16), d_vb=(v_buf, BF16))
    if upto == "A":
        _emit_debug(S, nc, debug, dbg_bufs)
        S.emit()
        return nc
    S.fence("A")
    S.emit()
    ast.close()

    bst = contextlib.ExitStack()
    qT = sb(bst, "qT", [128, 16 * 512], BF16)
    pX = [sb(bst, "pA", [128, 16 * 512], BF16), sb(bst, "pB", [128, 16 * 512], BF16)]
    attn_fm = sb(bst, "attn_fm", [128, 16 * 512], BF16)
    bo_bc = sb(bst, "bo_bc", [128, D], F32)
    qrl = Rot([sb(bst, "qrl0", [128, D], BF16)], "qrl")
    av_s = sb(bst, "av_s", [128, 512], F32)
    rs_s = sb(bst, "rs_s", [128, 8], F32)
    atm = sb(bst, "atm", [128, 256], BF16)
    xres = Rot([sb(bst, f"xres{i}", [128, 512], F32) for i in range(2)], "xr")
    z1m = [sb(bst, f"z1m{i}", [128, D], F32) for i in range(4)]
    bnbuf = sb(bst, "bnbuf", [128, 24], F32)
    x2t = Rot([sb(bst, f"x2f{i}", [128, D], F32) for i in range(2)], "x2s")
    tq = sb(bst, "tq", [128, 512], F32)
    r16 = sb(bst, "r16", [128, 512], I16)
    cdf = sb(bst, "cdf", [128, 512], F32)
    s1t = sb(bst, "s1t", [128, 512], F32)
    xqt = Rot([sb(bst, "xqt0", [128, D], BF16)], "xqs")

    psS = [ps(bst, f"psS{i}", [128, 512], F32) for i in range(4)]
    psAV = ps(bst, "psAV", [128, 512], F32)
    psBF = ps(bst, "psBF", [128, 1024], BF16)
    psW = [ps(bst, f"psW{i}", [128, 512], F32) for i in range(2)]

    S.add("sync", "d_bo", ("bbo", 0), lambda e: e.dma_start(out=bo_bc[0:1, :], in_=bo_h.ap()), dma=True)
    for n in range(4):
        S.add("tensor", "mm", ("bbm", n),
              lambda e, n=n: e.matmul(psW[n % 2][:], onesr[:],
                                      bo_bc[0:1, n * 512:(n + 1) * 512],
                                      start=True, stop=True),
              deps=[("bbo", 0), ("bboe", n - 2) if n >= 2 else None])
        S.add("vector", "ve", ("bboe", n),
              lambda e, n=n: e.tensor_copy(bo_bc[:, n * 512:(n + 1) * 512], psW[n % 2][:]),
              deps=[("bbm", n)])
    S.add("vector", "ve", ("bbo", 1), lambda e: e.engine_nop(),
          deps=[("bboe", n) for n in range(4)])

    def pair_heads(p):
        b, i = divmod(p, 4)
        return 8 * b + i, 8 * b + i + 4, b

    last_bf = [None]
    last_av = [None]
    last_psw = {0: ("bboe", 2), 1: ("bboe", 3)}
    last_wo = [None]
    qspill_done = [S.last(f"d_qr{i}") for i in range(2)]

    for g in range(NQB):
        nkt = 4 * g + 4
        for m in range(4):
            sl, buf, fr = qrl.next()
            S.add("sync", qrl.ch(sl), ("bql", g, m),
                  lambda e, b=buf, g=g, m=m: e.dma_start(
                      out=b[:], in_=qsp_h.ap()[g * 512 + m * 128:g * 512 + (m + 1) * 128, :]),
                  deps=[fr] + qspill_done, dma=True)
            for hh in range(2):
                def qtf(e, b=buf, hh=hh):
                    for pi in range(8 * hh, 8 * hh + 8):
                        pa, pb_, kv = pair_heads(pi)
                        e.transpose(psBF[0:64, (pi % 8) * 128:(pi % 8 + 1) * 128],
                                    b[:, pa * 64:(pa + 1) * 64], identb[:])
                        i = e.transpose(psBF[64:128, (pi % 8) * 128:(pi % 8 + 1) * 128],
                                        b[:, pb_ * 64:(pb_ + 1) * 64], identb[:])
                    return i
                S.add("tensor", "mm", ("bqt", g, m, hh), qtf,
                      deps=[("bql", g, m), last_bf[0],
                            ("bsc", g - 1, 15, 1, 4 * g - 1) if g > 0 else None])
                def qte(e, g=g, m=m, hh=hh):
                    dst = qT[:].rearrange("p (q t) -> p q t", q=16)[
                        :, 8 * hh:8 * hh + 8, m * 128:(m + 1) * 128]
                    return e.tensor_copy(dst, psBF[:].rearrange("p (s c) -> p s c", s=8))
                S.add("vector", "ve", ("bqte", g, m, hh), qte, deps=[("bqt", g, m, hh)])
                last_bf[0] = ("bqte", g, m, hh)
            qrl.used(sl, ("bqt", g, m, 1))
        for p in range(16):
            ha, hb, kv = pair_heads(p)
            for kt in range(nkt):
                for X in range(2):
                    poff = 64 * X
                    bank = 2 * (kt % 2) + X
                    if kt >= 2:
                        dep_ps = ("bexp", g, p, X, kt - 2)
                    elif p > 0:
                        dep_ps = ("bexp", g, p - 1, X, nkt - 2 + kt)
                    else:
                        dep_ps = None
                    S.add("tensor", "mm", ("bsc", g, p, X, kt),
                          lambda e, kv=kv, kt=kt, p=p, poff=poff, bank=bank: e.matmul(
                              psS[bank][:],
                              kT[poff:poff + 64, kv * T + kt * 128:kv * T + (kt + 1) * 128],
                              qT[poff:poff + 64, p * 512:(p + 1) * 512],
                              start=True, stop=True),
                          deps=[("bqte", g, 3, 1), dep_ps])
                    if p > 0:
                        pdep = ("bav", g, p - 1, X)
                    elif g > 0:
                        pdep = ("bav", g - 1, 15, X)
                    else:
                        pdep = None
                    wo_dep = last_wo[0] if (p == 0 and kt == 0) else None
                    S.add("scalar", "sc", ("bexp", g, p, X, kt),
                          lambda e, pb=pX[X], kt=kt, bank=bank: e.activation(
                              pb[:, kt * 512:(kt + 1) * 512], psS[bank][:], AF.Exp),
                          deps=[("bsc", g, p, X, kt), pdep, wo_dep])
                    if kt >= 4 * g:
                        S.add("gpsimd", "gp", ("bmk", g, p, X, kt),
                              lambda e, pb=pX[X], kt=kt, g=g: e.affine_select(
                                  pb[:, kt * 512:(kt + 1) * 512],
                                  pb[:, kt * 512:(kt + 1) * 512],
                                  [[1, 512]], OP.is_ge, 0.0,
                                  base=g * 512 - kt * 128, channel_multiplier=-1),
                              deps=[("bexp", g, p, X, kt)])
            for X in range(2):
                hh = ha if X == 0 else hb
                hv = hh // 4
                def avf(e, hv=hv, pb=pX[X], nkt=nkt):
                    for kt in range(nkt):
                        i = e.matmul(psAV[0:65, :],
                                     v_buf[:, kt * 520 + hv * 65:kt * 520 + hv * 65 + 65],
                                     pb[:, kt * 512:(kt + 1) * 512],
                                     start=(kt == 0), stop=(kt == nkt - 1))
                    return i
                S.add("tensor", "mm", ("bav", g, p, X), avf,
                      deps=[("bexp", g, p, X, nkt - 1), last_av[0]] +
                           [("bmk", g, p, X, kt) for kt in range(4 * g, nkt)])
                S.add("vector", "ve", ("bave", g, p, X),
                      lambda e: e.tensor_copy(av_s[0:65, :], psAV[0:65, :]),
                      deps=[("bav", g, p, X)])
                last_av[0] = ("bave", g, p, X)
                def ttf(e):
                    for c in range(4):
                        i = e.transpose(psW[0][:, c * 65:c * 65 + 65],
                                        av_s[0:65, c * 128:(c + 1) * 128], identf[0:65, 0:65])
                    return i
                S.add("tensor", "mm", ("btt", g, p, X), ttf,
                      deps=[("bave", g, p, X), last_psw[0], ("ini", 1)])
                def nrm(e):
                    for c in range(4):
                        i = e.reciprocal(rs_s[:, c:c + 1], psW[0][:, c * 65 + 64:c * 65 + 65])
                    S.bar(e, i)
                    for c in range(4):
                        i = e.tensor_scalar(atm[:, c * 64:(c + 1) * 64],
                                            psW[0][:, c * 65:c * 65 + 64],
                                            rs_s[:, c:c + 1], None, OP.mult)
                    return i
                S.add("vector", "ve", ("bnr", g, p, X), nrm, deps=[("btt", g, p, X)], nbar=1)
                last_psw[0] = ("bnr", g, p, X)
                po2 = (hh % 2) * 64
                def btf(e, po2=po2):
                    for c in range(4):
                        i = e.transpose(psBF[po2:po2 + 64, c * 128:(c + 1) * 128],
                                        atm[:, c * 64:(c + 1) * 64], identb[:])
                    return i
                S.add("tensor", "mm", ("bbt", g, p, X), btf,
                      deps=[("bnr", g, p, X), last_bf[0]])
                def bte(e, hh=hh, po2=po2):
                    dst = attn_fm[po2:po2 + 64, (hh // 2) * 512:(hh // 2 + 1) * 512]
                    return e.tensor_copy(dst, psBF[po2:po2 + 64, 0:512])
                S.add("vector", "ve", ("bbe", g, p, X), bte,
                      deps=[("bbt", g, p, X), last_wo[0]])
                last_bf[0] = ("bbe", g, p, X)
        for n in range(4):
            wsl3 = n % 2
            wbuf3 = pX[wsl3]
            slab_dep = [("bav", g, 15, wsl3)] if n < 2 else [("bwo", g, n - 2, 3)]
            S.add("sync", f"d_wosl{wsl3}", ("bwl", g, n),
                  lambda e, b=wbuf3, n=n: e.dma_start(
                      out=b[:].rearrange("p (k c) -> p k c", k=16),
                      in_=wobf_h.ap()[:, n * 512:(n + 1) * 512].rearrange("(k p) c -> p k c", p=128)),
                  deps=slab_dep + [S.last("d_wos0"), S.last("d_wos1")], dma=True)
            for m in range(4):
                wps = psW[m % 2]
                att_dep = [("bbe", g, 15, 1)] if (n == 0 and m == 0) else []
                def wof(e, b=wbuf3, m=m, p_=wps):
                    for kc in range(16):
                        i = e.matmul(p_[:], attn_fm[:, kc * 512 + m * 128:kc * 512 + (m + 1) * 128],
                                     b[:, kc * 512:(kc + 1) * 512],
                                     start=(kc == 0), stop=(kc == 15))
                    return i
                S.add("tensor", "mm", ("bwo", g, n, m), wof,
                      deps=[("bwl", g, n), last_psw.get(m % 2)] + att_dep)
                xsl, xb3, xfr3 = xres.next()
                S.add("sync", xres.ch(xsl), ("bxl", g, n, m),
                      lambda e, b=xb3, g=g, n=n, m=m: e.dma_start(
                          out=b[:], in_=x_h.ap()[g * 512 + m * 128:g * 512 + (m + 1) * 128,
                                                 n * 512:(n + 1) * 512]),
                      deps=[xfr3], dma=True)
                def z1f(e, p_=wps, xb=xb3, n=n, m=m):
                    sl_ = z1m[m][:, n * 512:(n + 1) * 512]
                    i = e.tensor_tensor(sl_, p_[:], bo_bc[:, n * 512:(n + 1) * 512], OP.add)
                    S.bar(e, i)
                    return e.tensor_tensor(sl_, sl_, xb[:], OP.add)
                S.add("vector", "ve", ("bz1", g, n, m), z1f,
                      deps=[("bwo", g, n, m), ("bxl", g, n, m), ("bbo", 1)], nbar=1)
                xres.used(xsl, ("bz1", g, n, m))
                last_psw[m % 2] = ("bz1", g, n, m)
            last_wo[0] = ("bwo", g, n, 3)
        for m in range(4):
            gm = g * 4 + m
            prev_brs = ("brs", g, m - 1) if m > 0 else \
                (("brs", g - 1, 3) if g > 0 else None)
            def lnf(e, m=m):
                for c in range(4):
                    i = e.bn_stats(bnbuf[:, c * 6:(c + 1) * 6], z1m[m][:, c * 512:(c + 1) * 512])
                S.bar(e, i)
                i = e.bn_aggr(smv[:, 0:2], bnbuf[:, 0:24])
                S.bar(e, i)
                i = e.tensor_scalar(smv[:, 2:3], smv[:, 1:2], 1.0, EPS_LN, OP.mult, OP.add)
                S.bar(e, i)
                return e.reciprocal(smv[:, 3:4], smv[:, 2:3])
            S.add("vector", "ve", ("bln", g, m), lnf, deps=[("bz1", g, 3, m), prev_brs], nbar=3)
            S.add("scalar", "sc", ("bsq", g, m),
                  lambda e: e.sqrt(smv[:, 4:5], smv[:, 3:4]), deps=[("bln", g, m)])
            xsl2, x2b, x2fr = x2t.next()
            def app(e, m=m, xb=x2b):
                i = e.tensor_tensor(smv[:, 5:6], smv[:, 0:1], smv[:, 4:5], OP.mult)
                S.bar(e, i)
                i = e.tensor_scalar(smv[:, 6:7], smv[:, 5:6], -1.0, None, OP.mult)
                S.bar(e, i)
                return e.tensor_scalar(xb[:], z1m[m][:], smv[:, 4:5], smv[:, 6:7],
                                       OP.mult, OP.add)
            S.add("vector", "ve", ("bx2", g, m), app, deps=[("bsq", g, m)] + [x2fr], nbar=2)
            S.add("gpsimd", x2t.ch(xsl2), ("bx2s", gm),
                  lambda e, xb=x2b, gm=gm: e.dma_start(
                      out=x2sp_h.ap()[gm * 128:(gm + 1) * 128, :], in_=xb[:]),
                  deps=[("bx2", g, m)], dma=True)
            x2t.used(xsl2, ("bx2s", gm))
            def rmsf(e):
                i = e.tensor_tensor(smv[:, 7:8], smv[:, 1:2], smv[:, 3:4], OP.mult)
                S.bar(e, i)
                i = e.tensor_scalar(smv[:, 8:9], smv[:, 7:8], 1.0, EPS_RMS, OP.mult, OP.add)
                S.bar(e, i)
                return e.reciprocal(smv[:, 9:10], smv[:, 8:9])
            S.add("vector", "ve", ("brm", g, m), rmsf, deps=[("bln", g, m)], nbar=2)
            S.add("scalar", "sc", ("brs", g, m),
                  lambda e: e.sqrt(smv[:, 10:11], smv[:, 9:10]), deps=[("brm", g, m)])
            def bamf(e, xb=x2b):
                e.tensor_reduce(smv[:, 11:12], xb[:], AX.X, OP.max)
                i = e.tensor_reduce(smv[:, 12:13], xb[:], AX.X, OP.min)
                S.bar(e, i)
                i = e.tensor_scalar(smv[:, 12:13], smv[:, 12:13], -1.0, None, OP.mult)
                S.bar(e, i)
                return e.tensor_tensor(smv[:, 11:12], smv[:, 11:12], smv[:, 12:13], OP.max)
            S.add("vector", "ve", ("bam", g, m), bamf, deps=[("bx2", g, m)], nbar=2)
            def scf(e, gm=gm):
                i = e.tensor_tensor(smv[:, 13:14], smv[:, 10:11], smv[:, 11:12], OP.mult)
                S.bar(e, i)
                i = e.tensor_scalar(smv[:, 13:14], smv[:, 13:14], 1e-5, None, OP.max)
                S.bar(e, i)
                i = e.reciprocal(smv[:, 14:15], smv[:, 13:14])
                S.bar(e, i)
                i = e.tensor_scalar(sxq[:, gm:gm + 1], smv[:, 14:15], 127.0, None, OP.mult)
                S.bar(e, i)
                return e.tensor_tensor(smv[:, 15:16], smv[:, 10:11], sxq[:, gm:gm + 1], OP.mult)
            S.add("vector", "ve", ("bs2", g, m), scf, deps=[("brs", g, m), ("bam", g, m)], nbar=4)
            qsl2, xqb, xqfr = xqt.next()
            def quant(e, xb=x2b, o=xqb):
                for hf2 in range(4):
                    slx = slice(hf2 * 512, (hf2 + 1) * 512)
                    if hf2:
                        S.bar(e, i)
                    i = e.tensor_scalar(tq[:], xb[:, slx], smv[:, 15:16], None, OP.mult)
                    S.bar(e, i)
                    i = e.tensor_copy(r16[:], tq[:])
                    S.bar(e, i)
                    i = e.tensor_tensor(cdf[:], tq[:], r16[:], OP.subtract)
                    S.bar(e, i)
                    i = e.scalar_tensor_tensor(s1t[:], cdf[:], -0.5, r16[:], OP.is_lt, OP.subtract)
                    S.bar(e, i)
                    i = e.scalar_tensor_tensor(o[:, slx], cdf[:], 0.5, s1t[:], OP.is_gt, OP.subtract)
                return i
            S.add("vector", "ve", ("bqq", g, m), quant, deps=[("bs2", g, m)] + [xqfr], nbar=19)
            x2t.used(xsl2, ("bqq", g, m))
            S.add("gpsimd", xqt.ch(qsl2), ("bxqs", gm),
                  lambda e, o=xqb, gm=gm: e.dma_start(
                      out=xqsp_h.ap()[gm * 128:(gm + 1) * 128, :], in_=o[:]),
                  deps=[("bqq", g, m)], dma=True)
            xqt.used(qsl2, ("bxqs", gm))

    for idx in range(40, 192):
        pp_piece(idx)

    dbg_bufs.update(d_qT=(qT, BF16), d_pA=(pX[0], BF16), d_attnfm=(attn_fm, BF16),
                    d_sxq=(sxq, F32))
    if upto == "B":
        _emit_debug(S, nc, debug, dbg_bufs)
        S.emit()
        return nc
    S.fence("B")
    S.emit()
    bst.close()
    ab.close()

    fst = contextlib.ExitStack()
    xqT = sb(fst, "xqT", [128, ND * T], BF16)
    xqrl = Rot([sb(fst, f"xqrl{i}", [128, D], BF16) for i in range(2)], "xqr")
    wgst = Rot([sb(fst, "wgst0", [128, D], F32)], "wg")
    wust = Rot([sb(fst, "wust0", [128, D], F32)], "wu")
    wgbf = Rot([sb(fst, "wgbf0", [128, D], BF16)], "wgb")
    wubf = Rot([sb(fst, "wubf0", [128, D], BF16)], "wub")
    pq_b = sb(fst, "pq_b", [128, D], BF16)
    qq_b = sb(fst, "qq_b", [128, D], BF16)
    terng = sb(fst, "terng", [128, D], BF16)
    ternu = sb(fst, "ternu", [128, D], BF16)
    wgT = Rot([sb(fst, "wgT0", [128, ND * 128], BF16)], "wgT")
    wuT = Rot([sb(fst, "wuT0", [128, ND * 128], BF16)], "wuT")
    gsil = Rot([sb(fst, "gsil0", [128, 512], F32)], "gs")
    gtr = sb(fst, "gtr", [128, 512], F32)
    h2c = Rot([sb(fst, f"h2c{i}", [128, 512], F32) for i in range(2)], "h2")
    sqc = Rot([sb(fst, f"sqc{i}", [128, 512], BF16) for i in range(2)], "sq")
    wdst = Rot([sb(fst, "wdst0", [128, D], F32)], "wd")
    wdbf = Rot([sb(fst, "wdbf0", [128, D], BF16)], "wdb")
    ternd = Rot([sb(fst, "ternd0", [128, D], BF16)], "td")
    wdtst = Rot([sb(fst, f"wdtst{i}", [128, D], BF16) for i in range(2)], "wdt")
    stp = sb(fst, "stp", [128, 1024], BF16)
    gbc = sb(fst, "gbc", [128, 2048], F32)
    amn = sb(fst, "amn", [128, 2048], F32)
    ftm = sb(fst, "ftm", [128, 64], F32)

    psG = Rot([ps(fst, f"psG{i}", [128, 512], F32) for i in range(2)], "psG")
    psU = Rot([ps(fst, f"psU{i}", [128, 512], F32) for i in range(2)], "psU")
    psTT = ps(fst, "psTT", [128, 1024], BF16)
    psTD = ps(fst, "psTD", [128, 1024], BF16)
    psST = [ps(fst, f"psST{i}", [128, 512], F32) for i in range(2)]

    def f0a(e):
        e.tensor_reduce(wsc[:, 0:1], ppart[:, 0:64], AX.X, OP.add)
        e.tensor_reduce(wsc[:, 1:2], ppart[:, 64:128], AX.X, OP.add)
        return e.tensor_reduce(wsc[:, 2:3], ppart[:, 128:192], AX.X, OP.add)
    S.add("vector", "ve", ("f0a",), f0a, deps=[("ppa", 191)])
    S.add("tensor", "mm", ("f0b1",),
          lambda e: e.matmul(psST[0][0:1, 0:3], onesf[:], wsc[:, 0:3],
                             start=True, stop=True),
          deps=[("f0a",), ("ini", "ones2")])
    S.add("vector", "ve", ("f0b2",),
          lambda e: e.tensor_copy(wsc[0:1, 10:13], psST[0][0:1, 0:3]),
          deps=[("f0b1",)])
    S.add("tensor", "mm", ("f0b3",),
          lambda e: e.matmul(psST[0][:, 0:3], onesr[:], wsc[0:1, 10:13],
                             start=True, stop=True),
          deps=[("f0b2",)])
    S.add("vector", "ve", ("f0b",),
          lambda e: e.tensor_copy(wsc[:, 0:3], psST[0][:, 0:3]),
          deps=[("f0b3",)])
    def f0c(e):
        i = e.tensor_scalar(wsc[:, 6:9], wsc[:, 0:3], 1.0 / NW, 1e-5, OP.mult, OP.max)
        S.bar(e, i)
        e.reciprocal(wsc[:, 0:1], wsc[:, 6:7])
        e.reciprocal(wsc[:, 1:2], wsc[:, 7:8])
        i = e.reciprocal(wsc[:, 2:3], wsc[:, 8:9])
        S.bar(e, i)
        e.tensor_scalar(wsc[:, 3:6], wsc[:, 0:3], -1.0, None, OP.mult)
        return e.tensor_tensor(wsc[:, 9:10], wsc[:, 7:8], wsc[:, 0:1], OP.mult)
    S.add("vector", "ve", ("f0c",), f0c, deps=[("f0b",)], nbar=2)
    S.add("gpsimd", "d_f0r", ("f0r",),
          lambda e: e.dma_start(out=rowb_h.ap()[:, :], in_=sxq[:, 0:16]),
          deps=[("bs2", NQB - 1, 3)], dma=True)
    S.add("sync", "d_f0l", ("f0l",),
          lambda e: e.dma_start(
              out=rows4[96:97, :].rearrange("p (m q) -> p m q", q=128),
              in_=bass.AP(rowb_h, 0, [[0, 1], [1, 16], [16, 128]]),
              allow_slow_non_contiguous=True),
          deps=[("f0r",)], dma=True)
    def f0d(e):
        i = e.reciprocal(rows4[32:33, :], rows4[96:97, :])
        S.bar(e, i)
        return e.tensor_scalar(rows4[0:1, :], rows4[32:33, :], wsc[32:33, 6:7], None, OP.mult)
    S.add("vector", "ve", ("f0d",), f0d, deps=[("f0l",), ("f0c",)], nbar=1)
    for n in range(4):
        S.add("tensor", "mm", ("f0g", n),
              lambda e, n=n: e.matmul(psST[1][:], onesr[:],
                                      rows4[0:1, n * 512:(n + 1) * 512],
                                      start=True, stop=True),
              deps=[("f0d",), ("f0ge", n - 1) if n > 0 else None])
        S.add("vector", "ve", ("f0ge", n),
              lambda e, n=n: e.tensor_copy(gbc[:, n * 512:(n + 1) * 512], psST[1][:]),
              deps=[("f0g", n)])
    S.add("vector", "ve", ("f0g",), lambda e: e.engine_nop(),
          deps=[("f0ge", n) for n in range(4)])

    last_tt = [None]
    xq_sp_done = [S.last("d_xqs0"), S.last("d_xqs1")]
    for m in range(NT):
        sl, buf, fr = xqrl.next()
        S.add("sync", xqrl.ch(sl), ("fxl", m),
              lambda e, b=buf, m=m: e.dma_start(
                  out=b[:], in_=xqsp_h.ap()[m * 128:(m + 1) * 128, :]),
              deps=[fr] + xq_sp_done, dma=True)
        for j in range(2):
            def xtf(e, b=buf, j=j):
                for c in range(8 * j, 8 * j + 8):
                    i = e.transpose(psTT[:, (c % 8) * 128:(c % 8 + 1) * 128],
                                    b[:, c * 128:(c + 1) * 128], identb[:])
                return i
            S.add("tensor", "mm", ("fxt", m, j), xtf, deps=[("fxl", m), last_tt[0]])
            def xte(e, m=m, j=j):
                dst = xqT[:].rearrange("p (f t) -> p f t", f=ND)[:, 8 * j:8 * j + 8,
                                                                m * 128:(m + 1) * 128]
                return e.tensor_copy(dst, psTT[:].rearrange("p (s c) -> p s c", s=8))
            S.add("vector", "ve", ("fxe", m, j), xte, deps=[("fxt", m, j)])
            last_tt[0] = ("fxe", m, j)
        xqrl.used(sl, ("fxt", m, 1))

    S.add("vector", "ve", ("f1amn",), lambda e: e.memset(amn[:], 3.0e38))
    last_st = {0: ("f0b",), 1: ("f0ge", 3)}
    last_fde = [None]
    for ht in range(NFH):
        gsl, gst, gfr = wgst.next()
        usl, ust, ufr = wust.next()
        S.add("sync", wgst.ch(gsl), ("fgl", ht),
              lambda e, b=gst, ht=ht: e.dma_start(
                  out=b[:], in_=wg_h.ap()[ht * 128:(ht + 1) * 128, :]), deps=[gfr], dma=True)
        S.add("sync", wust.ch(usl), ("ful", ht),
              lambda e, b=ust, ht=ht: e.dma_start(
                  out=b[:], in_=wu_h.ap()[ht * 128:(ht + 1) * 128, :]), deps=[ufr], dma=True)
        gbsl, gbf, gbfr = wgbf.next()
        ubsl, ubf, ubfr = wubf.next()
        S.add("scalar", "sc", ("fgc", ht),
              lambda e, i_=gst, o=gbf: e.activation(o[:], i_[:], AF.Copy),
              deps=[("fgl", ht)] + [gbfr])
        wgst.used(gsl, ("fgc", ht))
        wgst.used(gsl, ("fgq", ht))
        S.add("scalar", "sc", ("fuc", ht),
              lambda e, i_=ust, o=ubf: e.activation(o[:], i_[:], AF.Copy),
              deps=[("ful", ht)] + [ubfr])
        wust.used(usl, ("fuc", ht))
        wust.used(usl, ("fuq", ht))
        def qg(e, w=gst):
            e.tensor_scalar(pq_b[:], w[:], wsc[:, 0:1], 0.5, OP.mult, OP.is_ge)
            i = e.tensor_scalar(qq_b[:], w[:], wsc[:, 3:4], 0.5, OP.mult, OP.is_ge)
            S.bar(e, i)
            return e.tensor_tensor(terng[:], pq_b[:], qq_b[:], OP.subtract)
        S.add("vector", "ve", ("fgq", ht), qg,
              deps=[("fgc", ht), ("f0c",), ("ftt", "g", ht - 1, 1) if ht else None], nbar=1)
        wgbf.used(gbsl, ("fgq", ht))
        def qu(e, w=ust):
            e.tensor_scalar(pq_b[:], w[:], wsc[:, 1:2], 0.5, OP.mult, OP.is_ge)
            i = e.tensor_scalar(qq_b[:], w[:], wsc[:, 4:5], 0.5, OP.mult, OP.is_ge)
            S.bar(e, i)
            return e.tensor_tensor(ternu[:], pq_b[:], qq_b[:], OP.subtract)
        S.add("vector", "ve", ("fuq", ht), qu,
              deps=[("fuc", ht), ("f0c",), ("ftt", "u", ht - 1, 1) if ht else None], nbar=1)
        wubf.used(ubsl, ("fuq", ht))
        gtsl, gT, gTfr = wgT.next()
        utsl, uT, uTfr = wuT.next()
        for nm, tern, dstT, dfr, qkey in (("g", terng, gT, gTfr, ("fgq", ht)),
                                          ("u", ternu, uT, uTfr, ("fuq", ht))):
            for j in range(2):
                def ttf2(e, tern=tern, j=j):
                    for c in range(8 * j, 8 * j + 8):
                        i = e.transpose(psTT[:, (c % 8) * 128:(c % 8 + 1) * 128],
                                        tern[:, c * 128:(c + 1) * 128], identb[:])
                    return i
                S.add("tensor", "mm", ("ftt", nm, ht, j), ttf2,
                      deps=[qkey, last_tt[0]] + (dfr if j == 0 else []))
                def tte(e, dstT=dstT, j=j):
                    dst = dstT[:].rearrange("p (f c) -> p f c", f=ND)[:, 8 * j:8 * j + 8, :]
                    return e.tensor_copy(dst, psTT[:].rearrange("p (s c) -> p s c", s=8))
                S.add("vector", "ve", ("fte", nm, ht, j), tte, deps=[("ftt", nm, ht, j)])
                last_tt[0] = ("fte", nm, ht, j)
        for n in range(4):
            gpsl, gps, gpfr = psG.next()
            def gmm(e, gT=gT, n=n, p_=gps):
                for kc in range(16):
                    i = e.matmul(p_[:], gT[:, kc * 128:(kc + 1) * 128],
                                 xqT[:, kc * T + n * 512:kc * T + n * 512 + 512],
                                 start=(kc == 0), stop=(kc == 15))
                return i
            S.add("tensor", "mm", ("fgm", ht, n), gmm,
                  deps=[("fte", "g", ht, 1), ("fxe", NT - 1, 1)] + [gpfr])
            gssl, gsb3, gsfr = gsil.next()
            S.add("vector", "ve", ("fgs", ht, n),
                  lambda e, p_=gps, n=n: e.tensor_tensor(
                      gtr[:], p_[:], gbc[:, n * 512:(n + 1) * 512], OP.mult),
                  deps=[("fgm", ht, n), ("f0g",)])
            psG.used(gpsl, ("fgs", ht, n))
            S.add("scalar", "sc", ("fsi", ht, n),
                  lambda e, o=gsb3: e.activation(o[:], gtr[:], AF.Silu),
                  deps=[("fgs", ht, n)] + [gsfr])
            S.add("vector", "ve", ("fgu", ht, n),
                  lambda e, o=gsb3, n=n: e.scalar_tensor_tensor(
                      o[:], o[:], wsc[:, 9:10], gbc[:, n * 512:(n + 1) * 512],
                      OP.mult, OP.mult),
                  deps=[("fsi", ht, n)])
            upsl, ups, upfr = psU.next()
            def umm(e, uT=uT, n=n, p_=ups):
                for kc in range(16):
                    i = e.matmul(p_[:], uT[:, kc * 128:(kc + 1) * 128],
                                 xqT[:, kc * T + n * 512:kc * T + n * 512 + 512],
                                 start=(kc == 0), stop=(kc == 15))
                return i
            S.add("tensor", "mm", ("fum", ht, n), umm,
                  deps=[("fte", "u", ht, 1), ("fxe", NT - 1, 1)] + [upfr])
            h2sl, h2b, h2fr = h2c.next()
            S.add("vector", "ve", ("fh2", ht, n),
                  lambda e, p_=ups, o=h2b, g_=gsb3: e.tensor_tensor(o[:], p_[:], g_[:], OP.mult),
                  deps=[("fum", ht, n), ("fgu", ht, n)] + [h2fr])
            psU.used(upsl, ("fh2", ht, n))
            gsil.used(gssl, ("fh2", ht, n))
            S.add("vector", "ve", ("famP", ht, n),
                  lambda e, o=h2b, n=n: e.tensor_tensor(
                      cbc[:, n * 512:(n + 1) * 512], cbc[:, n * 512:(n + 1) * 512],
                      o[:], OP.max),
                  deps=[("fh2", ht, n), ("ini", "amx0")])
            S.add("vector", "ve", ("famN", ht, n),
                  lambda e, o=h2b, n=n: e.tensor_tensor(
                      amn[:, n * 512:(n + 1) * 512], amn[:, n * 512:(n + 1) * 512],
                      o[:], OP.min),
                  deps=[("fh2", ht, n), ("f1amn",)])
            sqsl, sqb3, sqfr = sqc.next()
            S.add("scalar", "sc", ("fsq", ht, n),
                  lambda e, i_=h2b, o=sqb3: e.square(o[:], i_[:]),
                  deps=[("fh2", ht, n)] + [sqfr])
            bank = n // 2
            S.add("tensor", "mm", ("fsm", ht, n),
                  lambda e, ht=ht, n=n, q_=sqb3, bank=bank: e.matmul(
                      psST[bank][(n % 2) * 64:(n % 2) * 64 + 64, :],
                      onesband[:, 63 - ht:127 - ht], q_[:],
                      start=(ht == 0), stop=(ht == NFH - 1), skip_group_check=True),
                  deps=[("fsq", ht, n), last_st.get(bank), ("ini", 2)])
            last_st[bank] = ("fsm", ht, n)
            sqc.used(sqsl, ("fsm", ht, n))
            S.add("gpsimd", h2c.ch(h2sl), ("fh2s", ht, n),
                  lambda e, o=h2b, ht=ht, n=n: e.dma_start(
                      out=h2sp_h.ap()[ht * 128:(ht + 1) * 128, n * 512:(n + 1) * 512],
                      in_=o[:]),
                  deps=[("fh2", ht, n)], dma=True)
            h2c.used(h2sl, ("fh2s", ht, n))
            h2c.used(h2sl, ("famP", ht, n))
            h2c.used(h2sl, ("famN", ht, n))
        wgT.used(gtsl, ("fgm", ht, 3))
        wuT.used(utsl, ("fum", ht, 3))
        dt, pc = divmod(ht, 4)
        dsl, dstg, dfr2 = wdst.next()
        S.add("sync", wdst.ch(dsl), ("fdl", ht),
              lambda e, b=dstg, dt=dt, pc=pc: e.dma_start(
                  out=b[:], in_=wd_h.ap()[dt * 128:(dt + 1) * 128, pc * 2048:(pc + 1) * 2048]),
              deps=[dfr2], dma=True)
        dbsl, dbf, dbfr = wdbf.next()
        S.add("scalar", "sc", ("fdc", ht),
              lambda e, i_=dstg, o=dbf: e.activation(o[:], i_[:], AF.Copy),
              deps=[("fdl", ht)] + [dbfr])
        wdst.used(dsl, ("fdc", ht))
        wdst.used(dsl, ("fdq", ht))
        tdsl, tdb, tdfr = ternd.next()
        def qd(e, w=dstg, o=tdb):
            e.tensor_scalar(pq_b[:], w[:], wsc[:, 2:3], 0.5, OP.mult, OP.is_ge)
            i = e.tensor_scalar(qq_b[:], w[:], wsc[:, 5:6], 0.5, OP.mult, OP.is_ge)
            S.bar(e, i)
            return e.tensor_tensor(o[:], pq_b[:], qq_b[:], OP.subtract)
        S.add("vector", "ve", ("fdq", ht), qd, deps=[("fdc", ht), ("f0c",)] + [tdfr], nbar=1)
        wdbf.used(dbsl, ("fdq", ht))
        wtsl, wtb, wtfr = wdtst.next()
        for j in range(2):
            def dtf(e, o=tdb, j=j):
                for c in range(8 * j, 8 * j + 8):
                    i = e.transpose(psTD[:, (c % 8) * 128:(c % 8 + 1) * 128],
                                    o[:, c * 128:(c + 1) * 128], identb[:])
                return i
            S.add("tensor", "mm", ("fdt", ht, j), dtf,
                  deps=[("fdq", ht), last_fde[0]] + (wtfr if j == 0 else []))
            def dte(e, wtb=wtb, j=j):
                dst = wtb[:].rearrange("p (c q) -> p c q", c=ND)[:, 8 * j:8 * j + 8, :]
                return e.tensor_copy(dst, psTD[:].rearrange("p (s c) -> p s c", s=8))
            S.add("vector", "ve", ("fde", ht, j), dte, deps=[("fdt", ht, j)])
            last_fde[0] = ("fde", ht, j)
        ternd.used(tdsl, ("fdt", ht, 1))
        def dts(e, wtb=wtb, dt=dt, pc=pc):
            dst = bass.AP(wdtsp_h, (pc * 2048) * D + dt * 128,
                          [[D, 128], [128 * D, 16], [1, 128]])
            return e.dma_start(out=dst, in_=wtb[:].rearrange("p (c q) -> p c q", c=16))
        S.add("gpsimd", wdtst.ch(wtsl), ("fds", ht), dts, deps=[("fde", ht, 1)], dma=True)
        wdtst.used(wtsl, ("fds", ht))

    S.add("vector", "ve", ("fsv",),
          lambda e: (e.tensor_copy(stp[:, 0:512], psST[0][:]),
                     e.tensor_copy(stp[:, 512:1024], psST[1][:]))[-1],
          deps=[last_st[0], last_st[1]])
    for n in range(4):
        po = (n % 2) * 64
        for c in range(4):
            idx = n * 4 + c
            S.add("tensor", "mm", ("fstt", idx),
                  lambda e, po=po, n=n, c=c, idx=idx: e.transpose(
                      psTT[:, (idx % 8) * 128:(idx % 8) * 128 + 64],
                      stp[po:po + 64, (n // 2) * 512 + c * 128:(n // 2) * 512 + (c + 1) * 128],
                      identb[po:po + 64, po:po + 64]),
                  deps=[("fsv",), ("fstr", idx - 1) if idx >= 1 else None,
                        ("fte", "u", NFH - 1, 1)])
            S.add("vector", "ve", ("fstr", idx),
                  lambda e, idx=idx: e.tensor_reduce(
                      ftm[:, idx:idx + 1], psTT[:, (idx % 8) * 128:(idx % 8) * 128 + 64],
                      AX.X, OP.add),
                  deps=[("fstt", idx)])
    fam_last = [("famP", NFH - 1, 3), ("famN", NFH - 1, 3)]
    for c in range(16):
        S.add("tensor", "mm", ("famt", c),
              lambda e, c=c: e.transpose(psST[0][:, (c % 4) * 128:(c % 4 + 1) * 128],
                                         cbc[:, c * 128:(c + 1) * 128], identf[:]),
              deps=[("fsv",), ("famr", c - 1) if c >= 1 else None] + fam_last)
        S.add("vector", "ve", ("famr", c),
              lambda e, c=c: e.tensor_reduce(
                  ftm[:, 16 + c:17 + c], psST[0][:, (c % 4) * 128:(c % 4 + 1) * 128],
                  AX.X, OP.max),
              deps=[("famt", c)])
        S.add("tensor", "mm", ("fant", c),
              lambda e, c=c: e.transpose(psST[1][:, (c % 4) * 128:(c % 4 + 1) * 128],
                                         amn[:, c * 128:(c + 1) * 128], identf[:]),
              deps=[("fsv",), ("fanr", c - 1) if c >= 1 else None] + fam_last)
        S.add("vector", "ve", ("fanr", c),
              lambda e, c=c: e.tensor_reduce(
                  ftm[:, 32 + c:33 + c], psST[1][:, (c % 4) * 128:(c % 4 + 1) * 128],
                  AX.X, OP.min),
              deps=[("fant", c)])
    def ftok1(e):
        i = e.tensor_scalar(ftm[:, 32:48], ftm[:, 32:48], -1.0, None, OP.mult)
        S.bar(e, i)
        e.tensor_tensor(ftm[:, 16:32], ftm[:, 16:32], ftm[:, 32:48], OP.max)
        i = e.tensor_scalar(ftm[:, 0:16], ftm[:, 0:16], 1.0 / FF, EPS_RMS, OP.mult, OP.add)
        S.bar(e, i)
        return e.reciprocal(ftm[:, 32:48], ftm[:, 0:16])
    S.add("vector", "ve", ("ftok1",), ftok1, nbar=2,
          deps=[("fstr", i) for i in range(16)] + [("famr", c) for c in range(16)]
               + [("fanr", c) for c in range(16)])
    S.add("scalar", "sc", ("ftok2",),
          lambda e: e.sqrt(ftm[:, 32:48], ftm[:, 32:48]), deps=[("ftok1",)])
    def ftok3(e):
        i = e.tensor_tensor(ftm[:, 48:64], ftm[:, 32:48], ftm[:, 16:32], OP.mult)
        S.bar(e, i)
        i = e.tensor_scalar(ftm[:, 48:64], ftm[:, 48:64], 1e-5, None, OP.max)
        S.bar(e, i)
        i = e.reciprocal(ftm[:, 48:64], ftm[:, 48:64])
        S.bar(e, i)
        i = e.tensor_scalar(ftm[:, 48:64], ftm[:, 48:64], 127.0, None, OP.mult)
        S.bar(e, i)
        e.tensor_tensor(ftm[:, 0:16], ftm[:, 32:48], ftm[:, 48:64], OP.mult)
        i = e.reciprocal(c3col[:, 0:16], ftm[:, 48:64])
        S.bar(e, i)
        return e.tensor_scalar(c3col[:, 0:16], c3col[:, 0:16], wsc[:, 8:9], None, OP.mult)
    S.add("vector", "ve", ("ftok3",), ftok3, deps=[("ftok2",), ("f0c",)], nbar=5)
    S.add("gpsimd", "d_f1r", ("f1r",),
          lambda e: e.dma_start(out=rowb_h.ap(), in_=ftm[:, 0:16]),
          deps=[("ftok3",)], dma=True)
    S.add("sync", "d_f1l", ("f1l",),
          lambda e: e.dma_start(
              out=rows4[0:1, :].rearrange("p (m q) -> p m q", q=128),
              in_=bass.AP(rowb_h, 0, [[0, 1], [1, 16], [16, 128]]),
              allow_slow_non_contiguous=True),
          deps=[("f1r",)], dma=True)
    for n in range(4):
        S.add("tensor", "mm", ("fcbm", n),
              lambda e, n=n: e.matmul(psST[0][:], onesr[:],
                                      rows4[0:1, n * 512:(n + 1) * 512],
                                      start=True, stop=True),
              deps=[("f1l",), ("famt", 15),
                    ("fcbe", n - 1) if n > 0 else ("famr", 15)])
        S.add("vector", "ve", ("fcbe", n),
              lambda e, n=n: e.tensor_copy(cbc[:, n * 512:(n + 1) * 512], psST[0][:]),
              deps=[("fcbm", n)])
    S.add("vector", "ve", ("fcb",), lambda e: e.engine_nop(),
          deps=[("fcbe", n) for n in range(4)])

    dbg_bufs.update(d_xqT=(xqT, BF16), d_wsc=(wsc, F32), d_cbc=(cbc, F32),
                    d_gbc=(gbc, F32), d_c3col=(c3col, F32), d_rows=(rows4, F32), d_ftm=(ftm, F32))
    if upto == "F1":
        _emit_debug(S, nc, debug, dbg_bufs)
        S.emit()
        return nc
    S.fence("F1")
    S.emit()
    fst.close()
    rw.close()

    gstk = contextlib.ExitStack()
    xq2res = sb(gstk, "xq2res", [128, NFH * 1024], I8)
    h2rl = Rot([sb(gstk, f"h2rl{i}", [128, 1024], F32) for i in range(2)], "h2r")
    tq2 = Rot([sb(gstk, "tq20", [128, 1024], F32)], "tq2")
    r162 = sb(gstk, "r162", [128, 1024], I16)
    cdf2 = sb(gstk, "cdf2", [128, 1024], F32)
    s1t2 = sb(gstk, "s1t2", [128, 1024], F32)
    wdts = Rot([sb(gstk, "wdts0", [128, NFH * 256], BF16)], "wdl")
    xcol = Rot([sb(gstk, f"xcol{i}", [128, NFH * 128], BF16) for i in range(2)], "xc")
    x2c = Rot([sb(gstk, f"x2c{i}", [128, 256], F32) for i in range(2)], "x2c")
    z2h = sb(gstk, "z2h", [128, 8 * D], BF16)
    bnb2 = sb(gstk, "bnb2", [128, 24], F32)
    outst = Rot([sb(gstk, "outst0", [128, D], F32)], "os")
    psD = Rot([ps(gstk, f"psD{i}", [128, 512], F32) for i in range(2)], "psD")

    h2_sp_done = [S.last("d_h20"), S.last("d_h21")]
    wdt_sp_done = [S.last("d_wdt0"), S.last("d_wdt1")]
    x2_sp_done = [S.last("d_x2s0"), S.last("d_x2s1")]
    NB = 8
    last_g3c = [None]
    for hf in range(2):
        for ht in range(NFH):
            sl, buf, fr = h2rl.next()
            S.add("sync", h2rl.ch(sl), ("g2l", hf, ht),
                  lambda e, b=buf, hf=hf, ht=ht: e.dma_start(
                      out=b[:], in_=h2sp_h.ap()[ht * 128:(ht + 1) * 128,
                                                hf * 1024:(hf + 1) * 1024]),
                  deps=[fr] + h2_sp_done, dma=True)
            tsl, tb, tfr = tq2.next()
            S.add("gpsimd", "gp", ("g2m", hf, ht),
                  lambda e, b=buf, o=tb, hf=hf: e.tensor_tensor(
                      o[:], b[:], cbc[:, hf * 1024:(hf + 1) * 1024], OP.mult),
                  deps=[("g2l", hf, ht), ("fcb",), last_g3c[0] if ht == 0 else None] + tfr)
            h2rl.used(sl, ("g2m", hf, ht))
            def q2(e, t=tb, hf=hf, ht=ht):
                i = e.tensor_copy(r162[:], t[:])
                S.bar(e, i)
                i = e.tensor_tensor(cdf2[:], t[:], r162[:], OP.subtract)
                S.bar(e, i)
                i = e.scalar_tensor_tensor(s1t2[:], cdf2[:], -0.5, r162[:], OP.is_lt, OP.subtract)
                S.bar(e, i)
                return e.scalar_tensor_tensor(xq2res[:, ht * 1024:(ht + 1) * 1024],
                                              cdf2[:], 0.5, s1t2[:], OP.is_gt, OP.subtract)
            S.add("vector", "ve", ("g2q", hf, ht), q2, deps=[("g2m", hf, ht)], nbar=3)
            tq2.used(tsl, ("g2q", hf, ht))
        for n in range(NB):
            wsl4, wb4, wfr4 = wdts.next()
            S.add("sync", wdts.ch(wsl4), ("g3w", hf, n),
                  lambda e, b=wb4, n=n: e.dma_start(
                      out=b[:].rearrange("p (k c) -> p k c", k=NFH),
                      in_=wdtsp_h.ap()[:, n * 256:(n + 1) * 256].rearrange(
                          "(k p) c -> p k c", p=128)),
                  deps=[wfr4] + wdt_sp_done, dma=True)
            for m in range(8):
                gm = hf * 8 + m
                csl, cb4, cfr4 = xcol.next()
                S.add("gpsimd", "gp", ("g3c", hf, n, m),
                      lambda e, o=cb4, m=m: e.tensor_copy(
                          o[:].rearrange("p (k c) -> p k c", k=NFH),
                          xq2res[:].rearrange("p (k c) -> p k c", k=NFH)[
                              :, :, m * 128:(m + 1) * 128]),
                      deps=[("g2q", hf, NFH - 1)] + cfr4)
                last_g3c[0] = ("g3c", hf, n, m)
                dsl4, dps4, dfr4 = psD.next()
                def dmm(e, w=wb4, c_=cb4, p_=dps4):
                    for hk in range(NFH):
                        i = e.matmul(p_[0:128, 0:256], c_[:, hk * 128:(hk + 1) * 128],
                                     w[:, hk * 256:(hk + 1) * 256],
                                     start=(hk == 0), stop=(hk == NFH - 1))
                    return i
                S.add("tensor", "mm", ("g3m", hf, n, m), dmm,
                      deps=[("g3w", hf, n), ("g3c", hf, n, m)] + dfr4)
                xcol.used(csl, ("g3m", hf, n, m))
                xsl4, xb4, xfr4 = x2c.next()
                S.add("sync", x2c.ch(xsl4), ("g3x", hf, n, m),
                      lambda e, b=xb4, gm=gm, n=n: e.dma_start(
                          out=b[:], in_=x2sp_h.ap()[gm * 128:(gm + 1) * 128,
                                                    n * 256:(n + 1) * 256]),
                      deps=[xfr4] + x2_sp_done, dma=True)
                S.add("vector", "ve", ("g3z", hf, n, m),
                      lambda e, p_=dps4, xb=xb4, m=m, n=n, gm=gm: e.scalar_tensor_tensor(
                          z2h[:, m * D + n * 256:m * D + n * 256 + 256],
                          p_[0:128, 0:256], c3col[:, gm:gm + 1], xb[:], OP.mult, OP.add),
                      deps=[("g3m", hf, n, m), ("g3x", hf, n, m), ("ftok3",)])
                x2c.used(xsl4, ("g3z", hf, n, m))
                psD.used(dsl4, ("g3z", hf, n, m))
            wdts.used(wsl4, ("g3m", hf, n, 7))
        for m in range(8):
            gm = hf * 8 + m
            def ln2(e, m=m):
                for c in range(4):
                    i = e.bn_stats(bnb2[:, c * 6:(c + 1) * 6],
                                   z2h[:, m * D + c * 512:m * D + c * 512 + 512])
                S.bar(e, i)
                i = e.bn_aggr(smv[:, 0:2], bnb2[:, 0:24])
                S.bar(e, i)
                i = e.tensor_scalar(smv[:, 2:3], smv[:, 1:2], 1.0, EPS_LN, OP.mult, OP.add)
                S.bar(e, i)
                return e.reciprocal(smv[:, 3:4], smv[:, 2:3])
            prev_q = ("g4q", hf, m - 1) if m > 0 else \
                (("g4q", hf - 1, 7) if hf > 0 else None)
            S.add("vector", "ve", ("g4s", hf, m), ln2,
                  deps=[("g3z", hf, n_, m) for n_ in range(NB)] + [prev_q], nbar=3)
            S.add("scalar", "sc", ("g4q", hf, m),
                  lambda e: e.sqrt(smv[:, 4:5], smv[:, 3:4]), deps=[("g4s", hf, m)])
            osl, ob, ofr = outst.next()
            def app2(e, m=m, ob=ob):
                i = e.tensor_tensor(smv[:, 5:6], smv[:, 0:1], smv[:, 4:5], OP.mult)
                S.bar(e, i)
                i = e.tensor_scalar(smv[:, 6:7], smv[:, 5:6], -1.0, None, OP.mult)
                S.bar(e, i)
                return e.tensor_scalar(ob[:], z2h[:, m * D:(m + 1) * D],
                                       smv[:, 4:5], smv[:, 6:7], OP.mult, OP.add)
            S.add("vector", "ve", ("g4a", hf, m), app2, deps=[("g4q", hf, m)] + ofr, nbar=2)
            S.add("gpsimd", outst.ch(osl), ("g4o", gm),
                  lambda e, ob=ob, gm=gm: e.dma_start(
                      out=out_h.ap()[gm * 128:(gm + 1) * 128, :], in_=ob[:]),
                  deps=[("g4a", hf, m)], dma=True)
            outst.used(osl, ("g4o", gm))

    S.add("gpsimd", "gp", ("fin",), lambda e: e.engine_nop(),
          deps=[("g4o", NT - 1)])
    dbg_bufs.update(d_xq2=(xq2res, I8))
    _emit_debug(S, nc, debug, dbg_bufs)
    S.emit()
    gstk.close()
    top.close()
    return nc


def _emit_debug(S, nc, debug, dbg_bufs):
    if not debug:
        return
    names = [n for n in debug if n in dbg_bufs]
    alldeps = list(S.lastkey.values())
    for name in names:
        buf, dt = dbg_bufs[name]
        hh = nc.dram_tensor(name, list(buf.shape), dt, kind="ExternalOutput")
        S.add("gpsimd", f"d_dbg_{name}", ("dbg", name),
              lambda e, b=buf, hh=hh: e.dma_start(out=hh.ap(), in_=b[:]),
              deps=alldeps, dma=True)
    S.add("gpsimd", "gp", ("dbgw",), lambda e: e.engine_nop(),
          deps=[("dbg", n) for n in names])


def make_inputs(L0_attn_x, L0_attn_Wq, L0_attn_bq, L0_attn_Wk, L0_attn_bk,
                L0_attn_Wv, L0_attn_bv, L0_attn_Wo, L0_attn_bo,
                L0_ffn_w_gate, L0_ffn_w_up, L0_ffn_w_down):
    consts = host_constants()
    f32 = np.float32
    base = {
        "Wq": np.ascontiguousarray(L0_attn_Wq, dtype=f32),
        "bq": np.ascontiguousarray(L0_attn_bq, dtype=f32).reshape(1, D),
        "Wk": np.ascontiguousarray(L0_attn_Wk, dtype=f32),
        "bk": np.ascontiguousarray(L0_attn_bk, dtype=f32).reshape(1, 512),
        "Wv": np.ascontiguousarray(L0_attn_Wv, dtype=f32),
        "bv": np.ascontiguousarray(L0_attn_bv, dtype=f32).reshape(1, 512),
        "Wo": np.ascontiguousarray(L0_attn_Wo, dtype=f32),
        "bo": np.ascontiguousarray(L0_attn_bo, dtype=f32).reshape(1, D),
        "w_gate": np.ascontiguousarray(L0_ffn_w_gate, dtype=f32),
        "w_up": np.ascontiguousarray(L0_ffn_w_up, dtype=f32),
        "w_down": np.ascontiguousarray(L0_ffn_w_down, dtype=f32),
        "cosT": consts["cosT"], "sinT": consts["sinT"],
        "ident_bf": consts["ident_bf"], "ident_f32": consts["ident_f32"],
        "onesband": consts["onesband"],
    }
    x = np.ascontiguousarray(L0_attn_x, dtype=f32)
    return [dict(base, x=x[c]) for c in range(8)]


def kernel(L0_attn_x, L0_attn_Wq, L0_attn_bq, L0_attn_Wk, L0_attn_bk,
           L0_attn_Wv, L0_attn_bv, L0_attn_Wo, L0_attn_bo,
           L0_ffn_w_gate, L0_ffn_w_up, L0_ffn_w_down):
    nc = bass.Bass("TRN2", target_bir_lowering=False)
    nc = build(nc, debug=(), upto=BUILD_UPTO)
    in_maps = make_inputs(L0_attn_x, L0_attn_Wq, L0_attn_bq, L0_attn_Wk, L0_attn_bk,
                          L0_attn_Wv, L0_attn_bv, L0_attn_Wo, L0_attn_bo,
                          L0_ffn_w_gate, L0_ffn_w_up, L0_ffn_w_down)
    res = run_bass_kernel_spmd(nc, in_maps, core_ids=list(range(8)))
    return np.stack([np.asarray(r["out"]) for r in res.results], axis=0)


# version 05
# speedup vs baseline: 1.1698x; 1.1698x over previous
import contextlib
import numpy as np
import ml_dtypes
import concourse.bass as bass
import concourse.mybir as mybir
from concourse.bass_utils import run_bass_kernel_spmd
from concourse.alu_op_type import AluOpType as OP

F32 = mybir.dt.float32
BF16 = mybir.dt.bfloat16
I16 = mybir.dt.int16
I8 = mybir.dt.int8
AF = mybir.ActivationFunctionType
AX = mybir.AxisListType

T, D, NH, KVH, HD, FF = 2048, 2048, 32, 8, 64, 8192
NT, ND, QB, NQB, NFH = T // 128, D // 128, 512, 4, FF // 128
EPS_LN, EPS_RMS = 1e-5, 1e-6
NW = FF * D

BUILD_UPTO = "ALL"


def host_constants():
    inv_freq = 1.0 / (10000.0 ** (np.arange(0, HD, 2, dtype=np.float64) / HD))
    fr = np.outer(np.arange(T, dtype=np.float64), inv_freq)
    s = 8.0 ** -0.5
    cosf = np.repeat(np.cos(fr), 2, axis=1) * s
    sinf = np.repeat(np.sin(fr), 2, axis=1) * s
    bf = ml_dtypes.bfloat16
    return {
        "cosT": np.ascontiguousarray(np.tile(cosf, (1, NH))).astype(bf),
        "sinT": np.ascontiguousarray(np.tile(sinf, (1, NH))).astype(bf),
        "ident_bf": np.eye(128, dtype=np.float32).astype(bf),
        "ident_f32": np.eye(128, dtype=np.float32),
        "onesband": ((np.arange(127)[None, :] == 63) * np.ones((128, 1))).astype(np.float32).astype(bf),
    }


class Step:
    __slots__ = ("eng", "ch", "inc", "key", "deps", "fn")

    def __init__(self, eng, ch, inc, key, deps, fn):
        self.eng, self.ch, self.inc, self.key, self.deps, self.fn = \
            eng, ch, inc, key, deps, fn


class Sched:
    def __init__(self, nc, stack):
        self.nc, self.stack = nc, stack
        self.sems, self.table, self.ch_eng, self.cnt = {}, {}, {}, {}
        self.lastkey = {}
        self.steps = []
        self.ecnt = {}
        self._cur_ch = None

    def sem(self, ch):
        if ch not in self.sems:
            self.sems[ch] = self.stack.enter_context(self.nc.semaphore(ch))
        return self.sems[ch]

    def add(self, eng, ch, key, fn, deps=(), dma=False, nbar=0):
        if ch in self.ch_eng:
            assert self.ch_eng[ch] == eng, f"{ch}: {self.ch_eng[ch]} vs {eng}"
        else:
            self.ch_eng[ch] = eng
        inc = 16 if dma else 1
        self.cnt[ch] = self.cnt.get(ch, 0) + inc + nbar
        assert key not in self.table, f"dup {key}"
        self.table[key] = (ch, self.cnt[ch], eng)
        self.lastkey[ch] = key
        flat = []
        for d in deps:
            if d is None:
                continue
            if isinstance(d, list):
                flat += [x for x in d if x is not None]
            else:
                flat.append(d)
        self.steps.append(Step(eng, ch, inc, key, flat, fn))

    def bar(self, e, inst):
        ch = self._cur_ch
        self.ecnt[ch] = self.ecnt.get(ch, 0) + 1
        inst.then_inc(self.sem(ch), 1)
        e.wait_ge(self.sem(ch), self.ecnt[ch])

    def last(self, ch):
        return self.lastkey.get(ch)

    def fence(self, tag):
        deps = [k for c, k in self.lastkey.items() if c.startswith("d_")]
        self.add("vector", "ve", ("fence", tag), lambda e: e.engine_nop(), deps=deps)

    def emit(self):
        per = {}
        for s in self.steps:
            per.setdefault(s.eng, []).append(s)
        table = self.table

        ENGCH = {"vector": "ve", "scalar": "sc", "gpsimd": "gp"}

        def mk(engsteps):
            def body(eng):
                waited = {}
                for s in engsteps:
                    own = ENGCH.get(s.eng)
                    ownv = self.ecnt.get(own, 0) if own is not None else 0
                    if own is not None and ownv > waited.get(own, -1):
                        eng.wait_ge(self.sem(own), ownv)
                        waited[own] = ownv
                    for d in s.deps:
                        ch, v, peng = table[d]
                        if waited.get(ch, -1) >= v:
                            continue
                        eng.wait_ge(self.sem(ch), v)
                        waited[ch] = v
                    self._cur_ch = s.ch
                    inst = s.fn(eng)
                    inst.then_inc(self.sem(s.ch), s.inc)
                    self.ecnt[s.ch] = self.ecnt.get(s.ch, 0) + s.inc
                    assert self.ecnt[s.ch] == table[s.key][1], \
                        f"step {s.key}: emitted {self.ecnt[s.ch]} != planned {table[s.key][1]} (nbar mismatch)"
            return body

        with self.nc.Block() as blk:
            for engname, engsteps in per.items():
                getattr(blk, engname)(mk(engsteps))
        self.steps = []


class Rot:
    def __init__(self, bufs, name):
        self.bufs, self.name, self.n = bufs, name, len(bufs)
        self.i = -1
        self.cons = [[] for _ in range(self.n)]

    def next(self):
        self.i += 1
        s = self.i % self.n
        deps = self.cons[s]
        self.cons[s] = []
        return s, self.bufs[s], deps

    def ch(self, s):
        return f"d_{self.name}{s}"

    def used(self, s, key):
        self.cons[s].append(key)


def build(nc, debug=(), upto="ALL"):
    debug = set(debug)
    order = ["A", "B", "F1", "ALL"]
    lvl = order.index(upto)

    def want(p):
        return order.index(p) <= lvl

    def dram(name, shape, dtype, kind=None):
        return nc.dram_tensor(name, shape, dtype) if kind is None else \
            nc.dram_tensor(name, shape, dtype, kind=kind)

    x_h = dram("x", [T, D], F32, kind="ExternalInput")
    wq_h = dram("Wq", [D, D], F32, kind="ExternalInput")
    bq_h = dram("bq", [1, D], F32, kind="ExternalInput")
    wk_h = dram("Wk", [D, 512], F32, kind="ExternalInput")
    bk_h = dram("bk", [1, 512], F32, kind="ExternalInput")
    wv_h = dram("Wv", [D, 512], F32, kind="ExternalInput")
    bv_h = dram("bv", [1, 512], F32, kind="ExternalInput")
    wo_h = dram("Wo", [D, D], F32, kind="ExternalInput")
    bo_h = dram("bo", [1, D], F32, kind="ExternalInput")
    wg_h = dram("w_gate", [FF, D], F32, kind="ExternalInput")
    wu_h = dram("w_up", [FF, D], F32, kind="ExternalInput")
    wd_h = dram("w_down", [D, FF], F32, kind="ExternalInput")
    cos_h = dram("cosT", [T, D], BF16, kind="ExternalInput")
    sin_h = dram("sinT", [T, D], BF16, kind="ExternalInput")
    identb_h = dram("ident_bf", [128, 128], BF16, kind="ExternalInput")
    identf_h = dram("ident_f32", [128, 128], F32, kind="ExternalInput")
    onesband_h = dram("onesband", [128, 127], BF16, kind="ExternalInput")
    out_h = dram("out", [T, D], F32, kind="ExternalOutput")

    qsp_h = dram("q_spill", [T, D], BF16)
    wobf_h = dram("wo_bf", [D, D], BF16)
    x2sp_h = dram("x2_spill", [T, D], F32)
    xqsp_h = dram("xq_spill", [T, D], BF16)
    h2sp_h = dram("h2_spill", [FF, T], F32)
    wdtsp_h = dram("wdt_spill", [FF, D], BF16)
    rowb_h = dram("rowbounce", [128, 16], F32)
    rowf_h = dram("rowflat", [1, 2048], F32)

    top = contextlib.ExitStack()
    S = Sched(nc, top)
    dbg_bufs = {}

    def sb(stk, n, shp, dt):
        return stk.enter_context(nc.sbuf_tensor(n, shp, dt))

    def ps(stk, n, shp, dt):
        return stk.enter_context(nc.psum_tensor(n, shp, dt))

    identb = sb(top, "identb", [128, 128], BF16)
    identf = sb(top, "identf", [128, 128], F32)
    onesband = sb(top, "onesb", [128, 127], BF16)
    ppart = sb(top, "ppart", [128, 384], F32)
    wsc = sb(top, "wsc", [128, 16], F32)
    sxq = sb(top, "sxq", [128, NT], F32)
    c3col = sb(top, "c3col", [128, NT], F32)
    cbc = sb(top, "cbc", [128, 2048], F32)
    smv = sb(top, "smv", [128, 16], F32)
    onesr = sb(top, "onesr", [1, 128], F32)
    onesf = sb(top, "onesf", [128, 1], F32)

    ab = contextlib.ExitStack()
    kT = sb(ab, "kT", [128, 4 * T], BF16)
    v_buf = sb(ab, "v_buf", [128, NT * 520], BF16)
    pp_stg = Rot([sb(ab, f"pp_stg{i}", [128, 1024], F32) for i in range(3)], "pp")

    rw = contextlib.ExitStack()
    rows4 = rw.enter_context(nc.sbuf_tensor("rows4", [128, 2048], F32, side="right"))

    S.add("sync", "d_ini0", ("ini", 0), lambda e: e.dma_start(out=identb[:], in_=identb_h.ap()), dma=True)
    S.add("sync", "d_ini1", ("ini", 1), lambda e: e.dma_start(out=identf[:], in_=identf_h.ap()), dma=True)
    S.add("sync", "d_ini2", ("ini", 2), lambda e: e.dma_start(out=onesband[:], in_=onesband_h.ap()), dma=True)

    def vones(e):
        for m in range(NT):
            ap = v_buf[:, m * 520:(m + 1) * 520].rearrange("p (h c) -> p h c", c=65)[:, :, 64:65]
            i = e.memset(ap, 1.0)
        return i
    S.add("gpsimd", "gp", ("ini", "vones"), vones)
    S.add("vector", "ve", ("ini", "amx0"), lambda e: e.memset(cbc[:], -3.0e38))
    S.add("vector", "ve", ("ini", "ones1"), lambda e: e.memset(onesr[:], 1.0))
    S.add("vector", "ve", ("ini", "ones2"), lambda e: e.memset(onesf[:], 1.0))

    ast = contextlib.ExitStack()
    xT = sb(ast, "xT", [128, ND * T], BF16)
    bias_bc = sb(ast, "bias_bc", [128, 3072], F32)
    x_stage = Rot([sb(ast, f"x_stage{i}", [128, D], F32) for i in range(2)], "xst")
    xbf = Rot([sb(ast, f"xbf{i}", [128, D], BF16) for i in range(2)], "xbf")
    wpiece = Rot([sb(ast, f"wpiece{i}", [128, 512], F32) for i in range(2)], "wp")
    wslab = Rot([sb(ast, "wslab0", [128, 16 * 512], BF16)], "ws")
    ctile = Rot([sb(ast, f"ctile{i}", [128, 512], BF16) for i in range(2)], "ct")
    stile = Rot([sb(ast, f"stile{i}", [128, 512], BF16) for i in range(2)], "sn")
    qb_st = sb(ast, "qb_st", [128, 512], F32)
    qc_st = sb(ast, "qc_st", [128, 512], F32)
    qs_st = sb(ast, "qs_st", [128, 512], F32)
    qrope = Rot([sb(ast, f"qrope{i}", [128, 512], BF16) for i in range(2)], "qr")
    wop = Rot([sb(ast, f"wop{i}", [128, 512], F32) for i in range(2)], "wopl")
    wob = Rot([sb(ast, f"wob{i}", [128, 512], BF16) for i in range(2)], "wos")

    psX = ps(ast, "psX", [128, 1024], BF16)
    psQ = Rot([ps(ast, f"psQ{i}", [128, 512], F32) for i in range(2)], "psQ")

    S.add("sync", "d_ba0", ("abl", 0), lambda e: e.dma_start(out=bias_bc[0:1, 0:D], in_=bq_h.ap()), dma=True)
    S.add("sync", "d_ba1", ("abl", 1), lambda e: e.dma_start(out=bias_bc[0:1, D:D + 512], in_=bk_h.ap()), dma=True)
    S.add("sync", "d_ba2", ("abl", 2), lambda e: e.dma_start(out=bias_bc[0:1, D + 512:D + 1024], in_=bv_h.ap()), dma=True)
    for n in range(6):
        qsl0, qps0, qfr0 = psQ.next()
        S.add("tensor", "mm", ("abm", n),
              lambda e, n=n, p_=qps0: e.matmul(
                  p_[:], onesr[:], bias_bc[0:1, n * 512:(n + 1) * 512],
                  start=True, stop=True),
              deps=[("abl", 0), ("abl", 1), ("abl", 2), ("ini", "ones1")] + qfr0)
        S.add("vector", "ve", ("abe", n),
              lambda e, n=n, p_=qps0: e.tensor_copy(
                  bias_bc[:, n * 512:(n + 1) * 512], p_[:]),
              deps=[("abm", n)])
        psQ.used(qsl0, ("abe", n))
    S.add("vector", "ve", ("abb",), lambda e: e.engine_nop(),
          deps=[("abe", n) for n in range(6)])

    def pp_piece(idx):
        half, j = idx % 2, idx // 2
        if j < 64:
            src = wg_h.ap()[j * 128:(j + 1) * 128, half * 1024:(half + 1) * 1024]
        elif j < 128:
            src = wu_h.ap()[(j - 64) * 128:(j - 63) * 128, half * 1024:(half + 1) * 1024]
        else:
            jj = j - 128
            src = wd_h.ap()[(jj // 4) * 128:(jj // 4 + 1) * 128,
                            (jj % 4) * 2048 + half * 1024:(jj % 4) * 2048 + (half + 1) * 1024]
        sl, buf, fr = pp_stg.next()
        S.add("scalar", pp_stg.ch(sl), ("ppl", idx),
              lambda e, b=buf, s=src: e.dma_start(out=b[:], in_=s), deps=[fr], dma=True)
        S.add("vector", "ve", ("ppa", idx),
              lambda e, b=buf, idx=idx: e.tensor_reduce(
                  ppart[:, idx:idx + 1], b[:], AX.X, OP.add, apply_absolute_value=True),
              deps=[("ppl", idx)])
        pp_stg.used(sl, ("ppa", idx))

    for m in range(NT):
        sl, buf, fr = x_stage.next()
        S.add("sync", x_stage.ch(sl), ("axl", m),
              lambda e, b=buf, m=m: e.dma_start(out=b[:], in_=x_h.ap()[m * 128:(m + 1) * 128, :]),
              deps=[fr], dma=True)
        bsl, bb, bfr = xbf.next()
        S.add("scalar", "sc", ("axc", m),
              lambda e, o=bb, i_=buf: e.activation(o[:], i_[:], AF.Copy),
              deps=[("axl", m), bfr])
        x_stage.used(sl, ("axc", m))
        for j in range(4):
            g = m * 4 + j
            pd = ("axe",) + divmod(g - 1, 4) if g >= 1 else None
            def tf(e, b=bb, j=j):
                for c in range(4 * j, 4 * j + 4):
                    i = e.transpose(psX[:, (c % 8) * 128:(c % 8 + 1) * 128],
                                    b[:, c * 128:(c + 1) * 128], identb[:])
                return i
            S.add("tensor", "mm", ("axt", m, j), tf, deps=[("axc", m), pd, ("ini", 0)])
            def ef(e, m=m, j=j):
                dst = xT[:].rearrange("p (f t) -> p f t", f=ND)[:, 4 * j:4 * j + 4,
                                                               m * 128:(m + 1) * 128]
                src = psX[:].rearrange("p (s c) -> p s c", s=8)[:, (j % 2) * 4:(j % 2) * 4 + 4, :]
                return e.tensor_copy(dst, src)
            S.add("vector", "ve", ("axe", m, j), ef, deps=[("axt", m, j)])
        xbf.used(bsl, ("axt", m, 3))
        for _pp in range(8):
            pp_piece(m * 8 + _pp)

    for oc in range(6):
        wsl, wbuf, wfr = wslab.next()
        for kc in range(16):
            psl, pbuf, pfr = wpiece.next()
            src = (wq_h.ap()[:, oc * 512:(oc + 1) * 512] if oc < 4 else
                   (wk_h.ap() if oc == 4 else wv_h.ap()))
            S.add("sync", wpiece.ch(psl), ("awl", oc, kc),
                  lambda e, b=pbuf, s=src, kc=kc: e.dma_start(
                      out=b[:], in_=s[kc * 128:(kc + 1) * 128, :]),
                  deps=[pfr], dma=True)
            S.add("scalar", "sc", ("awc", oc, kc),
                  lambda e, o=wbuf, i_=pbuf, kc=kc: e.activation(
                      o[:, kc * 512:(kc + 1) * 512], i_[:], AF.Copy),
                  deps=[("awl", oc, kc), wfr if kc == 0 else None])
            wpiece.used(psl, ("awc", oc, kc))
        for m in range(NT):
            if oc < 5:
                c0 = oc * 512 if oc < 4 else 0
                csl, cb, cfr = ctile.next()
                ssl, sb2, sfr = stile.next()
                S.add("sync", ctile.ch(csl), ("acl", oc, m),
                      lambda e, b=cb, m=m, c0=c0: e.dma_start(
                          out=b[:], in_=cos_h.ap()[m * 128:(m + 1) * 128, c0:c0 + 512]),
                      deps=[cfr], dma=True)
                S.add("sync", stile.ch(ssl), ("asl", oc, m),
                      lambda e, b=sb2, m=m, c0=c0: e.dma_start(
                          out=b[:], in_=sin_h.ap()[m * 128:(m + 1) * 128, c0:c0 + 512]),
                      deps=[sfr], dma=True)
            qsl, qps, qfr = psQ.next()
            def mf(e, oc=oc, m=m, w=wbuf, p_=qps):
                for kc in range(16):
                    i = e.matmul(p_[:], xT[:, kc * T + m * 128:kc * T + (m + 1) * 128],
                                 w[:, kc * 512:(kc + 1) * 512],
                                 start=(kc == 0), stop=(kc == 15))
                return i
            S.add("tensor", "mm", ("aqm", oc, m), mf,
                  deps=[("awc", oc, 15), ("axe", NT - 1, 3), qfr])
            if m == NT - 1:
                wslab.used(wsl, ("aqm", oc, m))
            if oc < 5:
                S.add("vector", "ve", ("aqb", oc, m),
                      lambda e, p_=qps, oc=oc: e.tensor_tensor(
                          qb_st[:], p_[:], bias_bc[:, oc * 512:(oc + 1) * 512], OP.add),
                      deps=[("aqm", oc, m), ("abb",)])
                psQ.used(qsl, ("aqb", oc, m))
                S.add("vector", "ve", ("aqc", oc, m),
                      lambda e, b=cb: e.tensor_tensor(qc_st[:], qb_st[:], b[:], OP.mult),
                      deps=[("acl", oc, m)])
                ctile.used(csl, ("aqc", oc, m))
                S.add("vector", "ve", ("aqs", oc, m),
                      lambda e, b=sb2: e.tensor_tensor(qs_st[:], qb_st[:], b[:], OP.mult),
                      deps=[("asl", oc, m)])
                stile.used(ssl, ("aqs", oc, m))
                rsl, rb, rfr = qrope.next()
                def rp(e, o=rb):
                    qcr = qc_st[:].rearrange("p (h d) -> p h d", h=8)
                    qsr = qs_st[:].rearrange("p (h d) -> p h d", h=8)
                    dr = o[:].rearrange("p (h d) -> p h d", h=8)
                    e.tensor_tensor(dr[:, :, 0:32], qcr[:, :, 0:64:2], qsr[:, :, 1:64:2], OP.subtract)
                    return e.tensor_tensor(dr[:, :, 32:64], qcr[:, :, 1:64:2], qsr[:, :, 0:64:2], OP.add)
                S.add("vector", "ve", ("arp", oc, m), rp, deps=[("aqs", oc, m)] + [rfr])
                if oc < 4:
                    S.add("gpsimd", qrope.ch(rsl), ("aqsp", oc, m),
                          lambda e, b=rb, m=m, oc=oc: e.dma_start(
                              out=qsp_h.ap()[m * 128:(m + 1) * 128, oc * 512:(oc + 1) * 512],
                              in_=b[:]),
                          deps=[("arp", oc, m)], dma=True)
                    qrope.used(rsl, ("aqsp", oc, m))
                else:
                    def ktf(e, b=rb):
                        for c in range(4):
                            i = e.transpose(psX[:, c * 128:(c + 1) * 128],
                                            b[:, c * 128:(c + 1) * 128], identb[:])
                        return i
                    S.add("tensor", "mm", ("akt", m), ktf,
                          deps=[("arp", oc, m),
                                ("ake", m - 1) if m > 0 else ("axe", NT - 1, 3)])
                    qrope.used(rsl, ("akt", m))
                    def kte(e, m=m):
                        dst = kT[:].rearrange("p (f t) -> p f t", f=4)[:, :, m * 128:(m + 1) * 128]
                        return e.tensor_copy(dst, psX[:].rearrange("p (s c) -> p s c", s=8)[:, 0:4, :])
                    S.add("vector", "ve", ("ake", m), kte, deps=[("akt", m)])
            else:
                def vf(e, p_=qps, m=m):
                    dst = v_buf[:, m * 520:(m + 1) * 520].rearrange("p (h c) -> p h c", c=65)[:, :, 0:64]
                    return e.tensor_tensor(dst, p_[:], bias_bc[:, 2560:3072], OP.add)
                S.add("vector", "ve", ("av", m), vf,
                      deps=[("aqm", oc, m), ("ini", "vones"), ("abb",)])
                psQ.used(qsl, ("av", m))

    for i in range(64):
        r, c = divmod(i, 4)
        wsl2, wb2, wfr2 = wop.next()
        S.add("scalar", wop.ch(wsl2), ("awol", i),
              lambda e, b=wb2, r=r, c=c: e.dma_start(
                  out=b[:], in_=wo_h.ap()[r * 128:(r + 1) * 128, c * 512:(c + 1) * 512]),
              deps=[wfr2], dma=True)
        bsl2, bb2, bfr2 = wob.next()
        S.add("scalar", "sc", ("awoc", i),
              lambda e, o=bb2, i_=wb2: e.activation(o[:], i_[:], AF.Copy),
              deps=[("awol", i), bfr2])
        wop.used(wsl2, ("awoc", i))
        S.add("gpsimd", wob.ch(bsl2), ("awos", i),
              lambda e, b=bb2, r=r, c=c: e.dma_start(
                  out=wobf_h.ap()[r * 128:(r + 1) * 128, c * 512:(c + 1) * 512], in_=b[:]),
              deps=[("awoc", i)], dma=True)
        wob.used(bsl2, ("awos", i))


    if not want("A"):
        pass
    dbg_bufs.update(d_xT=(xT, BF16), d_kT=(kT, BF16), d_vb=(v_buf, BF16))
    if upto == "A":
        _emit_debug(S, nc, debug, dbg_bufs)
        S.emit()
        return nc
    S.fence("A")
    S.emit()
    ast.close()

    bst = contextlib.ExitStack()
    qT = sb(bst, "qT", [128, 16 * 512], BF16)
    pX = [sb(bst, "pA", [128, 16 * 512], BF16), sb(bst, "pB", [128, 16 * 512], BF16)]
    attn_fm = sb(bst, "attn_fm", [128, 16 * 512], BF16)
    bo_bc = sb(bst, "bo_bc", [128, D], F32)
    qrl = Rot([sb(bst, "qrl0", [128, D], BF16)], "qrl")
    av_s = sb(bst, "av_s", [128, 512], F32)
    rs_s = sb(bst, "rs_s", [128, 8], F32)
    atm = sb(bst, "atm", [128, 256], BF16)
    xres = Rot([sb(bst, f"xres{i}", [128, 512], F32) for i in range(2)], "xr")
    z1m = [sb(bst, f"z1m{i}", [128, D], F32) for i in range(4)]
    bnbuf = sb(bst, "bnbuf", [128, 24], F32)
    x2t = Rot([sb(bst, f"x2f{i}", [128, D], F32) for i in range(2)], "x2s")
    tq = sb(bst, "tq", [128, 512], F32)
    r16 = sb(bst, "r16", [128, 512], I16)
    cdf = sb(bst, "cdf", [128, 512], F32)
    s1t = sb(bst, "s1t", [128, 512], F32)
    xqt = Rot([sb(bst, "xqt0", [128, D], BF16)], "xqs")

    psS = [ps(bst, f"psS{i}", [128, 512], F32) for i in range(4)]
    psAV = ps(bst, "psAV", [128, 512], F32)
    psBF = ps(bst, "psBF", [128, 1024], BF16)
    psW = [ps(bst, f"psW{i}", [128, 512], F32) for i in range(2)]

    S.add("sync", "d_bo", ("bbo", 0), lambda e: e.dma_start(out=bo_bc[0:1, :], in_=bo_h.ap()), dma=True)
    for n in range(4):
        S.add("tensor", "mm", ("bbm", n),
              lambda e, n=n: e.matmul(psW[n % 2][:], onesr[:],
                                      bo_bc[0:1, n * 512:(n + 1) * 512],
                                      start=True, stop=True),
              deps=[("bbo", 0), ("bboe", n - 2) if n >= 2 else None])
        S.add("vector", "ve", ("bboe", n),
              lambda e, n=n: e.tensor_copy(bo_bc[:, n * 512:(n + 1) * 512], psW[n % 2][:]),
              deps=[("bbm", n)])
    S.add("vector", "ve", ("bbo", 1), lambda e: e.engine_nop(),
          deps=[("bboe", n) for n in range(4)])

    def pair_heads(p):
        b, i = divmod(p, 4)
        return 8 * b + i, 8 * b + i + 4, b

    last_bf = [None]
    last_av = [None]
    last_psw = {0: ("bboe", 2), 1: ("bboe", 3)}
    last_wo = [None]
    qspill_done = [S.last(f"d_qr{i}") for i in range(2)]

    for g in range(NQB):
        nkt = 4 * g + 4
        for m in range(4):
            sl, buf, fr = qrl.next()
            S.add("sync", qrl.ch(sl), ("bql", g, m),
                  lambda e, b=buf, g=g, m=m: e.dma_start(
                      out=b[:], in_=qsp_h.ap()[g * 512 + m * 128:g * 512 + (m + 1) * 128, :]),
                  deps=[fr] + qspill_done, dma=True)
            for hh in range(2):
                def qtf(e, b=buf, hh=hh):
                    for pi in range(8 * hh, 8 * hh + 8):
                        pa, pb_, kv = pair_heads(pi)
                        e.transpose(psBF[0:64, (pi % 8) * 128:(pi % 8 + 1) * 128],
                                    b[:, pa * 64:(pa + 1) * 64], identb[:])
                        i = e.transpose(psBF[64:128, (pi % 8) * 128:(pi % 8 + 1) * 128],
                                        b[:, pb_ * 64:(pb_ + 1) * 64], identb[:])
                    return i
                S.add("tensor", "mm", ("bqt", g, m, hh), qtf,
                      deps=[("bql", g, m), last_bf[0],
                            ("bsc", g - 1, 15, 1, 4 * g - 1) if g > 0 else None])
                def qte(e, g=g, m=m, hh=hh):
                    dst = qT[:].rearrange("p (q t) -> p q t", q=16)[
                        :, 8 * hh:8 * hh + 8, m * 128:(m + 1) * 128]
                    return e.tensor_copy(dst, psBF[:].rearrange("p (s c) -> p s c", s=8))
                S.add("vector", "ve", ("bqte", g, m, hh), qte, deps=[("bqt", g, m, hh)])
                last_bf[0] = ("bqte", g, m, hh)
            qrl.used(sl, ("bqt", g, m, 1))
        for p in range(16):
            ha, hb, kv = pair_heads(p)
            for kt in range(nkt):
                for X in range(2):
                    poff = 64 * X
                    bank = 2 * (kt % 2) + X
                    if kt >= 2:
                        dep_ps = ("bexp", g, p, X, kt - 2)
                    elif p > 0:
                        dep_ps = ("bexp", g, p - 1, X, nkt - 2 + kt)
                    else:
                        dep_ps = None
                    S.add("tensor", "mm", ("bsc", g, p, X, kt),
                          lambda e, kv=kv, kt=kt, p=p, poff=poff, bank=bank: e.matmul(
                              psS[bank][:],
                              kT[poff:poff + 64, kv * T + kt * 128:kv * T + (kt + 1) * 128],
                              qT[poff:poff + 64, p * 512:(p + 1) * 512],
                              start=True, stop=True),
                          deps=[("bqte", g, 3, 1), dep_ps])
                    if p > 0:
                        pdep = ("bav", g, p - 1, X)
                    elif g > 0:
                        pdep = ("bav", g - 1, 15, X)
                    else:
                        pdep = None
                    wo_dep = last_wo[0] if (p == 0 and kt == 0) else None
                    S.add("scalar", "sc", ("bexp", g, p, X, kt),
                          lambda e, pb=pX[X], kt=kt, bank=bank: e.activation(
                              pb[:, kt * 512:(kt + 1) * 512], psS[bank][:], AF.Exp),
                          deps=[("bsc", g, p, X, kt), pdep, wo_dep])
                    if kt >= 4 * g:
                        S.add("gpsimd", "gp", ("bmk", g, p, X, kt),
                              lambda e, pb=pX[X], kt=kt, g=g: e.affine_select(
                                  pb[:, kt * 512:(kt + 1) * 512],
                                  pb[:, kt * 512:(kt + 1) * 512],
                                  [[1, 512]], OP.is_ge, 0.0,
                                  base=g * 512 - kt * 128, channel_multiplier=-1),
                              deps=[("bexp", g, p, X, kt)])
            for _pp in range(4):
                pp_piece(128 + (g * 16 + p) * 4 + _pp)
            for X in range(2):
                hh = ha if X == 0 else hb
                hv = hh // 4
                def avf(e, hv=hv, pb=pX[X], nkt=nkt):
                    for kt in range(nkt):
                        i = e.matmul(psAV[0:65, :],
                                     v_buf[:, kt * 520 + hv * 65:kt * 520 + hv * 65 + 65],
                                     pb[:, kt * 512:(kt + 1) * 512],
                                     start=(kt == 0), stop=(kt == nkt - 1))
                    return i
                S.add("tensor", "mm", ("bav", g, p, X), avf,
                      deps=[("bexp", g, p, X, nkt - 1), last_av[0]] +
                           [("bmk", g, p, X, kt) for kt in range(4 * g, nkt)])
                S.add("vector", "ve", ("bave", g, p, X),
                      lambda e: e.tensor_copy(av_s[0:65, :], psAV[0:65, :]),
                      deps=[("bav", g, p, X)])
                last_av[0] = ("bave", g, p, X)
                def ttf(e):
                    for c in range(4):
                        i = e.transpose(psW[0][:, c * 65:c * 65 + 65],
                                        av_s[0:65, c * 128:(c + 1) * 128], identf[0:65, 0:65])
                    return i
                S.add("tensor", "mm", ("btt", g, p, X), ttf,
                      deps=[("bave", g, p, X), last_psw[0], ("ini", 1)])
                def nrm(e):
                    for c in range(4):
                        i = e.reciprocal(rs_s[:, c:c + 1], psW[0][:, c * 65 + 64:c * 65 + 65])
                    S.bar(e, i)
                    for c in range(4):
                        i = e.tensor_scalar(atm[:, c * 64:(c + 1) * 64],
                                            psW[0][:, c * 65:c * 65 + 64],
                                            rs_s[:, c:c + 1], None, OP.mult)
                    return i
                S.add("vector", "ve", ("bnr", g, p, X), nrm, deps=[("btt", g, p, X)], nbar=1)
                last_psw[0] = ("bnr", g, p, X)
                po2 = (hh % 2) * 64
                def btf(e, po2=po2):
                    for c in range(4):
                        i = e.transpose(psBF[po2:po2 + 64, c * 128:(c + 1) * 128],
                                        atm[:, c * 64:(c + 1) * 64], identb[:])
                    return i
                S.add("tensor", "mm", ("bbt", g, p, X), btf,
                      deps=[("bnr", g, p, X), last_bf[0]])
                def bte(e, hh=hh, po2=po2):
                    dst = attn_fm[po2:po2 + 64, (hh // 2) * 512:(hh // 2 + 1) * 512]
                    return e.tensor_copy(dst, psBF[po2:po2 + 64, 0:512])
                S.add("vector", "ve", ("bbe", g, p, X), bte,
                      deps=[("bbt", g, p, X), last_wo[0]])
                last_bf[0] = ("bbe", g, p, X)
        for n in range(4):
            wsl3 = n % 2
            wbuf3 = pX[wsl3]
            slab_dep = [("bav", g, 15, wsl3)] if n < 2 else [("bwo", g, n - 2, 3)]
            S.add("sync", f"d_wosl{wsl3}", ("bwl", g, n),
                  lambda e, b=wbuf3, n=n: e.dma_start(
                      out=b[:].rearrange("p (k c) -> p k c", k=16),
                      in_=wobf_h.ap()[:, n * 512:(n + 1) * 512].rearrange("(k p) c -> p k c", p=128)),
                  deps=slab_dep + [S.last("d_wos0"), S.last("d_wos1")], dma=True)
            for m in range(4):
                wps = psW[m % 2]
                att_dep = [("bbe", g, 15, 1)] if (n == 0 and m == 0) else []
                def wof(e, b=wbuf3, m=m, p_=wps):
                    for kc in range(16):
                        i = e.matmul(p_[:], attn_fm[:, kc * 512 + m * 128:kc * 512 + (m + 1) * 128],
                                     b[:, kc * 512:(kc + 1) * 512],
                                     start=(kc == 0), stop=(kc == 15))
                    return i
                S.add("tensor", "mm", ("bwo", g, n, m), wof,
                      deps=[("bwl", g, n), last_psw.get(m % 2)] + att_dep)
                xsl, xb3, xfr3 = xres.next()
                S.add("sync", xres.ch(xsl), ("bxl", g, n, m),
                      lambda e, b=xb3, g=g, n=n, m=m: e.dma_start(
                          out=b[:], in_=x_h.ap()[g * 512 + m * 128:g * 512 + (m + 1) * 128,
                                                 n * 512:(n + 1) * 512]),
                      deps=[xfr3], dma=True)
                def z1f(e, p_=wps, xb=xb3, n=n, m=m):
                    sl_ = z1m[m][:, n * 512:(n + 1) * 512]
                    i = e.tensor_tensor(sl_, p_[:], bo_bc[:, n * 512:(n + 1) * 512], OP.add)
                    S.bar(e, i)
                    return e.tensor_tensor(sl_, sl_, xb[:], OP.add)
                S.add("vector", "ve", ("bz1", g, n, m), z1f,
                      deps=[("bwo", g, n, m), ("bxl", g, n, m), ("bbo", 1)], nbar=1)
                xres.used(xsl, ("bz1", g, n, m))
                last_psw[m % 2] = ("bz1", g, n, m)
            last_wo[0] = ("bwo", g, n, 3)
        for m in range(4):
            gm = g * 4 + m
            prev_brs = ("brs", g, m - 1) if m > 0 else \
                (("brs", g - 1, 3) if g > 0 else None)
            def lnf(e, m=m):
                for c in range(4):
                    i = e.bn_stats(bnbuf[:, c * 6:(c + 1) * 6], z1m[m][:, c * 512:(c + 1) * 512])
                S.bar(e, i)
                i = e.bn_aggr(smv[:, 0:2], bnbuf[:, 0:24])
                S.bar(e, i)
                i = e.tensor_scalar(smv[:, 2:3], smv[:, 1:2], 1.0, EPS_LN, OP.mult, OP.add)
                S.bar(e, i)
                return e.reciprocal(smv[:, 3:4], smv[:, 2:3])
            S.add("vector", "ve", ("bln", g, m), lnf, deps=[("bz1", g, 3, m), prev_brs], nbar=3)
            S.add("scalar", "sc", ("bsq", g, m),
                  lambda e: e.sqrt(smv[:, 4:5], smv[:, 3:4]), deps=[("bln", g, m)])
            xsl2, x2b, x2fr = x2t.next()
            def app(e, m=m, xb=x2b):
                i = e.tensor_tensor(smv[:, 5:6], smv[:, 0:1], smv[:, 4:5], OP.mult)
                S.bar(e, i)
                i = e.tensor_scalar(smv[:, 6:7], smv[:, 5:6], -1.0, None, OP.mult)
                S.bar(e, i)
                return e.tensor_scalar(xb[:], z1m[m][:], smv[:, 4:5], smv[:, 6:7],
                                       OP.mult, OP.add)
            S.add("vector", "ve", ("bx2", g, m), app, deps=[("bsq", g, m)] + [x2fr], nbar=2)
            S.add("gpsimd", x2t.ch(xsl2), ("bx2s", gm),
                  lambda e, xb=x2b, gm=gm: e.dma_start(
                      out=x2sp_h.ap()[gm * 128:(gm + 1) * 128, :], in_=xb[:]),
                  deps=[("bx2", g, m)], dma=True)
            x2t.used(xsl2, ("bx2s", gm))
            def rmsf(e):
                i = e.tensor_tensor(smv[:, 7:8], smv[:, 1:2], smv[:, 3:4], OP.mult)
                S.bar(e, i)
                i = e.tensor_scalar(smv[:, 8:9], smv[:, 7:8], 1.0, EPS_RMS, OP.mult, OP.add)
                S.bar(e, i)
                return e.reciprocal(smv[:, 9:10], smv[:, 8:9])
            S.add("vector", "ve", ("brm", g, m), rmsf, deps=[("bln", g, m)], nbar=2)
            S.add("scalar", "sc", ("brs", g, m),
                  lambda e: e.sqrt(smv[:, 10:11], smv[:, 9:10]), deps=[("brm", g, m)])
            def bamf(e, xb=x2b):
                e.tensor_reduce(smv[:, 11:12], xb[:], AX.X, OP.max)
                i = e.tensor_reduce(smv[:, 12:13], xb[:], AX.X, OP.min)
                S.bar(e, i)
                i = e.tensor_scalar(smv[:, 12:13], smv[:, 12:13], -1.0, None, OP.mult)
                S.bar(e, i)
                return e.tensor_tensor(smv[:, 11:12], smv[:, 11:12], smv[:, 12:13], OP.max)
            S.add("vector", "ve", ("bam", g, m), bamf, deps=[("bx2", g, m)], nbar=2)
            def scf(e, gm=gm):
                i = e.tensor_tensor(smv[:, 13:14], smv[:, 10:11], smv[:, 11:12], OP.mult)
                S.bar(e, i)
                i = e.tensor_scalar(smv[:, 13:14], smv[:, 13:14], 1e-5, None, OP.max)
                S.bar(e, i)
                i = e.reciprocal(smv[:, 14:15], smv[:, 13:14])
                S.bar(e, i)
                i = e.tensor_scalar(sxq[:, gm:gm + 1], smv[:, 14:15], 127.0, None, OP.mult)
                S.bar(e, i)
                return e.tensor_tensor(smv[:, 15:16], smv[:, 10:11], sxq[:, gm:gm + 1], OP.mult)
            S.add("vector", "ve", ("bs2", g, m), scf, deps=[("brs", g, m), ("bam", g, m)], nbar=4)
            qsl2, xqb, xqfr = xqt.next()
            def quant(e, xb=x2b, o=xqb):
                for hf2 in range(4):
                    slx = slice(hf2 * 512, (hf2 + 1) * 512)
                    if hf2:
                        S.bar(e, i)
                    i = e.tensor_scalar(tq[:], xb[:, slx], smv[:, 15:16], None, OP.mult)
                    S.bar(e, i)
                    i = e.tensor_copy(r16[:], tq[:])
                    S.bar(e, i)
                    i = e.tensor_tensor(cdf[:], tq[:], r16[:], OP.subtract)
                    S.bar(e, i)
                    i = e.scalar_tensor_tensor(s1t[:], cdf[:], -0.5, r16[:], OP.is_lt, OP.subtract)
                    S.bar(e, i)
                    i = e.scalar_tensor_tensor(o[:, slx], cdf[:], 0.5, s1t[:], OP.is_gt, OP.subtract)
                return i
            S.add("vector", "ve", ("bqq", g, m), quant, deps=[("bs2", g, m)] + [xqfr], nbar=19)
            x2t.used(xsl2, ("bqq", g, m))
            S.add("gpsimd", xqt.ch(qsl2), ("bxqs", gm),
                  lambda e, o=xqb, gm=gm: e.dma_start(
                      out=xqsp_h.ap()[gm * 128:(gm + 1) * 128, :], in_=o[:]),
                  deps=[("bqq", g, m)], dma=True)
            xqt.used(qsl2, ("bxqs", gm))

    dbg_bufs.update(d_qT=(qT, BF16), d_pA=(pX[0], BF16), d_attnfm=(attn_fm, BF16),
                    d_sxq=(sxq, F32))
    if upto == "B":
        _emit_debug(S, nc, debug, dbg_bufs)
        S.emit()
        return nc
    S.fence("B")
    S.emit()
    bst.close()
    ab.close()

    fst = contextlib.ExitStack()
    xqT = sb(fst, "xqT", [128, ND * T], BF16)
    xqrl = Rot([sb(fst, f"xqrl{i}", [128, D], BF16) for i in range(2)], "xqr")
    wgst = Rot([sb(fst, "wgst0", [128, D], F32)], "wg")
    wust = Rot([sb(fst, "wust0", [128, D], F32)], "wu")
    pq_b = sb(fst, "pq_b", [128, D], BF16)
    qq_b = sb(fst, "qq_b", [128, D], BF16)
    terng2 = [sb(fst, f"terng{i}", [128, D], BF16) for i in range(2)]
    ternu2 = [sb(fst, f"ternu{i}", [128, D], BF16) for i in range(2)]
    wgT = Rot([sb(fst, "wgT0", [128, ND * 128], BF16)], "wgT")
    wuT = Rot([sb(fst, "wuT0", [128, ND * 128], BF16)], "wuT")
    gsil = Rot([sb(fst, "gsil0", [128, 512], F32)], "gs")
    gtr = sb(fst, "gtr", [128, 512], F32)
    h2c = Rot([sb(fst, f"h2c{i}", [128, 512], F32) for i in range(2)], "h2")
    sqc = Rot([sb(fst, f"sqc{i}", [128, 512], BF16) for i in range(2)], "sq")
    wdst = Rot([sb(fst, "wdst0", [128, D], F32)], "wd")
    ternd2 = [sb(fst, f"ternd{i}", [128, D], BF16) for i in range(2)]
    wdtst = Rot([sb(fst, f"wdtst{i}", [128, D], BF16) for i in range(2)], "wdt")
    stp = sb(fst, "stp", [128, 1024], BF16)
    gbc = sb(fst, "gbc", [128, 2048], F32)
    amn = sb(fst, "amn", [128, 2048], F32)
    ftm = sb(fst, "ftm", [128, 64], F32)

    psG = Rot([ps(fst, f"psG{i}", [128, 512], F32) for i in range(2)], "psG")
    psU = Rot([ps(fst, f"psU{i}", [128, 512], F32) for i in range(2)], "psU")
    psTT = ps(fst, "psTT", [128, 1024], BF16)
    psTD = ps(fst, "psTD", [128, 1024], BF16)
    psST = [ps(fst, f"psST{i}", [128, 512], F32) for i in range(2)]

    def f0a(e):
        e.tensor_reduce(wsc[:, 0:1], ppart[:, 0:128], AX.X, OP.add)
        e.tensor_reduce(wsc[:, 1:2], ppart[:, 128:256], AX.X, OP.add)
        return e.tensor_reduce(wsc[:, 2:3], ppart[:, 256:384], AX.X, OP.add)
    S.add("vector", "ve", ("f0a",), f0a, deps=[("ppa", 383)])
    S.add("tensor", "mm", ("f0b1",),
          lambda e: e.matmul(psST[0][0:1, 0:3], onesf[:], wsc[:, 0:3],
                             start=True, stop=True),
          deps=[("f0a",), ("ini", "ones2")])
    S.add("vector", "ve", ("f0b2",),
          lambda e: e.tensor_copy(wsc[0:1, 10:13], psST[0][0:1, 0:3]),
          deps=[("f0b1",)])
    S.add("tensor", "mm", ("f0b3",),
          lambda e: e.matmul(psST[0][:, 0:3], onesr[:], wsc[0:1, 10:13],
                             start=True, stop=True),
          deps=[("f0b2",)])
    S.add("vector", "ve", ("f0b",),
          lambda e: e.tensor_copy(wsc[:, 0:3], psST[0][:, 0:3]),
          deps=[("f0b3",)])
    def f0c(e):
        i = e.tensor_scalar(wsc[:, 6:9], wsc[:, 0:3], 1.0 / NW, 1e-5, OP.mult, OP.max)
        S.bar(e, i)
        e.reciprocal(wsc[:, 0:1], wsc[:, 6:7])
        e.reciprocal(wsc[:, 1:2], wsc[:, 7:8])
        i = e.reciprocal(wsc[:, 2:3], wsc[:, 8:9])
        S.bar(e, i)
        e.tensor_scalar(wsc[:, 3:6], wsc[:, 0:3], -1.0, None, OP.mult)
        return e.tensor_tensor(wsc[:, 9:10], wsc[:, 7:8], wsc[:, 0:1], OP.mult)
    S.add("vector", "ve", ("f0c",), f0c, deps=[("f0b",)], nbar=2)
    S.add("gpsimd", "d_f0r", ("f0r",),
          lambda e: e.dma_start(out=rowb_h.ap()[:, :], in_=sxq[:, 0:16]),
          deps=[("bs2", NQB - 1, 3)], dma=True)
    S.add("sync", "d_f0l", ("f0l",),
          lambda e: e.dma_start(
              out=rows4[96:97, :].rearrange("p (m q) -> p m q", q=128),
              in_=bass.AP(rowb_h, 0, [[0, 1], [1, 16], [16, 128]]),
              allow_slow_non_contiguous=True),
          deps=[("f0r",)], dma=True)
    def f0d(e):
        i = e.reciprocal(rows4[32:33, :], rows4[96:97, :])
        S.bar(e, i)
        return e.tensor_scalar(rows4[0:1, :], rows4[32:33, :], wsc[32:33, 6:7], None, OP.mult)
    S.add("vector", "ve", ("f0d",), f0d, deps=[("f0l",), ("f0c",)], nbar=1)
    for n in range(4):
        S.add("tensor", "mm", ("f0g", n),
              lambda e, n=n: e.matmul(psST[1][:], onesr[:],
                                      rows4[0:1, n * 512:(n + 1) * 512],
                                      start=True, stop=True),
              deps=[("f0d",), ("f0ge", n - 1) if n > 0 else None])
        S.add("vector", "ve", ("f0ge", n),
              lambda e, n=n: e.tensor_copy(gbc[:, n * 512:(n + 1) * 512], psST[1][:]),
              deps=[("f0g", n)])
    S.add("vector", "ve", ("f0g",), lambda e: e.engine_nop(),
          deps=[("f0ge", n) for n in range(4)])

    last_tt = [None]
    xq_sp_done = [S.last("d_xqs0"), S.last("d_xqs1")]
    for m in range(NT):
        sl, buf, fr = xqrl.next()
        S.add("sync", xqrl.ch(sl), ("fxl", m),
              lambda e, b=buf, m=m: e.dma_start(
                  out=b[:], in_=xqsp_h.ap()[m * 128:(m + 1) * 128, :]),
              deps=[fr] + xq_sp_done, dma=True)
        for j in range(2):
            def xtf(e, b=buf, j=j):
                for c in range(8 * j, 8 * j + 8):
                    i = e.transpose(psTT[:, (c % 8) * 128:(c % 8 + 1) * 128],
                                    b[:, c * 128:(c + 1) * 128], identb[:])
                return i
            S.add("tensor", "mm", ("fxt", m, j), xtf, deps=[("fxl", m), last_tt[0]])
            def xte(e, m=m, j=j):
                dst = xqT[:].rearrange("p (f t) -> p f t", f=ND)[:, 8 * j:8 * j + 8,
                                                                m * 128:(m + 1) * 128]
                return e.tensor_copy(dst, psTT[:].rearrange("p (s c) -> p s c", s=8))
            S.add("vector", "ve", ("fxe", m, j), xte, deps=[("fxt", m, j)])
            last_tt[0] = ("fxe", m, j)
        xqrl.used(sl, ("fxt", m, 1))

    S.add("vector", "ve", ("f1amn",), lambda e: e.memset(amn[:], 3.0e38))
    last_st = {0: ("f0b",), 1: ("f0ge", 3)}
    last_fde = [None]
    for ht in range(NFH):
        gsl, gst, gfr = wgst.next()
        usl, ust, ufr = wust.next()
        S.add("sync", wgst.ch(gsl), ("fgl", ht),
              lambda e, b=gst, ht=ht: e.dma_start(
                  out=b[:], in_=wg_h.ap()[ht * 128:(ht + 1) * 128, :]), deps=[gfr], dma=True)
        S.add("sync", wust.ch(usl), ("ful", ht),
              lambda e, b=ust, ht=ht: e.dma_start(
                  out=b[:], in_=wu_h.ap()[ht * 128:(ht + 1) * 128, :]), deps=[ufr], dma=True)
        terng = terng2[ht % 2]
        ternu = ternu2[ht % 2]
        def qg(e, w=gst, terng=terng):
            e.tensor_scalar(pq_b[:], w[:], wsc[:, 0:1], 0.5, OP.mult, OP.is_ge)
            i = e.tensor_scalar(qq_b[:], w[:], wsc[:, 3:4], 0.5, OP.mult, OP.is_ge)
            S.bar(e, i)
            return e.tensor_tensor(terng[:], pq_b[:], qq_b[:], OP.subtract)
        S.add("vector", "ve", ("fgq", ht), qg,
              deps=[("fgl", ht), ("f0c",),
                    ("ftt", "g", ht - 2, 1) if ht >= 2 else None], nbar=1)
        wgst.used(gsl, ("fgq", ht))
        def qu(e, w=ust, ternu=ternu):
            e.tensor_scalar(pq_b[:], w[:], wsc[:, 1:2], 0.5, OP.mult, OP.is_ge)
            i = e.tensor_scalar(qq_b[:], w[:], wsc[:, 4:5], 0.5, OP.mult, OP.is_ge)
            S.bar(e, i)
            return e.tensor_tensor(ternu[:], pq_b[:], qq_b[:], OP.subtract)
        S.add("vector", "ve", ("fuq", ht), qu,
              deps=[("ful", ht), ("f0c",),
                    ("ftt", "u", ht - 2, 1) if ht >= 2 else None], nbar=1)
        wust.used(usl, ("fuq", ht))
        gtsl, gT, gTfr = wgT.next()
        utsl, uT, uTfr = wuT.next()
        for nm, tern, dstT, dfr, qkey in (("g", terng, gT, gTfr, ("fgq", ht)),
                                          ("u", ternu, uT, uTfr, ("fuq", ht))):
            for j in range(2):
                def ttf2(e, tern=tern, j=j):
                    for c in range(8 * j, 8 * j + 8):
                        i = e.transpose(psTT[:, (c % 8) * 128:(c % 8 + 1) * 128],
                                        tern[:, c * 128:(c + 1) * 128], identb[:])
                    return i
                S.add("tensor", "mm", ("ftt", nm, ht, j), ttf2,
                      deps=[qkey, last_tt[0]] + (dfr if j == 0 else []))
                def tte(e, dstT=dstT, j=j):
                    dst = dstT[:].rearrange("p (f c) -> p f c", f=ND)[:, 8 * j:8 * j + 8, :]
                    return e.tensor_copy(dst, psTT[:].rearrange("p (s c) -> p s c", s=8))
                S.add("vector", "ve", ("fte", nm, ht, j), tte, deps=[("ftt", nm, ht, j)])
                last_tt[0] = ("fte", nm, ht, j)
        for n in range(4):
            gpsl, gps, gpfr = psG.next()
            def gmm(e, gT=gT, n=n, p_=gps):
                for kc in range(16):
                    i = e.matmul(p_[:], gT[:, kc * 128:(kc + 1) * 128],
                                 xqT[:, kc * T + n * 512:kc * T + n * 512 + 512],
                                 start=(kc == 0), stop=(kc == 15))
                return i
            S.add("tensor", "mm", ("fgm", ht, n), gmm,
                  deps=[("fte", "g", ht, 1), ("fxe", NT - 1, 1)] + [gpfr])
            gssl, gsb3, gsfr = gsil.next()
            S.add("vector", "ve", ("fgs", ht, n),
                  lambda e, p_=gps, n=n: e.tensor_tensor(
                      gtr[:], p_[:], gbc[:, n * 512:(n + 1) * 512], OP.mult),
                  deps=[("fgm", ht, n), ("f0g",)])
            psG.used(gpsl, ("fgs", ht, n))
            S.add("scalar", "sc", ("fsi", ht, n),
                  lambda e, o=gsb3: e.activation(o[:], gtr[:], AF.Silu),
                  deps=[("fgs", ht, n)] + [gsfr])
            S.add("vector", "ve", ("fgu", ht, n),
                  lambda e, o=gsb3, n=n: e.scalar_tensor_tensor(
                      o[:], o[:], wsc[:, 9:10], gbc[:, n * 512:(n + 1) * 512],
                      OP.mult, OP.mult),
                  deps=[("fsi", ht, n)])
            upsl, ups, upfr = psU.next()
            def umm(e, uT=uT, n=n, p_=ups):
                for kc in range(16):
                    i = e.matmul(p_[:], uT[:, kc * 128:(kc + 1) * 128],
                                 xqT[:, kc * T + n * 512:kc * T + n * 512 + 512],
                                 start=(kc == 0), stop=(kc == 15))
                return i
            S.add("tensor", "mm", ("fum", ht, n), umm,
                  deps=[("fte", "u", ht, 1), ("fxe", NT - 1, 1)] + [upfr])
            h2sl, h2b, h2fr = h2c.next()
            S.add("vector", "ve", ("fh2", ht, n),
                  lambda e, p_=ups, o=h2b, g_=gsb3: e.tensor_tensor(o[:], p_[:], g_[:], OP.mult),
                  deps=[("fum", ht, n), ("fgu", ht, n)] + [h2fr])
            psU.used(upsl, ("fh2", ht, n))
            gsil.used(gssl, ("fh2", ht, n))
            S.add("vector", "ve", ("famP", ht, n),
                  lambda e, o=h2b, n=n: e.tensor_tensor(
                      cbc[:, n * 512:(n + 1) * 512], cbc[:, n * 512:(n + 1) * 512],
                      o[:], OP.max),
                  deps=[("fh2", ht, n), ("ini", "amx0")])
            S.add("vector", "ve", ("famN", ht, n),
                  lambda e, o=h2b, n=n: e.tensor_tensor(
                      amn[:, n * 512:(n + 1) * 512], amn[:, n * 512:(n + 1) * 512],
                      o[:], OP.min),
                  deps=[("fh2", ht, n), ("f1amn",)])
            sqsl, sqb3, sqfr = sqc.next()
            S.add("scalar", "sc", ("fsq", ht, n),
                  lambda e, i_=h2b, o=sqb3: e.square(o[:], i_[:]),
                  deps=[("fh2", ht, n)] + [sqfr])
            bank = n // 2
            S.add("tensor", "mm", ("fsm", ht, n),
                  lambda e, ht=ht, n=n, q_=sqb3, bank=bank: e.matmul(
                      psST[bank][(n % 2) * 64:(n % 2) * 64 + 64, :],
                      onesband[:, 63 - ht:127 - ht], q_[:],
                      start=(ht == 0), stop=(ht == NFH - 1), skip_group_check=True),
                  deps=[("fsq", ht, n), last_st.get(bank), ("ini", 2)])
            last_st[bank] = ("fsm", ht, n)
            sqc.used(sqsl, ("fsm", ht, n))
            S.add("gpsimd", h2c.ch(h2sl), ("fh2s", ht, n),
                  lambda e, o=h2b, ht=ht, n=n: e.dma_start(
                      out=h2sp_h.ap()[ht * 128:(ht + 1) * 128, n * 512:(n + 1) * 512],
                      in_=o[:]),
                  deps=[("fh2", ht, n)], dma=True)
            h2c.used(h2sl, ("fh2s", ht, n))
            h2c.used(h2sl, ("famP", ht, n))
            h2c.used(h2sl, ("famN", ht, n))
        wgT.used(gtsl, ("fgm", ht, 3))
        wuT.used(utsl, ("fum", ht, 3))
        dt, pc = divmod(ht, 4)
        dsl, dstg, dfr2 = wdst.next()
        S.add("sync", wdst.ch(dsl), ("fdl", ht),
              lambda e, b=dstg, dt=dt, pc=pc: e.dma_start(
                  out=b[:], in_=wd_h.ap()[dt * 128:(dt + 1) * 128, pc * 2048:(pc + 1) * 2048]),
              deps=[dfr2], dma=True)
        tdb = ternd2[ht % 2]
        def qd(e, w=dstg, o=tdb):
            e.tensor_scalar(pq_b[:], w[:], wsc[:, 2:3], 0.5, OP.mult, OP.is_ge)
            i = e.tensor_scalar(qq_b[:], w[:], wsc[:, 5:6], 0.5, OP.mult, OP.is_ge)
            S.bar(e, i)
            return e.tensor_tensor(o[:], pq_b[:], qq_b[:], OP.subtract)
        S.add("vector", "ve", ("fdq", ht), qd,
              deps=[("fdl", ht), ("f0c",),
                    ("fdt", ht - 2, 1) if ht >= 2 else None], nbar=1)
        wdst.used(dsl, ("fdq", ht))
        wtsl, wtb, wtfr = wdtst.next()
        for j in range(2):
            def dtf(e, o=tdb, j=j):
                for c in range(8 * j, 8 * j + 8):
                    i = e.transpose(psTD[:, (c % 8) * 128:(c % 8 + 1) * 128],
                                    o[:, c * 128:(c + 1) * 128], identb[:])
                return i
            S.add("tensor", "mm", ("fdt", ht, j), dtf,
                  deps=[("fdq", ht), last_fde[0]] + (wtfr if j == 0 else []))
            def dte(e, wtb=wtb, j=j):
                dst = wtb[:].rearrange("p (c q) -> p c q", c=ND)[:, 8 * j:8 * j + 8, :]
                return e.tensor_copy(dst, psTD[:].rearrange("p (s c) -> p s c", s=8))
            S.add("vector", "ve", ("fde", ht, j), dte, deps=[("fdt", ht, j)])
            last_fde[0] = ("fde", ht, j)

        def dts(e, wtb=wtb, dt=dt, pc=pc):
            dst = bass.AP(wdtsp_h, (pc * 2048) * D + dt * 128,
                          [[D, 128], [128 * D, 16], [1, 128]])
            return e.dma_start(out=dst, in_=wtb[:].rearrange("p (c q) -> p c q", c=16))
        S.add("gpsimd", wdtst.ch(wtsl), ("fds", ht), dts, deps=[("fde", ht, 1)], dma=True)
        wdtst.used(wtsl, ("fds", ht))

    S.add("vector", "ve", ("fsv",),
          lambda e: (e.tensor_copy(stp[:, 0:512], psST[0][:]),
                     e.tensor_copy(stp[:, 512:1024], psST[1][:]))[-1],
          deps=[last_st[0], last_st[1]])
    for n in range(4):
        po = (n % 2) * 64
        for c in range(4):
            idx = n * 4 + c
            S.add("tensor", "mm", ("fstt", idx),
                  lambda e, po=po, n=n, c=c, idx=idx: e.transpose(
                      psTT[:, (idx % 8) * 128:(idx % 8) * 128 + 64],
                      stp[po:po + 64, (n // 2) * 512 + c * 128:(n // 2) * 512 + (c + 1) * 128],
                      identb[po:po + 64, po:po + 64]),
                  deps=[("fsv",), ("fstr", idx - 1) if idx >= 1 else None,
                        ("fte", "u", NFH - 1, 1)])
            S.add("vector", "ve", ("fstr", idx),
                  lambda e, idx=idx: e.tensor_reduce(
                      ftm[:, idx:idx + 1], psTT[:, (idx % 8) * 128:(idx % 8) * 128 + 64],
                      AX.X, OP.add),
                  deps=[("fstt", idx)])
    fam_last = [("famP", NFH - 1, 3), ("famN", NFH - 1, 3)]
    for c in range(16):
        S.add("tensor", "mm", ("famt", c),
              lambda e, c=c: e.transpose(psST[0][:, (c % 4) * 128:(c % 4 + 1) * 128],
                                         cbc[:, c * 128:(c + 1) * 128], identf[:]),
              deps=[("fsv",), ("famr", c - 1) if c >= 1 else None] + fam_last)
        S.add("vector", "ve", ("famr", c),
              lambda e, c=c: e.tensor_reduce(
                  ftm[:, 16 + c:17 + c], psST[0][:, (c % 4) * 128:(c % 4 + 1) * 128],
                  AX.X, OP.max),
              deps=[("famt", c)])
        S.add("tensor", "mm", ("fant", c),
              lambda e, c=c: e.transpose(psST[1][:, (c % 4) * 128:(c % 4 + 1) * 128],
                                         amn[:, c * 128:(c + 1) * 128], identf[:]),
              deps=[("fsv",), ("fanr", c - 1) if c >= 1 else None] + fam_last)
        S.add("vector", "ve", ("fanr", c),
              lambda e, c=c: e.tensor_reduce(
                  ftm[:, 32 + c:33 + c], psST[1][:, (c % 4) * 128:(c % 4 + 1) * 128],
                  AX.X, OP.min),
              deps=[("fant", c)])
    def ftok1(e):
        i = e.tensor_scalar(ftm[:, 32:48], ftm[:, 32:48], -1.0, None, OP.mult)
        S.bar(e, i)
        e.tensor_tensor(ftm[:, 16:32], ftm[:, 16:32], ftm[:, 32:48], OP.max)
        i = e.tensor_scalar(ftm[:, 0:16], ftm[:, 0:16], 1.0 / FF, EPS_RMS, OP.mult, OP.add)
        S.bar(e, i)
        return e.reciprocal(ftm[:, 32:48], ftm[:, 0:16])
    S.add("vector", "ve", ("ftok1",), ftok1, nbar=2,
          deps=[("fstr", i) for i in range(16)] + [("famr", c) for c in range(16)]
               + [("fanr", c) for c in range(16)])
    S.add("scalar", "sc", ("ftok2",),
          lambda e: e.sqrt(ftm[:, 32:48], ftm[:, 32:48]), deps=[("ftok1",)])
    def ftok3(e):
        i = e.tensor_tensor(ftm[:, 48:64], ftm[:, 32:48], ftm[:, 16:32], OP.mult)
        S.bar(e, i)
        i = e.tensor_scalar(ftm[:, 48:64], ftm[:, 48:64], 1e-5, None, OP.max)
        S.bar(e, i)
        i = e.reciprocal(ftm[:, 48:64], ftm[:, 48:64])
        S.bar(e, i)
        i = e.tensor_scalar(ftm[:, 48:64], ftm[:, 48:64], 127.0, None, OP.mult)
        S.bar(e, i)
        e.tensor_tensor(ftm[:, 0:16], ftm[:, 32:48], ftm[:, 48:64], OP.mult)
        i = e.reciprocal(c3col[:, 0:16], ftm[:, 48:64])
        S.bar(e, i)
        return e.tensor_scalar(c3col[:, 0:16], c3col[:, 0:16], wsc[:, 8:9], None, OP.mult)
    S.add("vector", "ve", ("ftok3",), ftok3, deps=[("ftok2",), ("f0c",)], nbar=5)
    S.add("gpsimd", "d_f1r", ("f1r",),
          lambda e: e.dma_start(out=rowb_h.ap(), in_=ftm[:, 0:16]),
          deps=[("ftok3",)], dma=True)
    S.add("sync", "d_f1l", ("f1l",),
          lambda e: e.dma_start(
              out=rows4[0:1, :].rearrange("p (m q) -> p m q", q=128),
              in_=bass.AP(rowb_h, 0, [[0, 1], [1, 16], [16, 128]]),
              allow_slow_non_contiguous=True),
          deps=[("f1r",)], dma=True)
    for n in range(4):
        S.add("tensor", "mm", ("fcbm", n),
              lambda e, n=n: e.matmul(psST[0][:], onesr[:],
                                      rows4[0:1, n * 512:(n + 1) * 512],
                                      start=True, stop=True),
              deps=[("f1l",), ("famt", 15),
                    ("fcbe", n - 1) if n > 0 else ("famr", 15)])
        S.add("vector", "ve", ("fcbe", n),
              lambda e, n=n: e.tensor_copy(cbc[:, n * 512:(n + 1) * 512], psST[0][:]),
              deps=[("fcbm", n)])
    S.add("vector", "ve", ("fcb",), lambda e: e.engine_nop(),
          deps=[("fcbe", n) for n in range(4)])

    dbg_bufs.update(d_xqT=(xqT, BF16), d_wsc=(wsc, F32), d_cbc=(cbc, F32),
                    d_gbc=(gbc, F32), d_c3col=(c3col, F32), d_rows=(rows4, F32), d_ftm=(ftm, F32))
    if upto == "F1":
        _emit_debug(S, nc, debug, dbg_bufs)
        S.emit()
        return nc
    S.fence("F1")
    S.emit()
    fst.close()
    rw.close()

    gstk = contextlib.ExitStack()
    xq2res = sb(gstk, "xq2res", [128, NFH * 1024], I8)
    h2rl = Rot([sb(gstk, f"h2rl{i}", [128, 1024], F32) for i in range(2)], "h2r")
    tq2 = Rot([sb(gstk, "tq20", [128, 1024], F32)], "tq2")
    r162 = sb(gstk, "r162", [128, 1024], I16)
    cdf2 = sb(gstk, "cdf2", [128, 1024], F32)
    s1t2 = sb(gstk, "s1t2", [128, 1024], F32)
    wdts = Rot([sb(gstk, "wdts0", [128, NFH * 256], BF16)], "wdl")
    xcol = Rot([sb(gstk, f"xcol{i}", [128, NFH * 128], BF16) for i in range(2)], "xc")
    x2c = Rot([sb(gstk, f"x2c{i}", [128, 256], F32) for i in range(2)], "x2c")
    z2h = sb(gstk, "z2h", [128, 8 * D], BF16)
    bnb2 = sb(gstk, "bnb2", [128, 24], F32)
    outst = Rot([sb(gstk, "outst0", [128, D], F32)], "os")
    psD = Rot([ps(gstk, f"psD{i}", [128, 512], F32) for i in range(2)], "psD")

    h2_sp_done = [S.last("d_h20"), S.last("d_h21")]
    wdt_sp_done = [S.last("d_wdt0"), S.last("d_wdt1")]
    x2_sp_done = [S.last("d_x2s0"), S.last("d_x2s1")]
    NB = 8
    last_g3c = [None]
    for hf in range(2):
        for ht in range(NFH):
            sl, buf, fr = h2rl.next()
            S.add("sync", h2rl.ch(sl), ("g2l", hf, ht),
                  lambda e, b=buf, hf=hf, ht=ht: e.dma_start(
                      out=b[:], in_=h2sp_h.ap()[ht * 128:(ht + 1) * 128,
                                                hf * 1024:(hf + 1) * 1024]),
                  deps=[fr] + h2_sp_done, dma=True)
            tsl, tb, tfr = tq2.next()
            S.add("gpsimd", "gp", ("g2m", hf, ht),
                  lambda e, b=buf, o=tb, hf=hf: e.tensor_tensor(
                      o[:], b[:], cbc[:, hf * 1024:(hf + 1) * 1024], OP.mult),
                  deps=[("g2l", hf, ht), ("fcb",), last_g3c[0] if ht == 0 else None] + tfr)
            h2rl.used(sl, ("g2m", hf, ht))
            def q2(e, t=tb, hf=hf, ht=ht):
                i = e.tensor_copy(r162[:], t[:])
                S.bar(e, i)
                i = e.tensor_tensor(cdf2[:], t[:], r162[:], OP.subtract)
                S.bar(e, i)
                i = e.scalar_tensor_tensor(s1t2[:], cdf2[:], -0.5, r162[:], OP.is_lt, OP.subtract)
                S.bar(e, i)
                return e.scalar_tensor_tensor(xq2res[:, ht * 1024:(ht + 1) * 1024],
                                              cdf2[:], 0.5, s1t2[:], OP.is_gt, OP.subtract)
            S.add("vector", "ve", ("g2q", hf, ht), q2, deps=[("g2m", hf, ht)], nbar=3)
            tq2.used(tsl, ("g2q", hf, ht))
        for n in range(NB):
            wsl4, wb4, wfr4 = wdts.next()
            S.add("sync", wdts.ch(wsl4), ("g3w", hf, n),
                  lambda e, b=wb4, n=n: e.dma_start(
                      out=b[:].rearrange("p (k c) -> p k c", k=NFH),
                      in_=wdtsp_h.ap()[:, n * 256:(n + 1) * 256].rearrange(
                          "(k p) c -> p k c", p=128)),
                  deps=[wfr4] + wdt_sp_done, dma=True)
            for m in range(8):
                gm = hf * 8 + m
                csl, cb4, cfr4 = xcol.next()
                S.add("vector", "ve", ("g3c", hf, n, m),
                      lambda e, o=cb4, m=m: e.tensor_copy(
                          o[:].rearrange("p (k c) -> p k c", k=NFH),
                          xq2res[:].rearrange("p (k c) -> p k c", k=NFH)[
                              :, :, m * 128:(m + 1) * 128]),
                      deps=[("g2q", hf, NFH - 1)] + cfr4)
                last_g3c[0] = ("g3c", hf, n, m)
                dsl4, dps4, dfr4 = psD.next()
                def dmm(e, w=wb4, c_=cb4, p_=dps4):
                    for hk in range(NFH):
                        i = e.matmul(p_[0:128, 0:256], c_[:, hk * 128:(hk + 1) * 128],
                                     w[:, hk * 256:(hk + 1) * 256],
                                     start=(hk == 0), stop=(hk == NFH - 1))
                    return i
                S.add("tensor", "mm", ("g3m", hf, n, m), dmm,
                      deps=[("g3w", hf, n), ("g3c", hf, n, m)] + dfr4)
                xcol.used(csl, ("g3m", hf, n, m))
                xsl4, xb4, xfr4 = x2c.next()
                S.add("sync", x2c.ch(xsl4), ("g3x", hf, n, m),
                      lambda e, b=xb4, gm=gm, n=n: e.dma_start(
                          out=b[:], in_=x2sp_h.ap()[gm * 128:(gm + 1) * 128,
                                                    n * 256:(n + 1) * 256]),
                      deps=[xfr4] + x2_sp_done, dma=True)
                S.add("vector", "ve", ("g3z", hf, n, m),
                      lambda e, p_=dps4, xb=xb4, m=m, n=n, gm=gm: e.scalar_tensor_tensor(
                          z2h[:, m * D + n * 256:m * D + n * 256 + 256],
                          p_[0:128, 0:256], c3col[:, gm:gm + 1], xb[:], OP.mult, OP.add),
                      deps=[("g3m", hf, n, m), ("g3x", hf, n, m), ("ftok3",)])
                x2c.used(xsl4, ("g3z", hf, n, m))
                psD.used(dsl4, ("g3z", hf, n, m))
            wdts.used(wsl4, ("g3m", hf, n, 7))
        for m in range(8):
            gm = hf * 8 + m
            def ln2(e, m=m):
                for c in range(4):
                    i = e.bn_stats(bnb2[:, c * 6:(c + 1) * 6],
                                   z2h[:, m * D + c * 512:m * D + c * 512 + 512])
                S.bar(e, i)
                i = e.bn_aggr(smv[:, 0:2], bnb2[:, 0:24])
                S.bar(e, i)
                i = e.tensor_scalar(smv[:, 2:3], smv[:, 1:2], 1.0, EPS_LN, OP.mult, OP.add)
                S.bar(e, i)
                return e.reciprocal(smv[:, 3:4], smv[:, 2:3])
            prev_q = ("g4q", hf, m - 1) if m > 0 else \
                (("g4q", hf - 1, 7) if hf > 0 else None)
            S.add("vector", "ve", ("g4s", hf, m), ln2,
                  deps=[("g3z", hf, n_, m) for n_ in range(NB)] + [prev_q], nbar=3)
            S.add("scalar", "sc", ("g4q", hf, m),
                  lambda e: e.sqrt(smv[:, 4:5], smv[:, 3:4]), deps=[("g4s", hf, m)])
            osl, ob, ofr = outst.next()
            def app2(e, m=m, ob=ob):
                i = e.tensor_tensor(smv[:, 5:6], smv[:, 0:1], smv[:, 4:5], OP.mult)
                S.bar(e, i)
                i = e.tensor_scalar(smv[:, 6:7], smv[:, 5:6], -1.0, None, OP.mult)
                S.bar(e, i)
                return e.tensor_scalar(ob[:], z2h[:, m * D:(m + 1) * D],
                                       smv[:, 4:5], smv[:, 6:7], OP.mult, OP.add)
            S.add("vector", "ve", ("g4a", hf, m), app2, deps=[("g4q", hf, m)] + ofr, nbar=2)
            S.add("gpsimd", outst.ch(osl), ("g4o", gm),
                  lambda e, ob=ob, gm=gm: e.dma_start(
                      out=out_h.ap()[gm * 128:(gm + 1) * 128, :], in_=ob[:]),
                  deps=[("g4a", hf, m)], dma=True)
            outst.used(osl, ("g4o", gm))

    S.add("gpsimd", "gp", ("fin",), lambda e: e.engine_nop(),
          deps=[("g4o", NT - 1)])
    dbg_bufs.update(d_xq2=(xq2res, I8))
    _emit_debug(S, nc, debug, dbg_bufs)
    S.emit()
    gstk.close()
    top.close()
    return nc


def _emit_debug(S, nc, debug, dbg_bufs):
    if not debug:
        return
    names = [n for n in debug if n in dbg_bufs]
    alldeps = list(S.lastkey.values())
    for name in names:
        buf, dt = dbg_bufs[name]
        hh = nc.dram_tensor(name, list(buf.shape), dt, kind="ExternalOutput")
        S.add("gpsimd", f"d_dbg_{name}", ("dbg", name),
              lambda e, b=buf, hh=hh: e.dma_start(out=hh.ap(), in_=b[:]),
              deps=alldeps, dma=True)
    S.add("gpsimd", "gp", ("dbgw",), lambda e: e.engine_nop(),
          deps=[("dbg", n) for n in names])


def make_inputs(L0_attn_x, L0_attn_Wq, L0_attn_bq, L0_attn_Wk, L0_attn_bk,
                L0_attn_Wv, L0_attn_bv, L0_attn_Wo, L0_attn_bo,
                L0_ffn_w_gate, L0_ffn_w_up, L0_ffn_w_down):
    consts = host_constants()
    f32 = np.float32
    base = {
        "Wq": np.ascontiguousarray(L0_attn_Wq, dtype=f32),
        "bq": np.ascontiguousarray(L0_attn_bq, dtype=f32).reshape(1, D),
        "Wk": np.ascontiguousarray(L0_attn_Wk, dtype=f32),
        "bk": np.ascontiguousarray(L0_attn_bk, dtype=f32).reshape(1, 512),
        "Wv": np.ascontiguousarray(L0_attn_Wv, dtype=f32),
        "bv": np.ascontiguousarray(L0_attn_bv, dtype=f32).reshape(1, 512),
        "Wo": np.ascontiguousarray(L0_attn_Wo, dtype=f32),
        "bo": np.ascontiguousarray(L0_attn_bo, dtype=f32).reshape(1, D),
        "w_gate": np.ascontiguousarray(L0_ffn_w_gate, dtype=f32),
        "w_up": np.ascontiguousarray(L0_ffn_w_up, dtype=f32),
        "w_down": np.ascontiguousarray(L0_ffn_w_down, dtype=f32),
        "cosT": consts["cosT"], "sinT": consts["sinT"],
        "ident_bf": consts["ident_bf"], "ident_f32": consts["ident_f32"],
        "onesband": consts["onesband"],
    }
    x = np.ascontiguousarray(L0_attn_x, dtype=f32)
    return [dict(base, x=x[c]) for c in range(8)]


def kernel(L0_attn_x, L0_attn_Wq, L0_attn_bq, L0_attn_Wk, L0_attn_bk,
           L0_attn_Wv, L0_attn_bv, L0_attn_Wo, L0_attn_bo,
           L0_ffn_w_gate, L0_ffn_w_up, L0_ffn_w_down):
    nc = bass.Bass("TRN2", target_bir_lowering=False)
    nc = build(nc, debug=(), upto=BUILD_UPTO)
    in_maps = make_inputs(L0_attn_x, L0_attn_Wq, L0_attn_bq, L0_attn_Wk, L0_attn_bk,
                          L0_attn_Wv, L0_attn_bv, L0_attn_Wo, L0_attn_bo,
                          L0_ffn_w_gate, L0_ffn_w_up, L0_ffn_w_down)
    res = run_bass_kernel_spmd(nc, in_maps, core_ids=list(range(8)))
    return np.stack([np.asarray(r["out"]) for r in res.results], axis=0)


# version 06
# speedup vs baseline: 1.2227x; 1.0453x over previous
import contextlib
import numpy as np
import ml_dtypes
import concourse.bass as bass
import concourse.mybir as mybir
from concourse.bass_utils import run_bass_kernel_spmd
from concourse.alu_op_type import AluOpType as OP

F32 = mybir.dt.float32
BF16 = mybir.dt.bfloat16
I16 = mybir.dt.int16
I8 = mybir.dt.int8
AF = mybir.ActivationFunctionType
AX = mybir.AxisListType

T, D, NH, KVH, HD, FF = 2048, 2048, 32, 8, 64, 8192
NT, ND, QB, NQB, NFH = T // 128, D // 128, 512, 4, FF // 128
EPS_LN, EPS_RMS = 1e-5, 1e-6
NW = FF * D

BUILD_UPTO = "ALL"


def host_constants():
    inv_freq = 1.0 / (10000.0 ** (np.arange(0, HD, 2, dtype=np.float64) / HD))
    fr = np.outer(np.arange(T, dtype=np.float64), inv_freq)
    s = 8.0 ** -0.5
    cosf = np.repeat(np.cos(fr), 2, axis=1) * s
    sinf = np.repeat(np.sin(fr), 2, axis=1) * s
    bf = ml_dtypes.bfloat16
    return {
        "cosT": np.ascontiguousarray(np.tile(cosf, (1, NH))).astype(bf),
        "sinT": np.ascontiguousarray(np.tile(sinf, (1, NH))).astype(bf),
        "ident_bf": np.eye(128, dtype=np.float32).astype(bf),
        "ident_f32": np.eye(128, dtype=np.float32),
        "onesband": ((np.arange(127)[None, :] == 63) * np.ones((128, 1))).astype(np.float32).astype(bf),
    }


class Step:
    __slots__ = ("eng", "ch", "inc", "key", "deps", "fn")

    def __init__(self, eng, ch, inc, key, deps, fn):
        self.eng, self.ch, self.inc, self.key, self.deps, self.fn = \
            eng, ch, inc, key, deps, fn


class Sched:
    def __init__(self, nc, stack):
        self.nc, self.stack = nc, stack
        self.sems, self.table, self.ch_eng, self.cnt = {}, {}, {}, {}
        self.lastkey = {}
        self.steps = []
        self.ecnt = {}
        self._cur_ch = None

    def sem(self, ch):
        if ch not in self.sems:
            self.sems[ch] = self.stack.enter_context(self.nc.semaphore(ch))
        return self.sems[ch]

    def add(self, eng, ch, key, fn, deps=(), dma=False, nbar=0):
        if ch in self.ch_eng:
            assert self.ch_eng[ch] == eng, f"{ch}: {self.ch_eng[ch]} vs {eng}"
        else:
            self.ch_eng[ch] = eng
        inc = 16 if dma else 1
        self.cnt[ch] = self.cnt.get(ch, 0) + inc + nbar
        assert key not in self.table, f"dup {key}"
        self.table[key] = (ch, self.cnt[ch], eng)
        self.lastkey[ch] = key
        flat = []
        for d in deps:
            if d is None:
                continue
            if isinstance(d, list):
                flat += [x for x in d if x is not None]
            else:
                flat.append(d)
        self.steps.append(Step(eng, ch, inc, key, flat, fn))

    def bar(self, e, inst):
        ch = self._cur_ch
        self.ecnt[ch] = self.ecnt.get(ch, 0) + 1
        inst.then_inc(self.sem(ch), 1)
        e.wait_ge(self.sem(ch), self.ecnt[ch])

    def last(self, ch):
        return self.lastkey.get(ch)

    def fence(self, tag):
        deps = [k for c, k in self.lastkey.items() if c.startswith("d_")]
        self.add("vector", "ve", ("fence", tag), lambda e: e.engine_nop(), deps=deps)

    def emit(self):
        per = {}
        for s in self.steps:
            per.setdefault(s.eng, []).append(s)
        table = self.table

        ENGCH = {"vector": "ve", "scalar": "sc", "gpsimd": "gp"}

        def mk(engsteps):
            def body(eng):
                waited = {}
                for s in engsteps:
                    own = ENGCH.get(s.eng)
                    ownv = self.ecnt.get(own, 0) if own is not None else 0
                    if own is not None and ownv > waited.get(own, -1):
                        eng.wait_ge(self.sem(own), ownv)
                        waited[own] = ownv
                    for d in s.deps:
                        ch, v, peng = table[d]
                        if waited.get(ch, -1) >= v:
                            continue
                        eng.wait_ge(self.sem(ch), v)
                        waited[ch] = v
                    self._cur_ch = s.ch
                    inst = s.fn(eng)
                    inst.then_inc(self.sem(s.ch), s.inc)
                    self.ecnt[s.ch] = self.ecnt.get(s.ch, 0) + s.inc
                    assert self.ecnt[s.ch] == table[s.key][1], \
                        f"step {s.key}: emitted {self.ecnt[s.ch]} != planned {table[s.key][1]} (nbar mismatch)"
            return body

        with self.nc.Block() as blk:
            for engname, engsteps in per.items():
                getattr(blk, engname)(mk(engsteps))
        self.steps = []


class Rot:
    def __init__(self, bufs, name):
        self.bufs, self.name, self.n = bufs, name, len(bufs)
        self.i = -1
        self.cons = [[] for _ in range(self.n)]

    def next(self):
        self.i += 1
        s = self.i % self.n
        deps = self.cons[s]
        self.cons[s] = []
        return s, self.bufs[s], deps

    def ch(self, s):
        return f"d_{self.name}{s}"

    def used(self, s, key):
        self.cons[s].append(key)


def build(nc, debug=(), upto="ALL"):
    debug = set(debug)
    order = ["A", "B", "F1", "ALL"]
    lvl = order.index(upto)

    def want(p):
        return order.index(p) <= lvl

    def dram(name, shape, dtype, kind=None):
        return nc.dram_tensor(name, shape, dtype) if kind is None else \
            nc.dram_tensor(name, shape, dtype, kind=kind)

    x_h = dram("x", [T, D], F32, kind="ExternalInput")
    wq_h = dram("Wq", [D, D], F32, kind="ExternalInput")
    bq_h = dram("bq", [1, D], F32, kind="ExternalInput")
    wk_h = dram("Wk", [D, 512], F32, kind="ExternalInput")
    bk_h = dram("bk", [1, 512], F32, kind="ExternalInput")
    wv_h = dram("Wv", [D, 512], F32, kind="ExternalInput")
    bv_h = dram("bv", [1, 512], F32, kind="ExternalInput")
    wo_h = dram("Wo", [D, D], F32, kind="ExternalInput")
    bo_h = dram("bo", [1, D], F32, kind="ExternalInput")
    wg_h = dram("w_gate", [FF, D], F32, kind="ExternalInput")
    wu_h = dram("w_up", [FF, D], F32, kind="ExternalInput")
    wd_h = dram("w_down", [D, FF], F32, kind="ExternalInput")
    cos_h = dram("cosT", [T, D], BF16, kind="ExternalInput")
    sin_h = dram("sinT", [T, D], BF16, kind="ExternalInput")
    identb_h = dram("ident_bf", [128, 128], BF16, kind="ExternalInput")
    identf_h = dram("ident_f32", [128, 128], F32, kind="ExternalInput")
    onesband_h = dram("onesband", [128, 127], BF16, kind="ExternalInput")
    out_h = dram("out", [T, D], F32, kind="ExternalOutput")

    qsp_h = dram("q_spill", [T, D], BF16)
    wobf_h = dram("wo_bf", [D, D], BF16)
    x2sp_h = dram("x2_spill", [T, D], F32)
    xqsp_h = dram("xq_spill", [T, D], BF16)
    h2sp_h = dram("h2_spill", [FF, T], F32)
    wdtsp_h = dram("wdt_spill", [FF, D], BF16)
    rowb_h = dram("rowbounce", [128, 16], F32)
    rowf_h = dram("rowflat", [1, 2048], F32)

    top = contextlib.ExitStack()
    S = Sched(nc, top)
    dbg_bufs = {}

    def sb(stk, n, shp, dt):
        return stk.enter_context(nc.sbuf_tensor(n, shp, dt))

    def ps(stk, n, shp, dt):
        return stk.enter_context(nc.psum_tensor(n, shp, dt))

    identb = sb(top, "identb", [128, 128], BF16)
    identf = sb(top, "identf", [128, 128], F32)
    onesband = sb(top, "onesb", [128, 127], BF16)
    ppart = sb(top, "ppart", [128, 384], F32)
    wsc = sb(top, "wsc", [128, 16], F32)
    sxq = sb(top, "sxq", [128, NT], F32)
    c3col = sb(top, "c3col", [128, NT], F32)
    cbc = sb(top, "cbc", [128, 2048], F32)
    smv = sb(top, "smv", [128, 16], F32)
    onesr = sb(top, "onesr", [1, 128], F32)
    onesf = sb(top, "onesf", [128, 1], F32)

    ab = contextlib.ExitStack()
    kT = sb(ab, "kT", [128, 4 * T], BF16)
    v_buf = sb(ab, "v_buf", [128, NT * 520], BF16)
    pp_stg = Rot([sb(ab, f"pp_stg{i}", [128, 1024], F32) for i in range(3)], "pp")

    rw = contextlib.ExitStack()
    rows4 = rw.enter_context(nc.sbuf_tensor("rows4", [128, 2048], F32, side="right"))

    S.add("sync", "d_ini0", ("ini", 0), lambda e: e.dma_start(out=identb[:], in_=identb_h.ap()), dma=True)
    S.add("sync", "d_ini1", ("ini", 1), lambda e: e.dma_start(out=identf[:], in_=identf_h.ap()), dma=True)
    S.add("sync", "d_ini2", ("ini", 2), lambda e: e.dma_start(out=onesband[:], in_=onesband_h.ap()), dma=True)

    def vones(e):
        for m in range(NT):
            ap = v_buf[:, m * 520:(m + 1) * 520].rearrange("p (h c) -> p h c", c=65)[:, :, 64:65]
            i = e.memset(ap, 1.0)
        return i
    S.add("gpsimd", "gp", ("ini", "vones"), vones)
    S.add("vector", "ve", ("ini", "amx0"), lambda e: e.memset(cbc[:], -3.0e38))
    S.add("vector", "ve", ("ini", "ones1"), lambda e: e.memset(onesr[:], 1.0))
    S.add("vector", "ve", ("ini", "ones2"), lambda e: e.memset(onesf[:], 1.0))

    ast = contextlib.ExitStack()
    xT = sb(ast, "xT", [128, ND * T], BF16)
    bias_bc = sb(ast, "bias_bc", [128, 3072], F32)
    x_stage = Rot([sb(ast, f"x_stage{i}", [128, D], F32) for i in range(2)], "xst")
    xbf = Rot([sb(ast, f"xbf{i}", [128, D], BF16) for i in range(2)], "xbf")
    wpiece = Rot([sb(ast, f"wpiece{i}", [128, 512], F32) for i in range(2)], "wp")
    wslab = Rot([sb(ast, "wslab0", [128, 16 * 512], BF16)], "ws")
    ctile = Rot([sb(ast, f"ctile{i}", [128, 512], BF16) for i in range(2)], "ct")
    stile = Rot([sb(ast, f"stile{i}", [128, 512], BF16) for i in range(2)], "sn")
    qb_st = sb(ast, "qb_st", [128, 512], F32)
    qc_st = sb(ast, "qc_st", [128, 512], F32)
    qs_st = sb(ast, "qs_st", [128, 512], F32)
    qrope = Rot([sb(ast, f"qrope{i}", [128, 512], BF16) for i in range(2)], "qr")
    wop = Rot([sb(ast, f"wop{i}", [128, 512], F32) for i in range(2)], "wopl")
    wob = Rot([sb(ast, f"wob{i}", [128, 512], BF16) for i in range(2)], "wos")

    psX = ps(ast, "psX", [128, 1024], BF16)
    psQ = Rot([ps(ast, f"psQ{i}", [128, 512], F32) for i in range(2)], "psQ")

    S.add("sync", "d_ba0", ("abl", 0), lambda e: e.dma_start(out=bias_bc[0:1, 0:D], in_=bq_h.ap()), dma=True)
    S.add("sync", "d_ba1", ("abl", 1), lambda e: e.dma_start(out=bias_bc[0:1, D:D + 512], in_=bk_h.ap()), dma=True)
    S.add("sync", "d_ba2", ("abl", 2), lambda e: e.dma_start(out=bias_bc[0:1, D + 512:D + 1024], in_=bv_h.ap()), dma=True)
    for n in range(6):
        qsl0, qps0, qfr0 = psQ.next()
        S.add("tensor", "mm", ("abm", n),
              lambda e, n=n, p_=qps0: e.matmul(
                  p_[:], onesr[:], bias_bc[0:1, n * 512:(n + 1) * 512],
                  start=True, stop=True),
              deps=[("abl", 0), ("abl", 1), ("abl", 2), ("ini", "ones1")] + qfr0)
        S.add("vector", "ve", ("abe", n),
              lambda e, n=n, p_=qps0: e.tensor_copy(
                  bias_bc[:, n * 512:(n + 1) * 512], p_[:]),
              deps=[("abm", n)])
        psQ.used(qsl0, ("abe", n))
    S.add("vector", "ve", ("abb",), lambda e: e.engine_nop(),
          deps=[("abe", n) for n in range(6)])

    def pp_piece(idx):
        half, j = idx % 2, idx // 2
        if j < 64:
            src = wg_h.ap()[j * 128:(j + 1) * 128, half * 1024:(half + 1) * 1024]
        elif j < 128:
            src = wu_h.ap()[(j - 64) * 128:(j - 63) * 128, half * 1024:(half + 1) * 1024]
        else:
            jj = j - 128
            src = wd_h.ap()[(jj // 4) * 128:(jj // 4 + 1) * 128,
                            (jj % 4) * 2048 + half * 1024:(jj % 4) * 2048 + (half + 1) * 1024]
        sl, buf, fr = pp_stg.next()
        S.add("scalar" if sl != 1 else "sync", pp_stg.ch(sl), ("ppl", idx),
              lambda e, b=buf, s=src: e.dma_start(out=b[:], in_=s), deps=[fr], dma=True)
        S.add("vector", "ve", ("ppa", idx),
              lambda e, b=buf, idx=idx: e.tensor_reduce(
                  ppart[:, idx:idx + 1], b[:], AX.X, OP.add, apply_absolute_value=True),
              deps=[("ppl", idx)])
        pp_stg.used(sl, ("ppa", idx))

    for m in range(NT):
        sl, buf, fr = x_stage.next()
        S.add("sync", x_stage.ch(sl), ("axl", m),
              lambda e, b=buf, m=m: e.dma_start(out=b[:], in_=x_h.ap()[m * 128:(m + 1) * 128, :]),
              deps=[fr], dma=True)
        bsl, bb, bfr = xbf.next()
        S.add("scalar", "sc", ("axc", m),
              lambda e, o=bb, i_=buf: e.activation(o[:], i_[:], AF.Copy),
              deps=[("axl", m), bfr])
        x_stage.used(sl, ("axc", m))
        for j in range(4):
            g = m * 4 + j
            pd = ("axe",) + divmod(g - 1, 4) if g >= 1 else None
            def tf(e, b=bb, j=j):
                for c in range(4 * j, 4 * j + 4):
                    i = e.transpose(psX[:, (c % 8) * 128:(c % 8 + 1) * 128],
                                    b[:, c * 128:(c + 1) * 128], identb[:])
                return i
            S.add("tensor", "mm", ("axt", m, j), tf, deps=[("axc", m), pd, ("ini", 0)])
            def ef(e, m=m, j=j):
                dst = xT[:].rearrange("p (f t) -> p f t", f=ND)[:, 4 * j:4 * j + 4,
                                                               m * 128:(m + 1) * 128]
                src = psX[:].rearrange("p (s c) -> p s c", s=8)[:, (j % 2) * 4:(j % 2) * 4 + 4, :]
                return e.tensor_copy(dst, src)
            S.add("vector", "ve", ("axe", m, j), ef, deps=[("axt", m, j)])
        xbf.used(bsl, ("axt", m, 3))
        for _pp in range(8):
            pp_piece(m * 8 + _pp)

    for oc in range(6):
        wsl, wbuf, wfr = wslab.next()
        for kc in range(16):
            psl, pbuf, pfr = wpiece.next()
            src = (wq_h.ap()[:, oc * 512:(oc + 1) * 512] if oc < 4 else
                   (wk_h.ap() if oc == 4 else wv_h.ap()))
            S.add("sync", wpiece.ch(psl), ("awl", oc, kc),
                  lambda e, b=pbuf, s=src, kc=kc: e.dma_start(
                      out=b[:], in_=s[kc * 128:(kc + 1) * 128, :]),
                  deps=[pfr], dma=True)
            S.add("scalar", "sc", ("awc", oc, kc),
                  lambda e, o=wbuf, i_=pbuf, kc=kc: e.activation(
                      o[:, kc * 512:(kc + 1) * 512], i_[:], AF.Copy),
                  deps=[("awl", oc, kc), wfr if kc == 0 else None])
            wpiece.used(psl, ("awc", oc, kc))
        for m in range(NT):
            if oc < 5:
                c0 = oc * 512 if oc < 4 else 0
                csl, cb, cfr = ctile.next()
                ssl, sb2, sfr = stile.next()
                S.add("sync", ctile.ch(csl), ("acl", oc, m),
                      lambda e, b=cb, m=m, c0=c0: e.dma_start(
                          out=b[:], in_=cos_h.ap()[m * 128:(m + 1) * 128, c0:c0 + 512]),
                      deps=[cfr], dma=True)
                S.add("sync", stile.ch(ssl), ("asl", oc, m),
                      lambda e, b=sb2, m=m, c0=c0: e.dma_start(
                          out=b[:], in_=sin_h.ap()[m * 128:(m + 1) * 128, c0:c0 + 512]),
                      deps=[sfr], dma=True)
            qsl, qps, qfr = psQ.next()
            def mf(e, oc=oc, m=m, w=wbuf, p_=qps):
                for kc in range(16):
                    i = e.matmul(p_[:], xT[:, kc * T + m * 128:kc * T + (m + 1) * 128],
                                 w[:, kc * 512:(kc + 1) * 512],
                                 start=(kc == 0), stop=(kc == 15))
                return i
            S.add("tensor", "mm", ("aqm", oc, m), mf,
                  deps=[("awc", oc, 15), ("axe", NT - 1, 3), qfr])
            if m == NT - 1:
                wslab.used(wsl, ("aqm", oc, m))
            if oc < 5:
                S.add("vector", "ve", ("aqb", oc, m),
                      lambda e, p_=qps, oc=oc: e.tensor_tensor(
                          qb_st[:], p_[:], bias_bc[:, oc * 512:(oc + 1) * 512], OP.add),
                      deps=[("aqm", oc, m), ("abb",)])
                psQ.used(qsl, ("aqb", oc, m))
                S.add("vector", "ve", ("aqc", oc, m),
                      lambda e, b=cb: e.tensor_tensor(qc_st[:], qb_st[:], b[:], OP.mult),
                      deps=[("acl", oc, m)])
                ctile.used(csl, ("aqc", oc, m))
                S.add("vector", "ve", ("aqs", oc, m),
                      lambda e, b=sb2: e.tensor_tensor(qs_st[:], qb_st[:], b[:], OP.mult),
                      deps=[("asl", oc, m)])
                stile.used(ssl, ("aqs", oc, m))
                rsl, rb, rfr = qrope.next()
                def rp(e, o=rb):
                    qcr = qc_st[:].rearrange("p (h d) -> p h d", h=8)
                    qsr = qs_st[:].rearrange("p (h d) -> p h d", h=8)
                    dr = o[:].rearrange("p (h d) -> p h d", h=8)
                    e.tensor_tensor(dr[:, :, 0:32], qcr[:, :, 0:64:2], qsr[:, :, 1:64:2], OP.subtract)
                    return e.tensor_tensor(dr[:, :, 32:64], qcr[:, :, 1:64:2], qsr[:, :, 0:64:2], OP.add)
                S.add("vector", "ve", ("arp", oc, m), rp, deps=[("aqs", oc, m)] + [rfr])
                if oc < 4:
                    S.add("gpsimd", qrope.ch(rsl), ("aqsp", oc, m),
                          lambda e, b=rb, m=m, oc=oc: e.dma_start(
                              out=qsp_h.ap()[m * 128:(m + 1) * 128, oc * 512:(oc + 1) * 512],
                              in_=b[:]),
                          deps=[("arp", oc, m)], dma=True)
                    qrope.used(rsl, ("aqsp", oc, m))
                else:
                    def ktf(e, b=rb):
                        for c in range(4):
                            i = e.transpose(psX[:, c * 128:(c + 1) * 128],
                                            b[:, c * 128:(c + 1) * 128], identb[:])
                        return i
                    S.add("tensor", "mm", ("akt", m), ktf,
                          deps=[("arp", oc, m),
                                ("ake", m - 1) if m > 0 else ("axe", NT - 1, 3)])
                    qrope.used(rsl, ("akt", m))
                    def kte(e, m=m):
                        dst = kT[:].rearrange("p (f t) -> p f t", f=4)[:, :, m * 128:(m + 1) * 128]
                        return e.tensor_copy(dst, psX[:].rearrange("p (s c) -> p s c", s=8)[:, 0:4, :])
                    S.add("vector", "ve", ("ake", m), kte, deps=[("akt", m)])
            else:
                def vf(e, p_=qps, m=m):
                    dst = v_buf[:, m * 520:(m + 1) * 520].rearrange("p (h c) -> p h c", c=65)[:, :, 0:64]
                    return e.tensor_tensor(dst, p_[:], bias_bc[:, 2560:3072], OP.add)
                S.add("vector", "ve", ("av", m), vf,
                      deps=[("aqm", oc, m), ("ini", "vones"), ("abb",)])
                psQ.used(qsl, ("av", m))

    for i in range(64):
        r, c = divmod(i, 4)
        wsl2, wb2, wfr2 = wop.next()
        S.add("scalar", wop.ch(wsl2), ("awol", i),
              lambda e, b=wb2, r=r, c=c: e.dma_start(
                  out=b[:], in_=wo_h.ap()[r * 128:(r + 1) * 128, c * 512:(c + 1) * 512]),
              deps=[wfr2], dma=True)
        bsl2, bb2, bfr2 = wob.next()
        S.add("scalar", "sc", ("awoc", i),
              lambda e, o=bb2, i_=wb2: e.activation(o[:], i_[:], AF.Copy),
              deps=[("awol", i), bfr2])
        wop.used(wsl2, ("awoc", i))
        S.add("gpsimd", wob.ch(bsl2), ("awos", i),
              lambda e, b=bb2, r=r, c=c: e.dma_start(
                  out=wobf_h.ap()[r * 128:(r + 1) * 128, c * 512:(c + 1) * 512], in_=b[:]),
              deps=[("awoc", i)], dma=True)
        wob.used(bsl2, ("awos", i))


    if not want("A"):
        pass
    dbg_bufs.update(d_xT=(xT, BF16), d_kT=(kT, BF16), d_vb=(v_buf, BF16))
    if upto == "A":
        _emit_debug(S, nc, debug, dbg_bufs)
        S.emit()
        return nc
    S.fence("A")
    S.emit()
    ast.close()

    bst = contextlib.ExitStack()
    qT = sb(bst, "qT", [128, 16 * 512], BF16)
    pX = [sb(bst, "pA", [128, 16 * 512], BF16), sb(bst, "pB", [128, 16 * 512], BF16)]
    attn_fm = sb(bst, "attn_fm", [128, 16 * 512], BF16)
    bo_bc = sb(bst, "bo_bc", [128, D], F32)
    qrl = Rot([sb(bst, "qrl0", [128, D], BF16)], "qrl")
    av_s = sb(bst, "av_s", [128, 512], F32)
    rs_s = sb(bst, "rs_s", [128, 8], F32)
    atm = sb(bst, "atm", [128, 256], BF16)
    xres = Rot([sb(bst, f"xres{i}", [128, 512], F32) for i in range(2)], "xr")
    z1m = [sb(bst, f"z1m{i}", [128, D], F32) for i in range(4)]
    bnbuf = sb(bst, "bnbuf", [128, 24], F32)
    x2t = Rot([sb(bst, f"x2f{i}", [128, D], F32) for i in range(2)], "x2s")
    tq = sb(bst, "tq", [128, 512], F32)
    r16 = sb(bst, "r16", [128, 512], I16)
    cdf = sb(bst, "cdf", [128, 512], F32)
    s1t = sb(bst, "s1t", [128, 512], F32)
    xqt = Rot([sb(bst, "xqt0", [128, D], BF16)], "xqs")

    psS = [ps(bst, f"psS{i}", [128, 512], F32) for i in range(4)]
    psAV = ps(bst, "psAV", [128, 512], F32)
    psBF = ps(bst, "psBF", [128, 1024], BF16)
    psW = [ps(bst, f"psW{i}", [128, 512], F32) for i in range(2)]

    S.add("sync", "d_bo", ("bbo", 0), lambda e: e.dma_start(out=bo_bc[0:1, :], in_=bo_h.ap()), dma=True)
    for n in range(4):
        S.add("tensor", "mm", ("bbm", n),
              lambda e, n=n: e.matmul(psW[n % 2][:], onesr[:],
                                      bo_bc[0:1, n * 512:(n + 1) * 512],
                                      start=True, stop=True),
              deps=[("bbo", 0), ("bboe", n - 2) if n >= 2 else None])
        S.add("vector", "ve", ("bboe", n),
              lambda e, n=n: e.tensor_copy(bo_bc[:, n * 512:(n + 1) * 512], psW[n % 2][:]),
              deps=[("bbm", n)])
    S.add("vector", "ve", ("bbo", 1), lambda e: e.engine_nop(),
          deps=[("bboe", n) for n in range(4)])

    def pair_heads(p):
        b, i = divmod(p, 4)
        return 8 * b + i, 8 * b + i + 4, b

    last_bf = [None]
    last_av = [None]
    last_psw = {0: ("bboe", 2), 1: ("bboe", 3)}
    last_wo = [None]
    qspill_done = [S.last(f"d_qr{i}") for i in range(2)]

    for g in range(NQB):
        nkt = 4 * g + 4
        for m in range(4):
            sl, buf, fr = qrl.next()
            S.add("sync", qrl.ch(sl), ("bql", g, m),
                  lambda e, b=buf, g=g, m=m: e.dma_start(
                      out=b[:], in_=qsp_h.ap()[g * 512 + m * 128:g * 512 + (m + 1) * 128, :]),
                  deps=[fr] + qspill_done, dma=True)
            for hh in range(2):
                def qtf(e, b=buf, hh=hh):
                    for pi in range(8 * hh, 8 * hh + 8):
                        pa, pb_, kv = pair_heads(pi)
                        e.transpose(psBF[0:64, (pi % 8) * 128:(pi % 8 + 1) * 128],
                                    b[:, pa * 64:(pa + 1) * 64], identb[:])
                        i = e.transpose(psBF[64:128, (pi % 8) * 128:(pi % 8 + 1) * 128],
                                        b[:, pb_ * 64:(pb_ + 1) * 64], identb[:])
                    return i
                S.add("tensor", "mm", ("bqt", g, m, hh), qtf,
                      deps=[("bql", g, m), last_bf[0],
                            ("bsc", g - 1, 15, 1, 4 * g - 1) if g > 0 else None])
                def qte(e, g=g, m=m, hh=hh):
                    dst = qT[:].rearrange("p (q t) -> p q t", q=16)[
                        :, 8 * hh:8 * hh + 8, m * 128:(m + 1) * 128]
                    return e.tensor_copy(dst, psBF[:].rearrange("p (s c) -> p s c", s=8))
                S.add("vector", "ve", ("bqte", g, m, hh), qte, deps=[("bqt", g, m, hh)])
                last_bf[0] = ("bqte", g, m, hh)
            qrl.used(sl, ("bqt", g, m, 1))
        for p in range(16):
            ha, hb, kv = pair_heads(p)
            for kt in range(nkt):
                for X in range(2):
                    poff = 64 * X
                    bank = 2 * (kt % 2) + X
                    if kt >= 2:
                        dep_ps = ("bexp", g, p, X, kt - 2)
                    elif p > 0:
                        dep_ps = ("bexp", g, p - 1, X, nkt - 2 + kt)
                    else:
                        dep_ps = None
                    S.add("tensor", "mm", ("bsc", g, p, X, kt),
                          lambda e, kv=kv, kt=kt, p=p, poff=poff, bank=bank: e.matmul(
                              psS[bank][:],
                              kT[poff:poff + 64, kv * T + kt * 128:kv * T + (kt + 1) * 128],
                              qT[poff:poff + 64, p * 512:(p + 1) * 512],
                              start=True, stop=True),
                          deps=[("bqte", g, 3, 1), dep_ps])
                    if p > 0:
                        pdep = ("bav", g, p - 1, X)
                    elif g > 0:
                        pdep = ("bav", g - 1, 15, X)
                    else:
                        pdep = None
                    wo_dep = last_wo[0] if (p == 0 and kt == 0) else None
                    S.add("scalar", "sc", ("bexp", g, p, X, kt),
                          lambda e, pb=pX[X], kt=kt, bank=bank: e.activation(
                              pb[:, kt * 512:(kt + 1) * 512], psS[bank][:], AF.Exp),
                          deps=[("bsc", g, p, X, kt), pdep, wo_dep])
                    if kt >= 4 * g:
                        S.add("gpsimd", "gp", ("bmk", g, p, X, kt),
                              lambda e, pb=pX[X], kt=kt, g=g: e.affine_select(
                                  pb[:, kt * 512:(kt + 1) * 512],
                                  pb[:, kt * 512:(kt + 1) * 512],
                                  [[1, 512]], OP.is_ge, 0.0,
                                  base=g * 512 - kt * 128, channel_multiplier=-1),
                              deps=[("bexp", g, p, X, kt)])
            for _pp in range(4):
                pp_piece(128 + (g * 16 + p) * 4 + _pp)
            for X in range(2):
                hh = ha if X == 0 else hb
                hv = hh // 4
                def avf(e, hv=hv, pb=pX[X], nkt=nkt):
                    for kt in range(nkt):
                        i = e.matmul(psAV[0:65, :],
                                     v_buf[:, kt * 520 + hv * 65:kt * 520 + hv * 65 + 65],
                                     pb[:, kt * 512:(kt + 1) * 512],
                                     start=(kt == 0), stop=(kt == nkt - 1))
                    return i
                S.add("tensor", "mm", ("bav", g, p, X), avf,
                      deps=[("bexp", g, p, X, nkt - 1), last_av[0]] +
                           [("bmk", g, p, X, kt) for kt in range(4 * g, nkt)])
                S.add("vector", "ve", ("bave", g, p, X),
                      lambda e: e.tensor_copy(av_s[0:65, :], psAV[0:65, :]),
                      deps=[("bav", g, p, X)])
                last_av[0] = ("bave", g, p, X)
                def ttf(e):
                    for c in range(4):
                        i = e.transpose(psW[0][:, c * 65:c * 65 + 65],
                                        av_s[0:65, c * 128:(c + 1) * 128], identf[0:65, 0:65])
                    return i
                S.add("tensor", "mm", ("btt", g, p, X), ttf,
                      deps=[("bave", g, p, X), last_psw[0], ("ini", 1)])
                def nrm(e):
                    for c in range(4):
                        i = e.reciprocal(rs_s[:, c:c + 1], psW[0][:, c * 65 + 64:c * 65 + 65])
                    S.bar(e, i)
                    for c in range(4):
                        i = e.tensor_scalar(atm[:, c * 64:(c + 1) * 64],
                                            psW[0][:, c * 65:c * 65 + 64],
                                            rs_s[:, c:c + 1], None, OP.mult)
                    return i
                S.add("vector", "ve", ("bnr", g, p, X), nrm, deps=[("btt", g, p, X)], nbar=1)
                last_psw[0] = ("bnr", g, p, X)
                po2 = (hh % 2) * 64
                def btf(e, po2=po2):
                    for c in range(4):
                        i = e.transpose(psBF[po2:po2 + 64, c * 128:(c + 1) * 128],
                                        atm[:, c * 64:(c + 1) * 64], identb[:])
                    return i
                S.add("tensor", "mm", ("bbt", g, p, X), btf,
                      deps=[("bnr", g, p, X), last_bf[0]])
                def bte(e, hh=hh, po2=po2):
                    dst = attn_fm[po2:po2 + 64, (hh // 2) * 512:(hh // 2 + 1) * 512]
                    return e.tensor_copy(dst, psBF[po2:po2 + 64, 0:512])
                S.add("vector", "ve", ("bbe", g, p, X), bte,
                      deps=[("bbt", g, p, X), last_wo[0]])
                last_bf[0] = ("bbe", g, p, X)
        for n in range(4):
            wsl3 = n % 2
            wbuf3 = pX[wsl3]
            slab_dep = [("bav", g, 15, wsl3)] if n < 2 else [("bwo", g, n - 2, 3)]
            S.add("sync", f"d_wosl{wsl3}", ("bwl", g, n),
                  lambda e, b=wbuf3, n=n: e.dma_start(
                      out=b[:].rearrange("p (k c) -> p k c", k=16),
                      in_=wobf_h.ap()[:, n * 512:(n + 1) * 512].rearrange("(k p) c -> p k c", p=128)),
                  deps=slab_dep + [S.last("d_wos0"), S.last("d_wos1")], dma=True)
            for m in range(4):
                wps = psW[m % 2]
                att_dep = [("bbe", g, 15, 1)] if (n == 0 and m == 0) else []
                def wof(e, b=wbuf3, m=m, p_=wps):
                    for kc in range(16):
                        i = e.matmul(p_[:], attn_fm[:, kc * 512 + m * 128:kc * 512 + (m + 1) * 128],
                                     b[:, kc * 512:(kc + 1) * 512],
                                     start=(kc == 0), stop=(kc == 15))
                    return i
                S.add("tensor", "mm", ("bwo", g, n, m), wof,
                      deps=[("bwl", g, n), last_psw.get(m % 2)] + att_dep)
                xsl, xb3, xfr3 = xres.next()
                S.add("sync", xres.ch(xsl), ("bxl", g, n, m),
                      lambda e, b=xb3, g=g, n=n, m=m: e.dma_start(
                          out=b[:], in_=x_h.ap()[g * 512 + m * 128:g * 512 + (m + 1) * 128,
                                                 n * 512:(n + 1) * 512]),
                      deps=[xfr3], dma=True)
                def z1f(e, p_=wps, xb=xb3, n=n, m=m):
                    sl_ = z1m[m][:, n * 512:(n + 1) * 512]
                    i = e.tensor_tensor(sl_, p_[:], bo_bc[:, n * 512:(n + 1) * 512], OP.add)
                    S.bar(e, i)
                    return e.tensor_tensor(sl_, sl_, xb[:], OP.add)
                S.add("vector", "ve", ("bz1", g, n, m), z1f,
                      deps=[("bwo", g, n, m), ("bxl", g, n, m), ("bbo", 1)], nbar=1)
                xres.used(xsl, ("bz1", g, n, m))
                last_psw[m % 2] = ("bz1", g, n, m)
            last_wo[0] = ("bwo", g, n, 3)
        for m in range(4):
            gm = g * 4 + m
            prev_brs = ("brs", g, m - 1) if m > 0 else \
                (("brs", g - 1, 3) if g > 0 else None)
            def lnf(e, m=m):
                for c in range(4):
                    i = e.bn_stats(bnbuf[:, c * 6:(c + 1) * 6], z1m[m][:, c * 512:(c + 1) * 512])
                S.bar(e, i)
                i = e.bn_aggr(smv[:, 0:2], bnbuf[:, 0:24])
                S.bar(e, i)
                i = e.tensor_scalar(smv[:, 2:3], smv[:, 1:2], 1.0, EPS_LN, OP.mult, OP.add)
                S.bar(e, i)
                return e.reciprocal(smv[:, 3:4], smv[:, 2:3])
            S.add("vector", "ve", ("bln", g, m), lnf, deps=[("bz1", g, 3, m), prev_brs], nbar=3)
            S.add("scalar", "sc", ("bsq", g, m),
                  lambda e: e.sqrt(smv[:, 4:5], smv[:, 3:4]), deps=[("bln", g, m)])
            xsl2, x2b, x2fr = x2t.next()
            def app(e, m=m, xb=x2b):
                i = e.tensor_tensor(smv[:, 5:6], smv[:, 0:1], smv[:, 4:5], OP.mult)
                S.bar(e, i)
                i = e.tensor_scalar(smv[:, 6:7], smv[:, 5:6], -1.0, None, OP.mult)
                S.bar(e, i)
                return e.tensor_scalar(xb[:], z1m[m][:], smv[:, 4:5], smv[:, 6:7],
                                       OP.mult, OP.add)
            S.add("vector", "ve", ("bx2", g, m), app, deps=[("bsq", g, m)] + [x2fr], nbar=2)
            S.add("gpsimd", x2t.ch(xsl2), ("bx2s", gm),
                  lambda e, xb=x2b, gm=gm: e.dma_start(
                      out=x2sp_h.ap()[gm * 128:(gm + 1) * 128, :], in_=xb[:]),
                  deps=[("bx2", g, m)], dma=True)
            x2t.used(xsl2, ("bx2s", gm))
            def rmsf(e):
                i = e.tensor_tensor(smv[:, 7:8], smv[:, 1:2], smv[:, 3:4], OP.mult)
                S.bar(e, i)
                i = e.tensor_scalar(smv[:, 8:9], smv[:, 7:8], 1.0, EPS_RMS, OP.mult, OP.add)
                S.bar(e, i)
                return e.reciprocal(smv[:, 9:10], smv[:, 8:9])
            S.add("vector", "ve", ("brm", g, m), rmsf, deps=[("bln", g, m)], nbar=2)
            S.add("scalar", "sc", ("brs", g, m),
                  lambda e: e.sqrt(smv[:, 10:11], smv[:, 9:10]), deps=[("brm", g, m)])
            def bamf(e, xb=x2b):
                e.tensor_reduce(smv[:, 11:12], xb[:], AX.X, OP.max)
                i = e.tensor_reduce(smv[:, 12:13], xb[:], AX.X, OP.min)
                S.bar(e, i)
                i = e.tensor_scalar(smv[:, 12:13], smv[:, 12:13], -1.0, None, OP.mult)
                S.bar(e, i)
                return e.tensor_tensor(smv[:, 11:12], smv[:, 11:12], smv[:, 12:13], OP.max)
            S.add("vector", "ve", ("bam", g, m), bamf, deps=[("bx2", g, m)], nbar=2)
            def scf(e, gm=gm):
                i = e.tensor_tensor(smv[:, 13:14], smv[:, 10:11], smv[:, 11:12], OP.mult)
                S.bar(e, i)
                i = e.tensor_scalar(smv[:, 13:14], smv[:, 13:14], 1e-5, None, OP.max)
                S.bar(e, i)
                i = e.reciprocal(smv[:, 14:15], smv[:, 13:14])
                S.bar(e, i)
                i = e.tensor_scalar(sxq[:, gm:gm + 1], smv[:, 14:15], 127.0, None, OP.mult)
                S.bar(e, i)
                return e.tensor_tensor(smv[:, 15:16], smv[:, 10:11], sxq[:, gm:gm + 1], OP.mult)
            S.add("vector", "ve", ("bs2", g, m), scf, deps=[("brs", g, m), ("bam", g, m)], nbar=4)
            qsl2, xqb, xqfr = xqt.next()
            def quant(e, xb=x2b, o=xqb):
                for hf2 in range(4):
                    slx = slice(hf2 * 512, (hf2 + 1) * 512)
                    if hf2:
                        S.bar(e, i)
                    i = e.tensor_scalar(tq[:], xb[:, slx], smv[:, 15:16], None, OP.mult)
                    S.bar(e, i)
                    i = e.tensor_copy(r16[:], tq[:])
                    S.bar(e, i)
                    i = e.tensor_tensor(cdf[:], tq[:], r16[:], OP.subtract)
                    S.bar(e, i)
                    i = e.scalar_tensor_tensor(s1t[:], cdf[:], -0.5, r16[:], OP.is_lt, OP.subtract)
                    S.bar(e, i)
                    i = e.scalar_tensor_tensor(o[:, slx], cdf[:], 0.5, s1t[:], OP.is_gt, OP.subtract)
                return i
            S.add("vector", "ve", ("bqq", g, m), quant, deps=[("bs2", g, m)] + [xqfr], nbar=19)
            x2t.used(xsl2, ("bqq", g, m))
            S.add("gpsimd", xqt.ch(qsl2), ("bxqs", gm),
                  lambda e, o=xqb, gm=gm: e.dma_start(
                      out=xqsp_h.ap()[gm * 128:(gm + 1) * 128, :], in_=o[:]),
                  deps=[("bqq", g, m)], dma=True)
            xqt.used(qsl2, ("bxqs", gm))

    dbg_bufs.update(d_qT=(qT, BF16), d_pA=(pX[0], BF16), d_attnfm=(attn_fm, BF16),
                    d_sxq=(sxq, F32))
    if upto == "B":
        _emit_debug(S, nc, debug, dbg_bufs)
        S.emit()
        return nc
    S.fence("B")
    S.emit()
    bst.close()
    ab.close()

    fst = contextlib.ExitStack()
    xqT = sb(fst, "xqT", [128, ND * T], BF16)
    xqrl = Rot([sb(fst, f"xqrl{i}", [128, D], BF16) for i in range(2)], "xqr")
    wgst = Rot([sb(fst, "wgst0", [128, D], F32)], "wg")
    wust = Rot([sb(fst, "wust0", [128, D], F32)], "wu")
    pq_b = sb(fst, "pq_b", [128, D], BF16)
    qq_b = sb(fst, "qq_b", [128, D], BF16)
    terng2 = [sb(fst, f"terng{i}", [128, D], BF16) for i in range(2)]
    ternu2 = [sb(fst, f"ternu{i}", [128, D], BF16) for i in range(2)]
    wgT = Rot([sb(fst, "wgT0", [128, ND * 128], BF16)], "wgT")
    wuT = Rot([sb(fst, "wuT0", [128, ND * 128], BF16)], "wuT")
    gsil = Rot([sb(fst, "gsil0", [128, 512], F32)], "gs")
    gtr = sb(fst, "gtr", [128, 512], F32)
    h2c = Rot([sb(fst, f"h2c{i}", [128, 512], F32) for i in range(2)], "h2")
    sqc = Rot([sb(fst, f"sqc{i}", [128, 512], BF16) for i in range(2)], "sq")
    wdst = Rot([sb(fst, "wdst0", [128, D], F32)], "wd")
    ternd2 = [sb(fst, f"ternd{i}", [128, D], BF16) for i in range(2)]
    wdtst = Rot([sb(fst, f"wdtst{i}", [128, D], BF16) for i in range(2)], "wdt")
    stp = sb(fst, "stp", [128, 1024], BF16)
    gbc = sb(fst, "gbc", [128, 2048], F32)
    amn = sb(fst, "amn", [128, 2048], F32)
    ftm = sb(fst, "ftm", [128, 64], F32)

    psG = Rot([ps(fst, f"psG{i}", [128, 512], F32) for i in range(2)], "psG")
    psU = Rot([ps(fst, f"psU{i}", [128, 512], F32) for i in range(2)], "psU")
    psTX = Rot([ps(fst, f"psTX{i}", [128, 1024], BF16) for i in range(2)], "psTX")
    psST = [ps(fst, f"psST{i}", [128, 512], F32) for i in range(2)]

    def f0a(e):
        e.tensor_reduce(wsc[:, 0:1], ppart[:, 0:128], AX.X, OP.add)
        e.tensor_reduce(wsc[:, 1:2], ppart[:, 128:256], AX.X, OP.add)
        return e.tensor_reduce(wsc[:, 2:3], ppart[:, 256:384], AX.X, OP.add)
    S.add("vector", "ve", ("f0a",), f0a, deps=[("ppa", 383)])
    S.add("tensor", "mm", ("f0b1",),
          lambda e: e.matmul(psST[0][0:1, 0:3], onesf[:], wsc[:, 0:3],
                             start=True, stop=True),
          deps=[("f0a",), ("ini", "ones2")])
    S.add("vector", "ve", ("f0b2",),
          lambda e: e.tensor_copy(wsc[0:1, 10:13], psST[0][0:1, 0:3]),
          deps=[("f0b1",)])
    S.add("tensor", "mm", ("f0b3",),
          lambda e: e.matmul(psST[0][:, 0:3], onesr[:], wsc[0:1, 10:13],
                             start=True, stop=True),
          deps=[("f0b2",)])
    S.add("vector", "ve", ("f0b",),
          lambda e: e.tensor_copy(wsc[:, 0:3], psST[0][:, 0:3]),
          deps=[("f0b3",)])
    def f0c(e):
        i = e.tensor_scalar(wsc[:, 6:9], wsc[:, 0:3], 1.0 / NW, 1e-5, OP.mult, OP.max)
        S.bar(e, i)
        e.reciprocal(wsc[:, 0:1], wsc[:, 6:7])
        e.reciprocal(wsc[:, 1:2], wsc[:, 7:8])
        i = e.reciprocal(wsc[:, 2:3], wsc[:, 8:9])
        S.bar(e, i)
        e.tensor_scalar(wsc[:, 3:6], wsc[:, 0:3], -1.0, None, OP.mult)
        return e.tensor_tensor(wsc[:, 9:10], wsc[:, 7:8], wsc[:, 0:1], OP.mult)
    S.add("vector", "ve", ("f0c",), f0c, deps=[("f0b",)], nbar=2)
    S.add("gpsimd", "d_f0r", ("f0r",),
          lambda e: e.dma_start(out=rowb_h.ap()[:, :], in_=sxq[:, 0:16]),
          deps=[("bs2", NQB - 1, 3)], dma=True)
    S.add("sync", "d_f0l", ("f0l",),
          lambda e: e.dma_start(
              out=rows4[96:97, :].rearrange("p (m q) -> p m q", q=128),
              in_=bass.AP(rowb_h, 0, [[0, 1], [1, 16], [16, 128]]),
              allow_slow_non_contiguous=True),
          deps=[("f0r",)], dma=True)
    def f0d(e):
        i = e.reciprocal(rows4[32:33, :], rows4[96:97, :])
        S.bar(e, i)
        return e.tensor_scalar(rows4[0:1, :], rows4[32:33, :], wsc[32:33, 6:7], None, OP.mult)
    S.add("vector", "ve", ("f0d",), f0d, deps=[("f0l",), ("f0c",)], nbar=1)
    for n in range(4):
        S.add("tensor", "mm", ("f0g", n),
              lambda e, n=n: e.matmul(psST[1][:], onesr[:],
                                      rows4[0:1, n * 512:(n + 1) * 512],
                                      start=True, stop=True),
              deps=[("f0d",), ("f0ge", n - 1) if n > 0 else None])
        S.add("vector", "ve", ("f0ge", n),
              lambda e, n=n: e.tensor_copy(gbc[:, n * 512:(n + 1) * 512], psST[1][:]),
              deps=[("f0g", n)])
    S.add("vector", "ve", ("f0g",), lambda e: e.engine_nop(),
          deps=[("f0ge", n) for n in range(4)])

    xq_sp_done = [S.last("d_xqs0"), S.last("d_xqs1")]
    for m in range(NT):
        sl, buf, fr = xqrl.next()
        S.add("sync", xqrl.ch(sl), ("fxl", m),
              lambda e, b=buf, m=m: e.dma_start(
                  out=b[:], in_=xqsp_h.ap()[m * 128:(m + 1) * 128, :]),
              deps=[fr] + xq_sp_done, dma=True)
        for j in range(2):
            tsl, tbuf, tfr = psTX.next()
            def xtf(e, b=buf, j=j, tb=tbuf):
                for c in range(8 * j, 8 * j + 8):
                    i = e.transpose(tb[:, (c % 8) * 128:(c % 8 + 1) * 128],
                                    b[:, c * 128:(c + 1) * 128], identb[:])
                return i
            S.add("tensor", "mm", ("fxt", m, j), xtf, deps=[("fxl", m)] + tfr)
            def xte(e, m=m, j=j, tb=tbuf):
                dst = xqT[:].rearrange("p (f t) -> p f t", f=ND)[:, 8 * j:8 * j + 8,
                                                                m * 128:(m + 1) * 128]
                return e.tensor_copy(dst, tb[:].rearrange("p (s c) -> p s c", s=8))
            S.add("vector", "ve", ("fxe", m, j), xte, deps=[("fxt", m, j)])
            psTX.used(tsl, ("fxe", m, j))
        xqrl.used(sl, ("fxt", m, 1))

    S.add("vector", "ve", ("f1amn",), lambda e: e.memset(amn[:], 3.0e38))
    last_st = {0: ("f0b",), 1: ("f0ge", 3)}
    for ht in range(NFH):
        gsl, gst, gfr = wgst.next()
        usl, ust, ufr = wust.next()
        S.add("sync", wgst.ch(gsl), ("fgl", ht),
              lambda e, b=gst, ht=ht: e.dma_start(
                  out=b[:], in_=wg_h.ap()[ht * 128:(ht + 1) * 128, :]), deps=[gfr], dma=True)
        S.add("sync", wust.ch(usl), ("ful", ht),
              lambda e, b=ust, ht=ht: e.dma_start(
                  out=b[:], in_=wu_h.ap()[ht * 128:(ht + 1) * 128, :]), deps=[ufr], dma=True)
        terng = terng2[ht % 2]
        ternu = ternu2[ht % 2]
        def qg(e, w=gst, terng=terng):
            e.tensor_scalar(pq_b[:], w[:], wsc[:, 0:1], 0.5, OP.mult, OP.is_ge)
            i = e.tensor_scalar(qq_b[:], w[:], wsc[:, 3:4], 0.5, OP.mult, OP.is_ge)
            S.bar(e, i)
            return e.tensor_tensor(terng[:], pq_b[:], qq_b[:], OP.subtract)
        S.add("vector", "ve", ("fgq", ht), qg,
              deps=[("fgl", ht), ("f0c",),
                    ("ftt", "g", ht - 2, 1) if ht >= 2 else None], nbar=1)
        wgst.used(gsl, ("fgq", ht))
        def qu(e, w=ust, ternu=ternu):
            e.tensor_scalar(pq_b[:], w[:], wsc[:, 1:2], 0.5, OP.mult, OP.is_ge)
            i = e.tensor_scalar(qq_b[:], w[:], wsc[:, 4:5], 0.5, OP.mult, OP.is_ge)
            S.bar(e, i)
            return e.tensor_tensor(ternu[:], pq_b[:], qq_b[:], OP.subtract)
        S.add("vector", "ve", ("fuq", ht), qu,
              deps=[("ful", ht), ("f0c",),
                    ("ftt", "u", ht - 2, 1) if ht >= 2 else None], nbar=1)
        wust.used(usl, ("fuq", ht))
        gtsl, gT, gTfr = wgT.next()
        utsl, uT, uTfr = wuT.next()
        for nm, tern, dstT, dfr, qkey in (("g", terng, gT, gTfr, ("fgq", ht)),
                                          ("u", ternu, uT, uTfr, ("fuq", ht))):
            for j in range(2):
                tsl, tbuf, tfr = psTX.next()
                def ttf2(e, tern=tern, j=j, tb=tbuf):
                    for c in range(8 * j, 8 * j + 8):
                        i = e.transpose(tb[:, (c % 8) * 128:(c % 8 + 1) * 128],
                                        tern[:, c * 128:(c + 1) * 128], identb[:])
                    return i
                S.add("tensor", "mm", ("ftt", nm, ht, j), ttf2,
                      deps=[qkey] + tfr + (dfr if j == 0 else []))
                def tte(e, dstT=dstT, j=j, tb=tbuf):
                    dst = dstT[:].rearrange("p (f c) -> p f c", f=ND)[:, 8 * j:8 * j + 8, :]
                    return e.tensor_copy(dst, tb[:].rearrange("p (s c) -> p s c", s=8))
                S.add("vector", "ve", ("fte", nm, ht, j), tte, deps=[("ftt", nm, ht, j)])
                psTX.used(tsl, ("fte", nm, ht, j))
        for n in range(4):
            gpsl, gps, gpfr = psG.next()
            def gmm(e, gT=gT, n=n, p_=gps):
                for kc in range(16):
                    i = e.matmul(p_[:], gT[:, kc * 128:(kc + 1) * 128],
                                 xqT[:, kc * T + n * 512:kc * T + n * 512 + 512],
                                 start=(kc == 0), stop=(kc == 15))
                return i
            S.add("tensor", "mm", ("fgm", ht, n), gmm,
                  deps=[("fte", "g", ht, 1), ("fxe", NT - 1, 1)] + [gpfr])
            gssl, gsb3, gsfr = gsil.next()
            S.add("vector", "ve", ("fgs", ht, n),
                  lambda e, p_=gps, n=n: e.tensor_tensor(
                      gtr[:], p_[:], gbc[:, n * 512:(n + 1) * 512], OP.mult),
                  deps=[("fgm", ht, n), ("f0g",)])
            psG.used(gpsl, ("fgs", ht, n))
            S.add("scalar", "sc", ("fsi", ht, n),
                  lambda e, o=gsb3: e.activation(o[:], gtr[:], AF.Silu),
                  deps=[("fgs", ht, n)] + [gsfr])
            S.add("vector", "ve", ("fgu", ht, n),
                  lambda e, o=gsb3, n=n: e.scalar_tensor_tensor(
                      o[:], o[:], wsc[:, 9:10], gbc[:, n * 512:(n + 1) * 512],
                      OP.mult, OP.mult),
                  deps=[("fsi", ht, n)])
            upsl, ups, upfr = psU.next()
            def umm(e, uT=uT, n=n, p_=ups):
                for kc in range(16):
                    i = e.matmul(p_[:], uT[:, kc * 128:(kc + 1) * 128],
                                 xqT[:, kc * T + n * 512:kc * T + n * 512 + 512],
                                 start=(kc == 0), stop=(kc == 15))
                return i
            S.add("tensor", "mm", ("fum", ht, n), umm,
                  deps=[("fte", "u", ht, 1), ("fxe", NT - 1, 1)] + [upfr])
            h2sl, h2b, h2fr = h2c.next()
            S.add("vector", "ve", ("fh2", ht, n),
                  lambda e, p_=ups, o=h2b, g_=gsb3: e.tensor_tensor(o[:], p_[:], g_[:], OP.mult),
                  deps=[("fum", ht, n), ("fgu", ht, n)] + [h2fr])
            psU.used(upsl, ("fh2", ht, n))
            gsil.used(gssl, ("fh2", ht, n))
            S.add("vector", "ve", ("famP", ht, n),
                  lambda e, o=h2b, n=n: e.tensor_tensor(
                      cbc[:, n * 512:(n + 1) * 512], cbc[:, n * 512:(n + 1) * 512],
                      o[:], OP.max),
                  deps=[("fh2", ht, n), ("ini", "amx0")])
            S.add("vector", "ve", ("famN", ht, n),
                  lambda e, o=h2b, n=n: e.tensor_tensor(
                      amn[:, n * 512:(n + 1) * 512], amn[:, n * 512:(n + 1) * 512],
                      o[:], OP.min),
                  deps=[("fh2", ht, n), ("f1amn",)])
            sqsl, sqb3, sqfr = sqc.next()
            S.add("scalar", "sc", ("fsq", ht, n),
                  lambda e, i_=h2b, o=sqb3: e.square(o[:], i_[:]),
                  deps=[("fh2", ht, n)] + [sqfr])
            bank = n // 2
            S.add("tensor", "mm", ("fsm", ht, n),
                  lambda e, ht=ht, n=n, q_=sqb3, bank=bank: e.matmul(
                      psST[bank][(n % 2) * 64:(n % 2) * 64 + 64, :],
                      onesband[:, 63 - ht:127 - ht], q_[:],
                      start=(ht == 0), stop=(ht == NFH - 1), skip_group_check=True),
                  deps=[("fsq", ht, n), last_st.get(bank), ("ini", 2)])
            last_st[bank] = ("fsm", ht, n)
            sqc.used(sqsl, ("fsm", ht, n))
            S.add("gpsimd", h2c.ch(h2sl), ("fh2s", ht, n),
                  lambda e, o=h2b, ht=ht, n=n: e.dma_start(
                      out=h2sp_h.ap()[ht * 128:(ht + 1) * 128, n * 512:(n + 1) * 512],
                      in_=o[:]),
                  deps=[("fh2", ht, n)], dma=True)
            h2c.used(h2sl, ("fh2s", ht, n))
            h2c.used(h2sl, ("famP", ht, n))
            h2c.used(h2sl, ("famN", ht, n))
        wgT.used(gtsl, ("fgm", ht, 3))
        wuT.used(utsl, ("fum", ht, 3))
        dt, pc = divmod(ht, 4)
        dsl, dstg, dfr2 = wdst.next()
        S.add("sync", wdst.ch(dsl), ("fdl", ht),
              lambda e, b=dstg, dt=dt, pc=pc: e.dma_start(
                  out=b[:], in_=wd_h.ap()[dt * 128:(dt + 1) * 128, pc * 2048:(pc + 1) * 2048]),
              deps=[dfr2], dma=True)
        tdb = ternd2[ht % 2]
        def qd(e, w=dstg, o=tdb):
            e.tensor_scalar(pq_b[:], w[:], wsc[:, 2:3], 0.5, OP.mult, OP.is_ge)
            i = e.tensor_scalar(qq_b[:], w[:], wsc[:, 5:6], 0.5, OP.mult, OP.is_ge)
            S.bar(e, i)
            return e.tensor_tensor(o[:], pq_b[:], qq_b[:], OP.subtract)
        S.add("vector", "ve", ("fdq", ht), qd,
              deps=[("fdl", ht), ("f0c",),
                    ("fdt", ht - 2, 1) if ht >= 2 else None], nbar=1)
        wdst.used(dsl, ("fdq", ht))
        wtsl, wtb, wtfr = wdtst.next()
        for j in range(2):
            tsl, tbuf, tfr = psTX.next()
            def dtf(e, o=tdb, j=j, tb=tbuf):
                for c in range(8 * j, 8 * j + 8):
                    i = e.transpose(tb[:, (c % 8) * 128:(c % 8 + 1) * 128],
                                    o[:, c * 128:(c + 1) * 128], identb[:])
                return i
            S.add("tensor", "mm", ("fdt", ht, j), dtf,
                  deps=[("fdq", ht)] + tfr + (wtfr if j == 0 else []))
            def dte(e, wtb=wtb, j=j, tb=tbuf):
                dst = wtb[:].rearrange("p (c q) -> p c q", c=ND)[:, 8 * j:8 * j + 8, :]
                return e.tensor_copy(dst, tb[:].rearrange("p (s c) -> p s c", s=8))
            S.add("vector", "ve", ("fde", ht, j), dte, deps=[("fdt", ht, j)])
            psTX.used(tsl, ("fde", ht, j))

        def dts(e, wtb=wtb, dt=dt, pc=pc):
            dst = bass.AP(wdtsp_h, (pc * 2048) * D + dt * 128,
                          [[D, 128], [128 * D, 16], [1, 128]])
            return e.dma_start(out=dst, in_=wtb[:].rearrange("p (c q) -> p c q", c=16))
        S.add("gpsimd", wdtst.ch(wtsl), ("fds", ht), dts, deps=[("fde", ht, 1)], dma=True)
        wdtst.used(wtsl, ("fds", ht))

    S.add("vector", "ve", ("fsv",),
          lambda e: (e.tensor_copy(stp[:, 0:512], psST[0][:]),
                     e.tensor_copy(stp[:, 512:1024], psST[1][:]))[-1],
          deps=[last_st[0], last_st[1]])
    for n in range(4):
        po = (n % 2) * 64
        for c in range(4):
            idx = n * 4 + c
            tsl, tbuf, tfr = psTX.next()
            S.add("tensor", "mm", ("fstt", idx),
                  lambda e, po=po, n=n, c=c, tb=tbuf: e.transpose(
                      tb[:, 0:64],
                      stp[po:po + 64, (n // 2) * 512 + c * 128:(n // 2) * 512 + (c + 1) * 128],
                      identb[po:po + 64, po:po + 64]),
                  deps=[("fsv",), ("fte", "u", NFH - 1, 1)] + tfr)
            S.add("vector", "ve", ("fstr", idx),
                  lambda e, idx=idx, tb=tbuf: e.tensor_reduce(
                      ftm[:, idx:idx + 1], tb[:, 0:64],
                      AX.X, OP.add),
                  deps=[("fstt", idx)])
            psTX.used(tsl, ("fstr", idx))
    fam_last = [("famP", NFH - 1, 3), ("famN", NFH - 1, 3)]
    for c in range(16):
        S.add("tensor", "mm", ("famt", c),
              lambda e, c=c: e.transpose(psST[0][:, (c % 4) * 128:(c % 4 + 1) * 128],
                                         cbc[:, c * 128:(c + 1) * 128], identf[:]),
              deps=[("fsv",), ("famr", c - 1) if c >= 1 else None] + fam_last)
        S.add("vector", "ve", ("famr", c),
              lambda e, c=c: e.tensor_reduce(
                  ftm[:, 16 + c:17 + c], psST[0][:, (c % 4) * 128:(c % 4 + 1) * 128],
                  AX.X, OP.max),
              deps=[("famt", c)])
        S.add("tensor", "mm", ("fant", c),
              lambda e, c=c: e.transpose(psST[1][:, (c % 4) * 128:(c % 4 + 1) * 128],
                                         amn[:, c * 128:(c + 1) * 128], identf[:]),
              deps=[("fsv",), ("fanr", c - 1) if c >= 1 else None] + fam_last)
        S.add("vector", "ve", ("fanr", c),
              lambda e, c=c: e.tensor_reduce(
                  ftm[:, 32 + c:33 + c], psST[1][:, (c % 4) * 128:(c % 4 + 1) * 128],
                  AX.X, OP.min),
              deps=[("fant", c)])
    def ftok1(e):
        i = e.tensor_scalar(ftm[:, 32:48], ftm[:, 32:48], -1.0, None, OP.mult)
        S.bar(e, i)
        e.tensor_tensor(ftm[:, 16:32], ftm[:, 16:32], ftm[:, 32:48], OP.max)
        i = e.tensor_scalar(ftm[:, 0:16], ftm[:, 0:16], 1.0 / FF, EPS_RMS, OP.mult, OP.add)
        S.bar(e, i)
        return e.reciprocal(ftm[:, 32:48], ftm[:, 0:16])
    S.add("vector", "ve", ("ftok1",), ftok1, nbar=2,
          deps=[("fstr", i) for i in range(16)] + [("famr", c) for c in range(16)]
               + [("fanr", c) for c in range(16)])
    S.add("scalar", "sc", ("ftok2",),
          lambda e: e.sqrt(ftm[:, 32:48], ftm[:, 32:48]), deps=[("ftok1",)])
    def ftok3(e):
        i = e.tensor_tensor(ftm[:, 48:64], ftm[:, 32:48], ftm[:, 16:32], OP.mult)
        S.bar(e, i)
        i = e.tensor_scalar(ftm[:, 48:64], ftm[:, 48:64], 1e-5, None, OP.max)
        S.bar(e, i)
        i = e.reciprocal(ftm[:, 48:64], ftm[:, 48:64])
        S.bar(e, i)
        i = e.tensor_scalar(ftm[:, 48:64], ftm[:, 48:64], 127.0, None, OP.mult)
        S.bar(e, i)
        e.tensor_tensor(ftm[:, 0:16], ftm[:, 32:48], ftm[:, 48:64], OP.mult)
        i = e.reciprocal(c3col[:, 0:16], ftm[:, 48:64])
        S.bar(e, i)
        return e.tensor_scalar(c3col[:, 0:16], c3col[:, 0:16], wsc[:, 8:9], None, OP.mult)
    S.add("vector", "ve", ("ftok3",), ftok3, deps=[("ftok2",), ("f0c",)], nbar=5)
    S.add("gpsimd", "d_f1r", ("f1r",),
          lambda e: e.dma_start(out=rowb_h.ap(), in_=ftm[:, 0:16]),
          deps=[("ftok3",)], dma=True)
    S.add("sync", "d_f1l", ("f1l",),
          lambda e: e.dma_start(
              out=rows4[0:1, :].rearrange("p (m q) -> p m q", q=128),
              in_=bass.AP(rowb_h, 0, [[0, 1], [1, 16], [16, 128]]),
              allow_slow_non_contiguous=True),
          deps=[("f1r",)], dma=True)
    for n in range(4):
        S.add("tensor", "mm", ("fcbm", n),
              lambda e, n=n: e.matmul(psST[0][:], onesr[:],
                                      rows4[0:1, n * 512:(n + 1) * 512],
                                      start=True, stop=True),
              deps=[("f1l",), ("famt", 15),
                    ("fcbe", n - 1) if n > 0 else ("famr", 15)])
        S.add("vector", "ve", ("fcbe", n),
              lambda e, n=n: e.tensor_copy(cbc[:, n * 512:(n + 1) * 512], psST[0][:]),
              deps=[("fcbm", n)])
    S.add("vector", "ve", ("fcb",), lambda e: e.engine_nop(),
          deps=[("fcbe", n) for n in range(4)])

    dbg_bufs.update(d_xqT=(xqT, BF16), d_wsc=(wsc, F32), d_cbc=(cbc, F32),
                    d_gbc=(gbc, F32), d_c3col=(c3col, F32), d_rows=(rows4, F32), d_ftm=(ftm, F32))
    if upto == "F1":
        _emit_debug(S, nc, debug, dbg_bufs)
        S.emit()
        return nc
    S.fence("F1")
    S.emit()
    fst.close()
    rw.close()

    gstk = contextlib.ExitStack()
    xq2res = sb(gstk, "xq2res", [128, NFH * 1024], I8)
    h2rl = Rot([sb(gstk, f"h2rl{i}", [128, 1024], F32) for i in range(2)], "h2r")
    tq2 = Rot([sb(gstk, "tq20", [128, 1024], F32)], "tq2")
    r162 = sb(gstk, "r162", [128, 1024], I16)
    cdf2 = sb(gstk, "cdf2", [128, 1024], F32)
    s1t2 = sb(gstk, "s1t2", [128, 1024], F32)
    wdts = Rot([sb(gstk, "wdts0", [128, NFH * 256], BF16)], "wdl")
    xcol = Rot([sb(gstk, f"xcol{i}", [128, NFH * 128], BF16) for i in range(2)], "xc")
    x2c = Rot([sb(gstk, f"x2c{i}", [128, 256], F32) for i in range(2)], "x2c")
    z2h = sb(gstk, "z2h", [128, 8 * D], BF16)
    bnb2 = sb(gstk, "bnb2", [128, 24], F32)
    outst = Rot([sb(gstk, "outst0", [128, D], F32)], "os")
    psD = Rot([ps(gstk, f"psD{i}", [128, 512], F32) for i in range(2)], "psD")

    h2_sp_done = [S.last("d_h20"), S.last("d_h21")]
    wdt_sp_done = [S.last("d_wdt0"), S.last("d_wdt1")]
    x2_sp_done = [S.last("d_x2s0"), S.last("d_x2s1")]
    NB = 8
    last_g3c = [None]
    for hf in range(2):
        for ht in range(NFH):
            sl, buf, fr = h2rl.next()
            S.add("sync", h2rl.ch(sl), ("g2l", hf, ht),
                  lambda e, b=buf, hf=hf, ht=ht: e.dma_start(
                      out=b[:], in_=h2sp_h.ap()[ht * 128:(ht + 1) * 128,
                                                hf * 1024:(hf + 1) * 1024]),
                  deps=[fr] + h2_sp_done, dma=True)
            tsl, tb, tfr = tq2.next()
            S.add("gpsimd", "gp", ("g2m", hf, ht),
                  lambda e, b=buf, o=tb, hf=hf: e.tensor_tensor(
                      o[:], b[:], cbc[:, hf * 1024:(hf + 1) * 1024], OP.mult),
                  deps=[("g2l", hf, ht), ("fcb",), last_g3c[0] if ht == 0 else None] + tfr)
            h2rl.used(sl, ("g2m", hf, ht))
            def q2(e, t=tb, hf=hf, ht=ht):
                i = e.tensor_copy(r162[:], t[:])
                S.bar(e, i)
                i = e.tensor_tensor(cdf2[:], t[:], r162[:], OP.subtract)
                S.bar(e, i)
                i = e.scalar_tensor_tensor(s1t2[:], cdf2[:], -0.5, r162[:], OP.is_lt, OP.subtract)
                S.bar(e, i)
                return e.scalar_tensor_tensor(xq2res[:, ht * 1024:(ht + 1) * 1024],
                                              cdf2[:], 0.5, s1t2[:], OP.is_gt, OP.subtract)
            S.add("vector", "ve", ("g2q", hf, ht), q2, deps=[("g2m", hf, ht)], nbar=3)
            tq2.used(tsl, ("g2q", hf, ht))
        for n in range(NB):
            wsl4, wb4, wfr4 = wdts.next()
            S.add("sync", wdts.ch(wsl4), ("g3w", hf, n),
                  lambda e, b=wb4, n=n: e.dma_start(
                      out=b[:].rearrange("p (k c) -> p k c", k=NFH),
                      in_=wdtsp_h.ap()[:, n * 256:(n + 1) * 256].rearrange(
                          "(k p) c -> p k c", p=128)),
                  deps=[wfr4] + wdt_sp_done, dma=True)
            for m in range(8):
                gm = hf * 8 + m
                csl, cb4, cfr4 = xcol.next()
                S.add("vector", "ve", ("g3c", hf, n, m),
                      lambda e, o=cb4, m=m: e.tensor_copy(
                          o[:].rearrange("p (k c) -> p k c", k=NFH),
                          xq2res[:].rearrange("p (k c) -> p k c", k=NFH)[
                              :, :, m * 128:(m + 1) * 128]),
                      deps=[("g2q", hf, NFH - 1)] + cfr4)
                last_g3c[0] = ("g3c", hf, n, m)
                dsl4, dps4, dfr4 = psD.next()
                def dmm(e, w=wb4, c_=cb4, p_=dps4):
                    for hk in range(NFH):
                        i = e.matmul(p_[0:128, 0:256], c_[:, hk * 128:(hk + 1) * 128],
                                     w[:, hk * 256:(hk + 1) * 256],
                                     start=(hk == 0), stop=(hk == NFH - 1))
                    return i
                S.add("tensor", "mm", ("g3m", hf, n, m), dmm,
                      deps=[("g3w", hf, n), ("g3c", hf, n, m)] + dfr4)
                xcol.used(csl, ("g3m", hf, n, m))
                xsl4, xb4, xfr4 = x2c.next()
                S.add("sync", x2c.ch(xsl4), ("g3x", hf, n, m),
                      lambda e, b=xb4, gm=gm, n=n: e.dma_start(
                          out=b[:], in_=x2sp_h.ap()[gm * 128:(gm + 1) * 128,
                                                    n * 256:(n + 1) * 256]),
                      deps=[xfr4] + x2_sp_done, dma=True)
                S.add("vector", "ve", ("g3z", hf, n, m),
                      lambda e, p_=dps4, xb=xb4, m=m, n=n, gm=gm: e.scalar_tensor_tensor(
                          z2h[:, m * D + n * 256:m * D + n * 256 + 256],
                          p_[0:128, 0:256], c3col[:, gm:gm + 1], xb[:], OP.mult, OP.add),
                      deps=[("g3m", hf, n, m), ("g3x", hf, n, m), ("ftok3",)])
                x2c.used(xsl4, ("g3z", hf, n, m))
                psD.used(dsl4, ("g3z", hf, n, m))
            wdts.used(wsl4, ("g3m", hf, n, 7))
        for m in range(8):
            gm = hf * 8 + m
            def ln2(e, m=m):
                for c in range(4):
                    i = e.bn_stats(bnb2[:, c * 6:(c + 1) * 6],
                                   z2h[:, m * D + c * 512:m * D + c * 512 + 512])
                S.bar(e, i)
                i = e.bn_aggr(smv[:, 0:2], bnb2[:, 0:24])
                S.bar(e, i)
                i = e.tensor_scalar(smv[:, 2:3], smv[:, 1:2], 1.0, EPS_LN, OP.mult, OP.add)
                S.bar(e, i)
                return e.reciprocal(smv[:, 3:4], smv[:, 2:3])
            prev_q = ("g4q", hf, m - 1) if m > 0 else \
                (("g4q", hf - 1, 7) if hf > 0 else None)
            S.add("vector", "ve", ("g4s", hf, m), ln2,
                  deps=[("g3z", hf, n_, m) for n_ in range(NB)] + [prev_q], nbar=3)
            S.add("scalar", "sc", ("g4q", hf, m),
                  lambda e: e.sqrt(smv[:, 4:5], smv[:, 3:4]), deps=[("g4s", hf, m)])
            osl, ob, ofr = outst.next()
            def app2(e, m=m, ob=ob):
                i = e.tensor_tensor(smv[:, 5:6], smv[:, 0:1], smv[:, 4:5], OP.mult)
                S.bar(e, i)
                i = e.tensor_scalar(smv[:, 6:7], smv[:, 5:6], -1.0, None, OP.mult)
                S.bar(e, i)
                return e.tensor_scalar(ob[:], z2h[:, m * D:(m + 1) * D],
                                       smv[:, 4:5], smv[:, 6:7], OP.mult, OP.add)
            S.add("vector", "ve", ("g4a", hf, m), app2, deps=[("g4q", hf, m)] + ofr, nbar=2)
            S.add("gpsimd", outst.ch(osl), ("g4o", gm),
                  lambda e, ob=ob, gm=gm: e.dma_start(
                      out=out_h.ap()[gm * 128:(gm + 1) * 128, :], in_=ob[:]),
                  deps=[("g4a", hf, m)], dma=True)
            outst.used(osl, ("g4o", gm))

    S.add("gpsimd", "gp", ("fin",), lambda e: e.engine_nop(),
          deps=[("g4o", NT - 1)])
    dbg_bufs.update(d_xq2=(xq2res, I8))
    _emit_debug(S, nc, debug, dbg_bufs)
    S.emit()
    gstk.close()
    top.close()
    return nc


def _emit_debug(S, nc, debug, dbg_bufs):
    if not debug:
        return
    names = [n for n in debug if n in dbg_bufs]
    alldeps = list(S.lastkey.values())
    for name in names:
        buf, dt = dbg_bufs[name]
        hh = nc.dram_tensor(name, list(buf.shape), dt, kind="ExternalOutput")
        S.add("gpsimd", f"d_dbg_{name}", ("dbg", name),
              lambda e, b=buf, hh=hh: e.dma_start(out=hh.ap(), in_=b[:]),
              deps=alldeps, dma=True)
    S.add("gpsimd", "gp", ("dbgw",), lambda e: e.engine_nop(),
          deps=[("dbg", n) for n in names])


def make_inputs(L0_attn_x, L0_attn_Wq, L0_attn_bq, L0_attn_Wk, L0_attn_bk,
                L0_attn_Wv, L0_attn_bv, L0_attn_Wo, L0_attn_bo,
                L0_ffn_w_gate, L0_ffn_w_up, L0_ffn_w_down):
    consts = host_constants()
    f32 = np.float32
    base = {
        "Wq": np.ascontiguousarray(L0_attn_Wq, dtype=f32),
        "bq": np.ascontiguousarray(L0_attn_bq, dtype=f32).reshape(1, D),
        "Wk": np.ascontiguousarray(L0_attn_Wk, dtype=f32),
        "bk": np.ascontiguousarray(L0_attn_bk, dtype=f32).reshape(1, 512),
        "Wv": np.ascontiguousarray(L0_attn_Wv, dtype=f32),
        "bv": np.ascontiguousarray(L0_attn_bv, dtype=f32).reshape(1, 512),
        "Wo": np.ascontiguousarray(L0_attn_Wo, dtype=f32),
        "bo": np.ascontiguousarray(L0_attn_bo, dtype=f32).reshape(1, D),
        "w_gate": np.ascontiguousarray(L0_ffn_w_gate, dtype=f32),
        "w_up": np.ascontiguousarray(L0_ffn_w_up, dtype=f32),
        "w_down": np.ascontiguousarray(L0_ffn_w_down, dtype=f32),
        "cosT": consts["cosT"], "sinT": consts["sinT"],
        "ident_bf": consts["ident_bf"], "ident_f32": consts["ident_f32"],
        "onesband": consts["onesband"],
    }
    x = np.ascontiguousarray(L0_attn_x, dtype=f32)
    return [dict(base, x=x[c]) for c in range(8)]


def kernel(L0_attn_x, L0_attn_Wq, L0_attn_bq, L0_attn_Wk, L0_attn_bk,
           L0_attn_Wv, L0_attn_bv, L0_attn_Wo, L0_attn_bo,
           L0_ffn_w_gate, L0_ffn_w_up, L0_ffn_w_down):
    nc = bass.Bass("TRN2", target_bir_lowering=False)
    nc = build(nc, debug=(), upto=BUILD_UPTO)
    in_maps = make_inputs(L0_attn_x, L0_attn_Wq, L0_attn_bq, L0_attn_Wk, L0_attn_bk,
                          L0_attn_Wv, L0_attn_bv, L0_attn_Wo, L0_attn_bo,
                          L0_ffn_w_gate, L0_ffn_w_up, L0_ffn_w_down)
    res = run_bass_kernel_spmd(nc, in_maps, core_ids=list(range(8)))
    return np.stack([np.asarray(r["out"]) for r in res.results], axis=0)
